# version 03
# baseline (speedup 1.0000x reference)
import numpy as np
from contextlib import ExitStack

import concourse.bass as bass
import concourse.mybir as mybir
from concourse.bass_utils import run_bass_kernel_spmd

F32 = mybir.dt.float32
BF16 = mybir.dt.bfloat16
AF = mybir.ActivationFunctionType
ALU = mybir.AluOpType
AXX = mybir.AxisListType.X

NCORES = 8
B = 4
S = 8192
D = 1024
SL = S // NCORES
T = B * SL
NMT = SL // 128
NDT = D // 128
H = 16
DH = 64
NCH = 8
EPS_DN = 1e-6
LN_EPS = 1e-5
NE = 8
CAP = 100
FFH = 4096


class SemT:
    def __init__(self, stack, nc, name):
        self.h = stack.enter_context(nc.semaphore(name))
        self.v = 0


class Prog:
    ENGS = ("sync", "gpsimd", "tensor", "vector", "scalar")

    def __init__(self, nc, stack, sem_stack=None, tag=""):
        self.nc = nc
        self.stack = stack
        self.sem_stack = sem_stack or stack
        self.tag = tag
        self.items = {e: [] for e in self.ENGS}
        self.sems = {e: SemT(self.sem_stack, nc, f"{tag}c_{e}") for e in self.ENGS}

    def _norm(self, wait):
        if wait is None or wait == ():
            return []
        if not isinstance(wait, list):
            wait = [wait]
        out = []
        for w in wait:
            if w is None:
                continue
            out.append((w.h, w.v) if isinstance(w, SemT) else (w[0].h, w[1]))
        return out

    CHAINED = ("vector", "scalar")

    def op(self, eng, fn, wait=(), n=1):
        sem = self.sems[eng]
        wl = self._norm(wait)
        if eng in self.CHAINED and sem.v > 0:
            wl.append((sem.h, sem.v))
        sem.v += n
        tok = (sem, sem.v)

        def g(e, wl=wl, fn=fn, sem=sem, n=n):
            for h, v in wl:
                if v > 0:
                    e.wait_ge(h, v)
            fn(e).then_inc(sem.h, n)
        self.items[eng].append(g)
        return tok

    def opn(self, eng, fn, wait=()):
        if eng in self.CHAINED:
            return self.op(eng, fn, wait=wait)
        wl = self._norm(wait)

        def g(e, wl=wl, fn=fn):
            for h, v in wl:
                if v > 0:
                    e.wait_ge(h, v)
            fn(e)
        self.items[eng].append(g)

    def dma(self, eng, out, in_, wait=(), ch=None, **kw):
        key = "ch_" + (ch or eng)
        if key not in self.sems:
            self.sems[key] = SemT(self.sem_stack, self.nc, self.tag + key)
        sem = self.sems[key]
        wl = self._norm(wait) + ([(sem.h, sem.v)] if sem.v > 0 else [])
        sem.v += 16
        tok = (sem, sem.v)

        def g(e, wl=wl, out=out, in_=in_, sem=sem, kw=kw):
            for h, v in wl:
                if v > 0:
                    e.wait_ge(h, v)
            e.dma_start(out=out, in_=in_, **kw).then_inc(sem.h, 16)
        self.items[eng].append(g)
        return tok

    def tok(self, eng):
        s = self.sems[eng]
        return (s, s.v)

    def final_waits(self):
        for key, s in list(self.sems.items()):
            if key.startswith("ch_") and s.v > 0:
                self.opn("gpsimd", lambda g, s=s, v=s.v: g.wait_ge(s.h, v))

    def emit(self, blk):
        for name in self.ENGS:
            items = self.items[name]
            if not items:
                continue

            def body(e, items=items):
                for g in items:
                    g(e)
            getattr(blk, name)(body)


def host_consts(core):
    out = {}
    inv_freq = 1.0 / 10000.0 ** (np.arange(0, DH, 2, dtype=np.float64) / DH)
    t = np.arange(core * SL, (core + 1) * SL, dtype=np.float64)
    fr = t[:, None] * inv_freq[None, :]
    out["cos8"] = np.tile(np.cos(fr).astype(np.float32), (1, 8))
    out["sin8"] = np.tile(np.sin(fr).astype(np.float32), (1, 8))
    out["ident"] = np.eye(128, dtype=np.float32)
    r = np.arange(128)
    out["supper"] = (r[:, None] > r[None, :]).astype(np.float32)
    out["lower"] = (r[:, None] <= r[None, :]).astype(np.float32)
    sel = np.zeros((32, 4), dtype=np.float32)
    for j in range(NCORES):
        for bp in range(B):
            for b in range(B):
                if bp < b or (bp == b and j < core):
                    sel[j * 4 + bp, b] = 1.0
    out["sel"] = sel
    pairK2 = np.zeros((128, 512), dtype=np.float32)
    for a1 in range(4):
        for a2 in range(4):
            pp = a1 * 4 + a2
            for hh in range(2):
                ks = slice(hh * 64, (hh + 1) * 64)
                if a1 == a2:
                    for a3 in range(a1):
                        pairK2[ks, pp * 32 + hh * 16 + a1 * 4 + a3] = -1.0
                else:
                    pairK2[ks, pp * 32 + hh * 16 + pp] = 1.0
    out["pairK2"] = pairK2
    caus2 = np.zeros((32, 8), dtype=np.float32)
    for hh in range(2):
        for a in range(4):
            for ap_ in range(a + 1):
                caus2[hh * 16 + a * 4 + ap_, hh * 4 + a] = 1.0
    out["caus2"] = caus2
    denbc2 = np.zeros((8, 32), dtype=np.float32)
    for hh in range(2):
        for a in range(4):
            for ap_ in range(a + 1):
                denbc2[hh * 4 + a, hh * 16 + a * 4 + ap_] = 1.0
    out["denbc2"] = denbc2
    ps2 = np.zeros((32, 2048), dtype=np.float32)
    for a1 in range(4):
        for a2 in range(4):
            pp = a1 * 4 + a2
            for hh in range(2):
                ps2[hh * 16 + pp, pp * 128 + hh * 64:pp * 128 + (hh + 1) * 64] = 1.0
    out["pairsel2"] = ps2
    hsel = np.zeros((32, 16), dtype=np.float32)
    for hc in range(32):
        hsel[hc, hc >> 1] = 1.0
    out["hsel"] = hsel
    csel = np.zeros((32, 128), dtype=np.float32)
    for hc in range(32):
        for pp in range(128):
            if (hc & 1) == (pp >> 6):
                csel[hc, pp] = 1.0
    out["csel"] = csel
    out["wsel"] = np.array([[1.0 if j < core else 0.0 for j in range(8)]], np.float32)
    esel = np.zeros((8, 1024), dtype=np.float32)
    for e2 in range(8):
        esel[e2, e2 * 128:(e2 + 1) * 128] = 1.0
    out["esel"] = esel
    return out


IN_SHAPES = {
    "xs": [B, SL, D], "wq": [D, D], "wk": [D, D], "wv": [D, D],
    "wbeta": [D, 32], "bbeta": [1, 32], "wo": [D, D], "bo": [D],
    "gam": [D], "lnb": [D], "wgate": [D, NE], "wexp": [D, D], "bexp": [1, D],
    "wqkv": [3 * D, D], "wout": [D, D], "bout": [D],
    "W1": [NE, D, FFH], "b1": [NE, FFH], "W2": [NE, FFH, D], "b2": [NE, D],
    "Wr": [D, NE], "br": [NE],
    "cos8": [SL, 256], "sin8": [SL, 256], "ident": [128, 128],
    "supper": [128, 128], "lower": [128, 128], "sel": [32, 4],
    "pairK2": [128, 512], "caus2": [32, 8], "denbc2": [8, 32],
    "pairsel2": [32, 2048], "hsel": [32, 16], "csel": [32, 128], "wsel": [1, 8],
    "esel": [8, 1024],
}

INTER_SHAPES = {
    "xnT": ([128, NDT, T], BF16),
    "qtm": ([T, D], BF16),
    "ktm": ([T, D], BF16),
    "vtm": ([T, H * 65], BF16),
    "lbtm": ([T, 32], F32),
    "cl": ([B, 32, SL], F32),
    "sloc": ([B, NCH, 128, H * 65], BF16),
    "cntq": ([128, T], F32),
    "gat": ([128, 32], F32),
    "summ": ([128, 4256], F32),
    "summall": ([NCORES * 128, 4256], F32),
    "x1": ([T, D], F32),
    "xn1T": ([128, NDT, T], BF16),
    "s1tm": ([T, D], BF16),
    "g1tm": ([T, D], BF16),
    "yout": ([B, SL, D], F32),
}


def decl(nc, names, ext_in=(), ext_out=()):
    out = {}
    for n in names:
        if n in IN_SHAPES:
            out[n] = nc.dram_tensor(n, IN_SHAPES[n], F32, kind="ExternalInput").ap()
        else:
            shp, dt = INTER_SHAPES[n]
            k = "ExternalInput" if n in ext_in else (
                "ExternalOutput" if n in ext_out else "Internal")
            out[n] = nc.dram_tensor(n, shp, dt, kind=k).ap()
    return out


def emit_l1(nc, p, io):
    st = p.stack
    sb = lambda name, shape, dt: st.enter_context(nc.sbuf_tensor(p.tag + name, shape, dt))
    psm = lambda name, shape, dt=F32: st.enter_context(nc.psum_tensor(p.tag + name, shape, dt))

    xnT = sb("s_xnT", [128, NDT, T], BF16)
    wcat = sb("wcat", [128, NDT, 3104], BF16)
    cos8 = sb("cos8s", [128, NMT, 256], BF16)
    sin8 = sb("sin8s", [128, NMT, 256], BF16)
    identb = sb("identb", [128, 128], BF16)
    identf = sb("identf", [128, 128], F32)
    onesf = sb("onesf", [128, 128], F32)
    ones32 = sb("ones32", [32, SL], F32)
    bbc = sb("bbc", [128, 32], F32)
    cst = sb("cst", [128, 4], F32)

    ps_TF = psm("ps_TF", [128, 2, 512], F32)
    ps_A = psm("ps_A", [128, 2, 512], F32)

    t_idb = p.dma("gpsimd", identb[:, :], io["ident"], ch="cst")
    t_idf = p.dma("sync", identf[:, :], io["ident"], ch="cst2")
    t_cos = p.dma("gpsimd", cos8[:, :, :], io["cos8"].rearrange("(mt p) c -> p mt c", p=128), ch="cst")
    t_sin = p.dma("gpsimd", sin8[:, :, :], io["sin8"].rearrange("(mt p) c -> p mt c", p=128), ch="cst")
    for wname, c0 in (("wq", 0), ("wk", 1024), ("wv", 2048)):
        for dt in range(NDT):
            t_w = p.dma("gpsimd", wcat[:, dt, c0:c0 + 1024],
                        io[wname][dt * 128:(dt + 1) * 128, :], ch="cst")
    t_w = p.dma("gpsimd", wcat[:, :, 3072:3104],
                io["wbeta"].rearrange("(dt p) c -> p dt c", p=128), ch="cst")
    t_ones = p.op("vector", lambda v: v.memset(onesf[:, :], 1.0))
    p.opn("vector", lambda v: v.memset(cst[:, 0:1], LN_EPS))
    p.opn("vector", lambda v: v.memset(cst[:, 1:2], -0.5))
    p.opn("vector", lambda v: v.memset(cst[:, 2:3], 0.125))
    t_cst = p.op("vector", lambda v: v.memset(cst[:, 3:4], EPS_DN))
    t_o32 = p.op("vector", lambda v: v.memset(ones32[:, :], 1.0))
    bbrow = sb("bbrow", [1, 32], F32)
    t_bb = p.dma("sync", bbrow[:, :], io["bbeta"], ch="cst2")

    with ExitStack() as ph:
        s2 = lambda name, shape, dt: ph.enter_context(nc.sbuf_tensor(p.tag + name, shape, dt))
        ps_T = ph.enter_context(nc.psum_tensor(p.tag + "ps_T", [128, 2, 1024], BF16))
        xt = s2("xt", [128, 2, D], F32)
        xn16 = s2("xn16", [128, 2, D], BF16)
        stat = s2("stat", [128, 2, 16], F32)
        t_xfree = [None, None]
        t_x16free = [None, None]
        tok_cp = None
        for it in range(B * NMT):
            b, mt = divmod(it, NMT)
            bi = it % 2
            col0 = b * SL + mt * 128
            t_x = p.dma("sync", xt[:, bi, :], io["xs"][b, mt * 128:(mt + 1) * 128, :],
                        wait=t_xfree[bi], ch=f"xl{bi}")
            p.opn("vector", lambda v, bi=bi: v.bn_stats(stat[:, bi, 4:10], xt[:, bi, 0:512]),
                  wait=t_x)
            p.opn("vector", lambda v, bi=bi: v.bn_stats(stat[:, bi, 10:16], xt[:, bi, 512:1024]))
            t2 = p.op("vector", lambda v, bi=bi: v.bn_aggr(stat[:, bi, 0:2], stat[:, bi, 4:16]))
            p.opn("scalar", lambda s, bi=bi: s.activation(
                stat[:, bi, 2:3], stat[:, bi, 1:2], AF.Ln, bias=cst[:, 0:1]),
                wait=[t2, t_cst])
            t4 = p.op("scalar", lambda s, bi=bi: s.activation(
                stat[:, bi, 3:4], stat[:, bi, 2:3], AF.Exp, scale=cst[:, 1:2]))
            t5 = p.op("vector", lambda v, bi=bi: v.tensor_scalar(
                xn16[:, bi, :], xt[:, bi, :], stat[:, bi, 0:1], stat[:, bi, 3:4],
                ALU.subtract, ALU.mult), wait=[t4, t_x16free[bi]])
            t_xfree[bi] = t5
            ttr = None
            for dt in range(NDT):
                tb = dt % 2
                ttr = p.op("tensor", lambda t, bi=bi, dt=dt, tb=tb: t.transpose(
                    ps_T[:, tb, 0:128], xn16[:, bi, dt * 128:(dt + 1) * 128], identb[:, :]),
                    wait=[t5, t_idb] + ([tok_cp] if (it > 0 or dt >= 2) else []))
                tok_cp = p.op("scalar", lambda s, dt=dt, tb=tb, c=col0: s.copy(
                    xnT[:, dt, c:c + 128], ps_T[:, tb, 0:128]), wait=ttr)
            t_x16free[bi] = ttr
    t_xnT = tok_cp
    t_s0_end = [p.tok("tensor"), p.tok("scalar"), p.tok("vector")]

    for dt in range(NDT):
        p.dma("gpsimd", io["xnT"][:, dt, :], xnT[:, dt, :], wait=t_xnT, ch="sxnt")

    with ExitStack() as ph:
        s2 = lambda name, shape, dt: ph.enter_context(nc.sbuf_tensor(p.tag + name, shape, dt))
        wg = s2("wg", [128, NDT, 8], BF16)
        lE = s2("lE", [8, T], F32)
        maskE = s2("maskE", [8, T], F32)
        cntE = s2("cntE", [8, T], F32)
        ltm = s2("ltm", [128, 32, 8], F32)
        mask3 = s2("mask3", [128, 32, 8], F32)
        gw1 = s2("gw1", [128, 32], F32)
        gw2 = s2("gw2", [128, 32], F32)
        gw3 = s2("gw3", [128, 32], F32)
        gat = s2("s_gat", [128, 32], F32)
        ltot = s2("ltot", [8, 4], F32)

        t_wg = p.dma("gpsimd", wg[:, :, :], io["wgate"].rearrange("(dt p) c -> p dt c", p=128),
                     wait=t_s0_end, ch="cst")
        t_lE = None
        for i in range(8):
            pb = i % 2
            c0 = i * 512
            tokm = None
            for dt in range(NDT):
                tokm = p.op("tensor", lambda t, dt=dt, pb=pb, c0=c0: t.matmul(
                    ps_A[0:8, pb, :], wg[:, dt, :], xnT[:, dt, c0:c0 + 512],
                    start=(dt == 0), stop=(dt == NDT - 1)),
                    wait=([t_xnT, t_wg] + ([t_lE] if i >= 2 else [])) if dt == 0 else [])
            t_lE = p.op("vector", lambda v, pb=pb, c0=c0: v.tensor_copy(
                lE[:, c0:c0 + 512], ps_A[0:8, pb, :]), wait=tokm)
        t_ltm = None
        for c in range(32):
            tb = c % 2
            ttr = p.op("tensor", lambda t, c=c, tb=tb: t.transpose(
                ps_TF[:, tb, 0:8], lE[:, c * 128:(c + 1) * 128], identf[0:8, 0:8]),
                wait=[t_lE, t_idf] + ([t_ltm] if c >= 2 else []))
            t_ltm = p.op("scalar", lambda s, c=c, tb=tb: s.copy(
                ltm[:, c, :], ps_TF[:, tb, 0:8]), wait=ttr)
        p.opn("vector", lambda v: v.tensor_copy(gw1[:, :], ltm[:, :, 0]), wait=t_ltm)
        for e2 in range(1, NE):
            p.opn("vector", lambda v, e2=e2: v.tensor_max(gw1[:, :], gw1[:, :], ltm[:, :, e2]))
        p.opn("vector", lambda v: v.memset(gw2[:, :], 0.0))
        for e2 in range(NE):
            p.opn("vector", lambda v, e2=e2: v.tensor_tensor(
                gw3[:, :], ltm[:, :, e2], gw1[:, :], ALU.is_ge))
            p.opn("vector", lambda v, e2=e2: v.tensor_tensor(
                mask3[:, :, e2], gw3[:, :], gw2[:, :], ALU.mult))
            p.opn("vector", lambda v, e2=e2: v.tensor_tensor(
                mask3[:, :, e2], gw3[:, :], mask3[:, :, e2], ALU.subtract))
            p.opn("vector", lambda v, e2=e2: v.tensor_tensor(
                gw2[:, :], gw2[:, :], mask3[:, :, e2], ALU.add))
        for e2 in range(NE):
            p.opn("vector", lambda v, e2=e2: v.tensor_tensor(
                ltm[:, :, e2], ltm[:, :, e2], gw1[:, :], ALU.subtract))
        t_sub = p.op("vector", lambda v: v.tensor_copy(gw3[:, :], gw1[:, :]))
        t_exp = p.op("scalar", lambda s: s.activation(
            ltm[:, :, :], ltm[:, :, :], AF.Exp), wait=t_sub)
        p.opn("vector", lambda v: v.tensor_reduce(gw3[:, :], ltm[:, :, :], AXX, ALU.add),
              wait=t_exp)
        t_gat = p.op("vector", lambda v: v.reciprocal(gat[:, :], gw3[:, :]))
        p.dma("gpsimd", io["gat"], gat[:, :], wait=t_gat, ch="sgat")
        t_mE = None
        for c in range(32):
            tb = c % 2
            ttr = p.op("tensor", lambda t, c=c, tb=tb: t.transpose(
                ps_TF[0:8, tb, 0:128], mask3[:, c, :], identf[:, :]),
                wait=[t_gat] + ([t_mE] if c >= 2 else []))
            t_mE = p.op("vector", lambda v, c=c, tb=tb: v.tensor_copy(
                maskE[:, c * 128:(c + 1) * 128], ps_TF[0:8, tb, 0:128]), wait=ttr)
        for b in range(B):
            p.opn("vector", lambda v, b=b: v.tensor_tensor_scan(
                cntE[:, b * SL:(b + 1) * SL], ones32[0:8, :],
                maskE[:, b * SL:(b + 1) * SL], 0.0, ALU.mult, ALU.add),
                wait=[t_mE, t_o32] if b == 0 else [])
            p.opn("vector", lambda v, b=b: v.tensor_reduce(
                ltot[:, b:b + 1], maskE[:, b * SL:(b + 1) * SL], AXX, ALU.add))
        t_cnt = p.op("vector", lambda v: v.tensor_copy(gw3[0:8, 0:4], ltot[:, :]))
        p.dma("gpsimd", io["cntq"][0:8, :], lE[:, :], wait=t_cnt, ch="sgat")
        p.dma("gpsimd", io["cntq"][32:40, :], maskE[:, :], wait=t_cnt, ch="sgat")
        t_g_sp = p.dma("gpsimd", io["cntq"][64:72, :], cntE[:, :], wait=t_cnt, ch="sgat")
        p.dma("gpsimd", io["summ"][0:8, 4224:4228], ltot[:, :], wait=t_cnt)
    t_gate_v = p.tok("vector")

    ps_S = psm("ps_S", [64, 2, 512], F32)
    ps_B = psm("ps_B", [128, 512], F32)
    qtm = sb("s_qtm", [128, NMT, D], BF16)
    ktm = sb("s_ktm", [128, NMT, D], BF16)
    vtm = sb("s_vtm", [128, NMT, H * 65], BF16)
    lbT = sb("s_lbT", [128, NMT, 32], F32)
    lbe = sb("lbe", [32, SL], F32)
    cl = sb("s_cl", [32, SL], F32)
    Sst = sb("Sst", [64, 2, 2, H, 65], F32)
    at32 = sb("at32", [32, 4, 128], F32)
    atm = sb("atm", [128, 32], F32)
    scr32 = sb("scr32", [32, 2, 32], F32)
    albc = sb("albc", [64, 32], F32)
    ptbc = sb("ptbc", [64, 2, 16], F32)
    kst = sb("kst", [128, 2, 64], BF16)
    rtmp = sb("rtmp", [128, 4, 256], BF16)
    btmp = sb("btmp", [128, 2, 32], F32)

    t_m1 = p.op("tensor", lambda t: t.matmul(
        ps_B[:, 0:32], onesf[0:1, :], bbrow[:, :], start=True, stop=True),
        wait=[t_bb, t_ones, t_gate_v])
    t_bbc = p.op("vector", lambda v: v.tensor_copy(bbc[:, :], ps_B[:, 0:32]), wait=t_m1)
    p.opn("vector", lambda v: v.memset(scr32[:, :, :], 0.0), wait=t_g_sp)
    t_vones = p.op("vector", lambda v: v.memset(
        vtm[:, :, :].rearrange("p mt (h q) -> p (mt h) q", q=65)[:, :, 64:65], 1.0),
        wait=t_g_sp)

    t_ech = [t_gate_v, t_gate_v]
    t_bprev = [t_g_sp]
    t_pb_prev = None
    for b in range(B):
        t_lbT = None
        for mt in range(NMT):
            col0 = b * SL + mt * 128
            for ech in range(7):
                i = mt * 7 + ech
                pb = i % 2
                ew = 512 if ech < 6 else 32
                e0 = ech * 512
                tokm = None
                for dt in range(NDT):
                    tokm = p.op("tensor", lambda t, dt=dt, pb=pb, e0=e0, ew=ew, c=col0: t.matmul(
                        ps_A[:, pb, 0:ew], xnT[:, dt, c:c + 128], wcat[:, dt, e0:e0 + ew],
                        start=(dt == 0), stop=(dt == NDT - 1)),
                        wait=([t_xnT, t_w, t_ech[pb]] if dt == 0 else []))
                wext = t_bprev if mt == 0 else []
                if ech < 2:
                    t_ech[pb] = p.op("vector", lambda v, pb=pb, mt=mt, e0=e0: v.tensor_copy(
                        qtm[:, mt, e0:e0 + 512], ps_A[:, pb, 0:512]), wait=[tokm] + wext)
                elif ech < 4:
                    t_ech[pb] = p.op("vector", lambda v, pb=pb, mt=mt, e0=e0: v.tensor_copy(
                        ktm[:, mt, e0 - 1024:e0 - 1024 + 512], ps_A[:, pb, 0:512]),
                        wait=[tokm] + wext)
                elif ech < 6:
                    h0 = (ech - 4) * 8
                    t_ech[pb] = p.op("vector", lambda v, pb=pb, mt=mt, h0=h0: v.tensor_copy(
                        vtm[:, mt, :].rearrange("p (h q) -> p h q", q=65)[:, h0:h0 + 8, 0:64],
                        ps_A[:, pb, 0:512].rearrange("p (h q) -> p h q", q=64)),
                        wait=[tokm, t_vones] + wext)
                else:
                    tb0 = p.op("vector", lambda v, pb=pb: v.tensor_tensor(
                        btmp[:, 0, :], ps_A[:, pb, 0:32], bbc[:, :], ALU.add),
                        wait=[tokm, t_bbc] + wext)
                    tb1 = p.op("scalar", lambda s: s.activation(
                        btmp[:, 1, :], btmp[:, 0, :], AF.Sigmoid), wait=tb0)
                    tb2 = p.op("vector", lambda v: v.tensor_scalar(
                        btmp[:, 0, :], btmp[:, 1, :], 0.999, 0.001, ALU.min, ALU.max),
                        wait=tb1)
                    t_lbT = p.op("scalar", lambda s, mt=mt: s.activation(
                        lbT[:, mt, :], btmp[:, 0, :], AF.Ln), wait=tb2)
                    t_ech[pb] = t_lbT
        t_rope = None
        first = True
        for mt in range(NMT):
            for src in (qtm, ktm):
                for half in range(2):
                    hs = half * 8
                    qv = src[:, mt, :].rearrange("p (h q) -> p h q", q=64)[:, hs:hs + 8, :]
                    q1 = qv[:, :, 0:32]
                    q2 = qv[:, :, 32:64]
                    cv = cos8[:, mt, :].rearrange("p (h q) -> p h q", q=32)
                    sv = sin8[:, mt, :].rearrange("p (h q) -> p h q", q=32)
                    r0 = rtmp[:, 0, :].rearrange("p (h q) -> p h q", q=32)
                    r1 = rtmp[:, 1, :].rearrange("p (h q) -> p h q", q=32)
                    r2 = rtmp[:, 2, :].rearrange("p (h q) -> p h q", q=32)
                    r3 = rtmp[:, 3, :].rearrange("p (h q) -> p h q", q=32)
                    w0 = [t_cos, t_sin, t_lbT] if first else []
                    first = False
                    p.opn("vector", lambda v, a=r0, x=q1, y=cv: v.tensor_mul(a, x, y), wait=w0)
                    p.opn("vector", lambda v, a=r1, x=q2, y=sv: v.tensor_mul(a, x, y))
                    p.opn("vector", lambda v, a=r2, x=q1, y=sv: v.tensor_mul(a, x, y))
                    p.opn("vector", lambda v, a=r3, x=q2, y=cv: v.tensor_mul(a, x, y))
                    p.opn("vector", lambda v, a=q1, x=r0, y=r1: v.tensor_sub(a, x, y))
                    t_rope = p.op("vector", lambda v, a=q2, x=r2, y=r3: v.tensor_add(a, x, y))
            for src in (qtm, ktm):
                uu = src[:, mt, :]
                ft = rtmp[:, :, :].rearrange("p a c -> p (a c)")
                p.opn("vector", lambda v, u=uu, ft=ft: v.tensor_scalar(
                    ft, u, 0.0, None, ALU.min))
                tmn = p.op("vector", lambda v, u=uu: v.tensor_scalar(u, u, 0.0, None, ALU.max))
                tex = p.op("scalar", lambda s, ft=ft: s.activation(ft, ft, AF.Exp), wait=tmn)
                t_rope = p.op("vector", lambda v, u=uu, ft=ft: v.tensor_tensor(
                    u, u, ft, ALU.add), wait=tex)
        t_qspill = p.dma("gpsimd",
                         io["qtm"][b * SL:(b + 1) * SL, :].rearrange("(mt p) e -> p mt e", p=128),
                         qtm[:, :, :], wait=t_rope, ch="sq")
        t_lbe = None
        for mt in range(NMT):
            tb = mt % 2
            ttr = p.op("tensor", lambda t, mt=mt, tb=tb: t.transpose(
                ps_TF[0:32, tb, 0:128], lbT[:, mt, :], identf[:, :]),
                wait=[t_lbT, t_idf] + ([t_lbe] if mt >= 2 else []) + (t_bprev if mt < 2 else []))
            t_lbe = p.op("vector", lambda v, mt=mt, tb=tb: v.tensor_copy(
                lbe[:, mt * 128:(mt + 1) * 128], ps_TF[0:32, tb, 0:128]), wait=ttr)
        t_cl = p.op("vector", lambda v: v.tensor_tensor_scan(
            cl[:, :], ones32[:, :], lbe[:, :], 0.0, ALU.mult, ALU.add), wait=t_lbe)
        p.dma("gpsimd", io["cl"][b, :, :], cl[:, :], wait=t_cl, ch="scl")
        p.dma("gpsimd",
              io["lbtm"][b * SL:(b + 1) * SL, :].rearrange("(mt p) e -> p mt e", p=128),
              lbT[:, :, :], wait=t_lbT, ch="scl")
        t_s0 = p.op("vector", lambda v: v.memset(Sst[:, 0, :, :, :], 0.0), wait=t_bprev)
        t_supd = t_s0
        t_al = t_pb_prev if b > 0 else t_bbc
        supd_hist = []
        t_kst_user = None
        t_atm_user = None
        for ch in range(NCH):
            pg = ch % 2
            c0 = ch * 128
            if ch == 0:
                t_cll = p.op("vector", lambda v, c0=c0: v.tensor_copy(
                    at32[:, 0, :], cl[:, c0:c0 + 128]), wait=t_cl)
            else:
                t_cll = p.op("vector", lambda v, c0=c0: v.tensor_scalar(
                    at32[:, 0, :], cl[:, c0:c0 + 128], cl[:, c0 - 1:c0], None,
                    ALU.subtract))
            t_ae = p.op("scalar", lambda s: s.activation(
                at32[:, 2, :], at32[:, 0, :], AF.Exp), wait=t_cll)
            t_tl = p.op("vector", lambda v: v.tensor_scalar(
                at32[:, 1, :], at32[:, 0, :], -1.0, at32[:, 0, 127:128], ALU.mult, ALU.add))
            t_at = p.op("scalar", lambda s: s.activation(
                at32[:, 3, :], at32[:, 1, :], AF.Exp), wait=t_tl)
            ttr = p.op("tensor", lambda t: t.transpose(
                ps_TF[:, 0, 0:32], at32[:, 3, :], identf[0:32, 0:32]),
                wait=[t_at, t_atm_user, t_lbe])
            t_atm = p.op("vector", lambda v: v.tensor_copy(atm[:, :], ps_TF[:, 0, 0:32]),
                         wait=ttr)
            t_c1 = p.op("vector", lambda v: v.tensor_copy(scr32[:, 0, 0:1], at32[:, 2, 127:128]),
                        wait=t_ae)
            t_c2 = p.op("vector", lambda v: v.transpose(scr32[:, 1, :], scr32[:, 0, :]),
                        wait=t_c1)
            t_c3 = p.op("tensor", lambda t: t.matmul(
                ps_B[0:64, 0:32], onesf[0:1, 0:64], scr32[0:1, 1, :], start=True, stop=True),
                wait=[t_c2, t_ones, t_al])
            t_al = p.op("vector", lambda v: v.tensor_copy(albc[:, :], ps_B[0:64, 0:32]),
                        wait=t_c3)
            t_sp = p.dma("gpsimd", io["sloc"][b, ch, 0:64, :],
                         Sst[:, pg, 0, :, :].rearrange("p h q -> p (h q)"), wait=t_supd,
                         ch="ssl")
            t_sp = p.dma("gpsimd", io["sloc"][b, ch, 64:128, :],
                         Sst[:, pg, 1, :, :].rearrange("p h q -> p (h q)"), wait=t_supd,
                         ch="ssl")
            for h in range(H):
                t_k0 = p.op("vector", lambda v, h=h, ch=ch: v.tensor_scalar(
                    kst[:, 0, :], ktm[:, ch, h * 64:(h + 1) * 64], atm[:, 2 * h:2 * h + 1],
                    None, ALU.mult), wait=[t_atm, t_kst_user])
                t_k1 = p.op("vector", lambda v, h=h, ch=ch: v.tensor_scalar(
                    kst[:, 1, :], ktm[:, ch, h * 64:(h + 1) * 64], atm[:, 2 * h + 1:2 * h + 2],
                    None, ALU.mult))
                for c2 in range(2):
                    pb2 = (h * 2 + c2) % 2
                    wmm = [t_k1, t_sp]
                    if len(supd_hist) >= 2:
                        wmm.append(supd_hist[-2])
                    tokm = p.op("tensor", lambda t, h=h, c2=c2, pb2=pb2, ch=ch: t.matmul(
                        ps_S[:, pb2, 0:65], kst[:, c2, :],
                        vtm[:, ch, h * 65:(h + 1) * 65], start=True, stop=True), wait=wmm)
                    t_kst_user = tokm
                    t_supd = p.op("vector", lambda v, h=h, c2=c2, pb2=pb2, pg=pg: v.scalar_tensor_tensor(
                        Sst[:, 1 - pg, c2, h, :], Sst[:, pg, c2, h, :],
                        albc[:, 2 * h + c2:2 * h + c2 + 1],
                        ps_S[:, pb2, 0:65], ALU.mult, ALU.add), wait=[tokm, t_al])
                    supd_hist.append(t_supd)
            t_atm_user = t_kst_user
        fin = NCH % 2
        t_pe = p.op("scalar", lambda s: s.activation(
            scr32[:, 0, 0:1], cl[:, SL - 1:SL], AF.Exp), wait=[t_cl, t_c2])
        t_pt = p.op("vector", lambda v: v.transpose(scr32[:, 1, :], scr32[:, 0, :]), wait=t_pe)
        t_pm0 = p.op("tensor", lambda t: t.matmul(
            ps_B[0:64, 0:16], onesf[0:1, 0:64],
            scr32[0:1, 1, :].rearrange("o (h c) -> o h c", c=2)[:, :, 0],
            start=True, stop=True), wait=[t_pt, t_ones, t_al])
        t_pb0 = p.op("vector", lambda v: v.tensor_copy(ptbc[:, 0, :], ps_B[0:64, 0:16]),
                     wait=t_pm0)
        t_pm1 = p.op("tensor", lambda t: t.matmul(
            ps_B[0:64, 0:16], onesf[0:1, 0:64],
            scr32[0:1, 1, :].rearrange("o (h c) -> o h c", c=2)[:, :, 1],
            start=True, stop=True), wait=t_pb0)
        t_pb1 = p.op("vector", lambda v: v.tensor_copy(ptbc[:, 1, :], ps_B[0:64, 0:16]),
                     wait=t_pm1)
        t_sd1 = p.dma("gpsimd", io["summ"][0:64, b * 1040:(b + 1) * 1040],
                      Sst[:, fin, 0, :, :].rearrange("p h q -> p (h q)"), wait=t_supd, ch="ssm")
        t_sd1b = p.dma("gpsimd", io["summ"][64:128, b * 1040:(b + 1) * 1040],
                       Sst[:, fin, 1, :, :].rearrange("p h q -> p (h q)"), wait=t_supd, ch="ssm")
        t_sd2 = p.dma("gpsimd", io["summ"][0:64, 4160 + b * 16:4160 + (b + 1) * 16],
                      ptbc[:, 0, :], wait=[t_pb0, t_pb1], ch="ssm")
        t_sd2b = p.dma("gpsimd", io["summ"][64:128, 4160 + b * 16:4160 + (b + 1) * 16],
                       ptbc[:, 1, :], wait=[t_pb0, t_pb1], ch="ssm")
        t_kspill = p.dma("gpsimd",
                         io["ktm"][b * SL:(b + 1) * SL, :].rearrange("(mt p) e -> p mt e", p=128),
                         ktm[:, :, :], wait=t_supd, ch="sq")
        t_vspill = p.dma("gpsimd",
                         io["vtm"][b * SL:(b + 1) * SL, :].rearrange("(mt p) e -> p mt e", p=128),
                         vtm[:, :, :], wait=t_supd, ch="sq")
        t_bprev = [t_qspill, t_kspill, t_vspill, t_sd1, t_sd1b, t_sd2, t_sd2b, t_kst_user]
        t_pb_prev = t_pb1


import os as _os
RACE_DETECT = bool(int(_os.environ.get("RACE_DETECT", "0")))
L2_TRUNC = int(_os.environ.get("L2_TRUNC", "9"))


def build_l1():
    nc = bass.Bass("TRN2", target_bir_lowering=False, detect_race_conditions=RACE_DETECT)
    io = decl(nc, ["xs", "wq", "wk", "wv", "wbeta", "bbeta", "wgate",
                   "cos8", "sin8", "ident",
                   "xnT", "qtm", "ktm", "vtm", "lbtm", "cl", "sloc",
                   "cntq", "gat", "summ"],
              ext_out=["xnT", "qtm", "ktm", "vtm", "lbtm", "cl", "sloc",
                       "cntq", "gat", "summ"])
    with ExitStack() as stack:
        p = Prog(nc, stack)
        blk = stack.enter_context(nc.Block())
        emit_l1(nc, p, io)
        p.final_waits()
        p.emit(blk)
    return nc




def emit_l2(nc, p, io):
    st = p.stack
    sb = lambda name, shape, dt: st.enter_context(nc.sbuf_tensor(p.tag + name, shape, dt))
    psm = lambda name, shape, dt=F32: st.enter_context(nc.psum_tensor(p.tag + name, shape, dt))

    xnT = sb("s_xnT", [128, NDT, T], BF16)
    identb = sb("identb", [128, 128], BF16)
    identf = sb("identf", [128, 128], F32)
    onesf = sb("onesf", [128, 128], F32)
    lowerf = sb("lowerf", [128, 128], F32)
    supper16 = sb("supper16", [128, 128], BF16)
    hselc = sb("hselc", [32, 16], F32)
    cselc = sb("cselc", [32, 128], F32)
    cst = sb("cst", [128, 4], F32)
    wo16 = sb("wo16", [128, NDT, D], BF16)
    wexp16 = sb("wexp16", [128, NDT, D], BF16)
    bo8 = sb("bo8", [128, NDT], F32)
    gam8 = sb("gam8", [128, NDT], F32)
    lnb8 = sb("lnb8", [128, NDT], F32)
    bexbc = sb("bexbc", [128, D], F32)
    gatl = sb("gatl", [128, 32], F32)
    kgate = sb("kgate", [128, 32], F32)
    sin16 = sb("sin16", [128, B, H, 65], BF16)
    onesc = sb("onesc", [128, 1], F32)
    onescb = sb("onescb", [128, 1], BF16)

    ps_tr = psm("ps_tr", [128, 2, 1024], BF16)
    ps_w = psm("ps_w", [128, 2, 512], F32)
    ps_y = psm("ps_y", [128, 512], F32)
    ps_ln = psm("ps_ln", [128, 2, 512], F32)

    t_idb = p.dma("gpsimd", identb[:, :], io["ident"], ch="cst")
    t_idf = p.dma("sync", identf[:, :], io["ident"], ch="cst2")
    t_low = p.dma("sync", lowerf[:, :], io["lower"], ch="cst2")
    t_sup = p.dma("gpsimd", supper16[:, :], io["supper"], ch="cst")
    t_hs = p.dma("sync", hselc[:, :], io["hsel"], ch="cst2")
    t_cs = p.dma("gpsimd", cselc[:, :], io["csel"], ch="cst")
    for dt in range(NDT):
        t_wo = p.dma("gpsimd", wo16[:, dt, :], io["wo"][dt * 128:(dt + 1) * 128, :], ch="cst")
        t_wx = p.dma("gpsimd", wexp16[:, dt, :], io["wexp"][dt * 128:(dt + 1) * 128, :], ch="cst")
    t_bo = p.dma("sync", bo8[:, :], io["bo"].rearrange("(dt p) -> p dt", p=128), allow_slow_non_contiguous=True, ch="cst2")
    t_gm = p.dma("sync", gam8[:, :], io["gam"].rearrange("(dt p) -> p dt", p=128), allow_slow_non_contiguous=True, ch="cst2")
    t_lb = p.dma("sync", lnb8[:, :], io["lnb"].rearrange("(dt p) -> p dt", p=128), allow_slow_non_contiguous=True, ch="cst2")
    t_ga = p.dma("sync", gatl[:, :], io["gat"], ch="cst2")
    for dt in range(NDT):
        t_xnl = p.dma("gpsimd", xnT[:, dt, :], io["xnT"][:, dt, :], ch="cxn")
    t_one = p.op("vector", lambda v: v.memset(onesf[:, :], 1.0))
    t_onc = p.op("vector", lambda v: v.memset(onesc[:, :], 1.0))
    t_oncb = p.op("vector", lambda v: v.memset(onescb[:, :], 1.0))
    p.op("vector", lambda v: v.memset(cst[:, 0:1], LN_EPS))
    p.op("vector", lambda v: v.memset(cst[:, 1:2], -0.5))
    p.op("vector", lambda v: v.memset(cst[:, 2:3], 0.125))
    t_cst = p.op("vector", lambda v: v.memset(cst[:, 3:4], EPS_DN))
    bexr = sb("bexr", [1, D], F32)
    t_bx = p.dma("sync", bexr[:, :], io["bexp"], ch="cst2")
    for half in range(2):
        t_bm = p.op("tensor", lambda t, half=half: t.matmul(
            ps_ln[:, half, :], onesf[0:1, :], bexr[0:1, half * 512:(half + 1) * 512],
            start=True, stop=True), wait=[t_bx, t_one])
        t_bxc = p.op("vector", lambda v, half=half: v.tensor_copy(
            bexbc[:, half * 512:(half + 1) * 512], ps_ln[:, half, :]), wait=t_bm)

    with ExitStack() as ph:
        s2 = lambda name, shape, dt: ph.enter_context(nc.sbuf_tensor(p.tag + name, shape, dt))
        prj = s2("prj", [128, 1, 4256], F32)
        acc = s2("acc", [128, B, H, 65], F32)
        wbc = s2("wbc", [128, 8], F32)
        w1bc = s2("w1bc", [128, 8], F32)
        ptt = s2("ptt", [128, 2, 16], F32)
        wrow = s2("wrow", [1, 8], F32)
        cnta = s2("cnta", [32, 8], F32)
        cnt8 = s2("cnt8", [8, 32], F32)
        selc = s2("selc", [32, 4], F32)
        offs = s2("offs", [8, 8], F32)
        keepE = s2("keepE", [8, T], F32)
        cntE = s2("cntE", [8, T], F32)
        maskE = s2("maskE", [8, T], F32)
        kw = s2("kw", [8, SL], F32)
        kt3 = s2("kt3", [128, 32, 8], F32)

        t_wr = p.dma("sync", wrow[:, :], io["wsel"], ch="cst2")
        t_wm = p.op("tensor", lambda t: t.matmul(
            ps_ln[:, 0, 0:8], onesf[0:1, :], wrow[:, :], start=True, stop=True),
            wait=[t_wr, t_one, t_bxc])
        t_wb = p.op("vector", lambda v: v.tensor_copy(wbc[:, :], ps_ln[:, 0, 0:8]), wait=t_wm)
        t_w1 = p.op("vector", lambda v: v.tensor_scalar(
            w1bc[:, :], wbc[:, :], -1.0, 1.0, ALU.mult, ALU.add))
        t_acc0 = p.op("vector", lambda v: v.memset(acc[:, :, :, :], 0.0))
        t_prfree = [None]
        for j in range(NCORES - 1):
            bj = 0
            t_pr = p.dma("gpsimd", prj[:, bj, :], io["summall"][j * 128:(j + 1) * 128, :],
                         wait=t_prfree[bj], ch="cpr")
            for b in range(B):
                t_p1 = p.op("vector", lambda v, bj=bj, b=b, j=j: v.tensor_scalar(
                    ptt[:, bj, :], prj[:, bj, 4160 + b * 16:4160 + (b + 1) * 16],
                    wbc[:, j:j + 1], w1bc[:, j:j + 1], ALU.mult, ALU.add),
                    wait=[t_pr, t_w1, t_acc0])
                t_p2 = p.op("vector", lambda v, bj=bj, b=b: v.tensor_tensor(
                    acc[:, b, :, :], acc[:, b, :, :],
                    ptt[:, bj, :].rearrange("p (h q) -> p h q", q=1).broadcast_to([128, 16, 65]),
                    ALU.mult))
                t_p3 = p.op("vector", lambda v, bj=bj, b=b, j=j: v.scalar_tensor_tensor(
                    acc[:, b, :, :],
                    prj[:, bj, b * 1040:(b + 1) * 1040].rearrange("p (h q) -> p h q", q=65),
                    wbc[:, j:j + 1], acc[:, b, :, :], ALU.mult, ALU.add))
            t_prfree[0] = t_p3
        t_s16 = p.op("scalar", lambda s: s.copy(sin16[:, :, :, :], acc[:, :, :, :]),
                     wait=t_p3 if NCORES > 1 else t_acc0)
        t_c8 = p.dma("gpsimd", cnt8[:, :].rearrange("p (j b) -> p j b", b=4),
                     io["summall"][:, 4224:4228].rearrange("(j p) b -> p j b", p=128)[0:8, :, :],
                     allow_slow_non_contiguous=True, ch="cpr")
        t_ct = p.op("tensor", lambda t: t.transpose(
            ps_w[0:32, 0, 0:8], cnt8[:, :], identf[0:8, 0:8]), wait=[t_c8, t_idf])
        t_ca = p.op("vector", lambda v: v.tensor_copy(cnta[:, :], ps_w[0:32, 0, 0:8]),
                    wait=t_ct)
        t_se = p.dma("sync", selc[:, :], io["sel"], ch="cst2")
        t_om = p.op("tensor", lambda t: t.matmul(
            ps_ln[0:8, 1, 0:4], cnta[:, :], selc[:, :], start=True, stop=True),
            wait=[t_ca, t_se, t_wb])
        t_of = p.op("vector", lambda v: v.tensor_scalar(
            offs[:, 0:4], ps_ln[0:8, 1, 0:4], -1.0, None, ALU.add), wait=t_om)
        t_ma = p.dma("sync", maskE[:, :], io["cntq"][32:40, :], ch="cst2")
        t_cn = p.dma("sync", cntE[:, :], io["cntq"][64:72, :], ch="cst2")
        t_ke = None
        for b in range(B):
            p.op("vector", lambda v, b=b: v.tensor_scalar(
                kw[:, :], cntE[:, b * SL:(b + 1) * SL], offs[:, b:b + 1], None, ALU.add),
                wait=[t_cn, t_of])
            p.op("vector", lambda v: v.tensor_single_scalar(kw[:, :], kw[:, :], 100.0, ALU.is_lt))
            t_ke = p.op("vector", lambda v, b=b: v.tensor_tensor(
                keepE[:, b * SL:(b + 1) * SL], kw[:, :], maskE[:, b * SL:(b + 1) * SL],
                ALU.mult), wait=t_ma)
        t_kc = None
        for c in range(32):
            tb = c % 2
            ttr = p.op("tensor", lambda t, c=c, tb=tb: t.transpose(
                ps_w[:, tb, 0:8], keepE[:, c * 128:(c + 1) * 128], identf[0:8, 0:8]),
                wait=[t_ke, t_idf] + ([t_kc] if c >= 2 else []))
            t_kc = p.op("scalar", lambda s, c=c, tb=tb: s.copy(
                kt3[:, c, :], ps_w[:, tb, 0:8]), wait=ttr)
        t_ks = p.op("vector", lambda v: v.tensor_reduce(kgate[:, :], kt3[:, :, :], AXX, ALU.add),
                    wait=t_kc)
        t_kg = p.op("vector", lambda v: v.tensor_tensor(
            kgate[:, :], kgate[:, :], gatl[:, :], ALU.mult), wait=t_ga)
    t_stageA = [p.tok("vector"), p.tok("scalar"), p.tok("tensor"), t_kg, t_s16]

    qb = sb("qb", [128, NMT, D], BF16)
    kb = sb("kb", [128, NMT, D], BF16)
    vb = sb("vb", [128, NMT, H * 65], BF16)
    lbt = sb("lbt", [128, NMT, 32], F32)
    clb = sb("clb", [32, SL], F32)
    slc = sb("slc", [128, 1, H, 65], BF16)
    sgc = sb("sgc", [128, H, 65], BF16)
    plb = sb("plb", [128, 2, 16], F32)
    hw = sb("hw", [32, 16], F32)
    at32 = sb("at32", [32, 3, 128], F32)
    atm = sb("atm", [128, 32], F32)
    qT = sb("qT", [64, 128], BF16)
    kT = sb("kT", [64, 128], BF16)
    qs2 = sb("qs2", [128, 128], BF16)
    qsT = sb("qsT", [128, 128], BF16)
    msc = sb("msc", [128, 128], F32)
    rd = sb("rd", [128, 2, 128], BF16)
    ee = sb("ee", [128, 2, 128], BF16)
    ds = sb("ds", [128, 128], BF16)
    rcol = sb("rcol", [128, 2], F32)
    ytm = sb("ytm", [128, D], BF16)
    yT = sb("yT", [128, NDT, 128], BF16)
    h0 = sb("h0", [128, NDT, 128], F32)
    lnr = sb("lnr", [1, 5, 128], F32)
    s16 = sb("s16", [128, NDT, 128], BF16)
    stm = sb("stm", [128, D], BF16)
    gtm = sb("gtm", [128, D], BF16)
    ge1 = sb("ge1", [128, D], BF16)
    xtl = sb("xtl", [128, D], F32)
    x1t = sb("x1t", [128, D], F32)
    xs16 = sb("xs16", [128, D], BF16)
    xst = sb("xst", [128, 1, 16], F32)
    x1T = sb("x1T", [128, NDT, 128], BF16)

    t_bload = [None]
    t_chend = [None]
    t_x1sp = None
    t_xnsp = None
    for b in range(B):
        w_b = t_stageA if b == 0 else t_chend
        t_q = p.dma("gpsimd", qb[:, :, :],
                    io["qtm"][b * SL:(b + 1) * SL, :].rearrange("(mt p) e -> p mt e", p=128),
                    wait=w_b, ch="cq")
        t_k = p.dma("gpsimd", kb[:, :, :],
                    io["ktm"][b * SL:(b + 1) * SL, :].rearrange("(mt p) e -> p mt e", p=128),
                    wait=w_b, ch="ck")
        t_v = p.dma("gpsimd", vb[:, :, :],
                    io["vtm"][b * SL:(b + 1) * SL, :].rearrange("(mt p) e -> p mt e", p=128),
                    wait=w_b, ch="cv")
        t_lt = p.dma("gpsimd", lbt[:, :, :],
                     io["lbtm"][b * SL:(b + 1) * SL, :].rearrange("(mt p) e -> p mt e", p=128),
                     wait=w_b, ch="cq")
        t_cl = p.dma("sync", clb[:, :], io["cl"][b, :, :], wait=w_b, ch="ccl")
        for ch in range(NCH if L2_TRUNC >= 1 else 0):
            tt = b * NMT + ch
            col0 = b * SL + ch * 128
            sli = 0
            t_sl = p.dma("gpsimd", slc[:, sli, :, :].rearrange("p h q -> p (h q)"),
                         io["sloc"][b, ch, :, :], wait=t_chend, ch="csl")
            if ch == 0:
                t_pl = p.op("vector", lambda v: v.memset(plb[:, 0, :], 1.0), wait=t_chend)
                pli = 0
            else:
                pli = ch % 2
                t_pe = p.op("scalar", lambda s, ch=ch: s.activation(
                    at32[:, 0, 0:1], clb[:, ch * 128 - 1:ch * 128], AF.Exp),
                    wait=[t_cl] + t_chend)
                t_hw = p.op("vector", lambda v: v.tensor_scalar(
                    hw[:, :], hselc[:, :], at32[:, 0, 0:1], None, ALU.mult),
                    wait=[t_pe, t_hs])
                t_plm = p.op("tensor", lambda t, pli=pli: t.matmul(
                    ps_ln[:, pli, 0:16], cselc[:, :], hw[:, :], start=True, stop=True),
                    wait=[t_hw, t_cs] + t_chend)
                t_pl = p.op("vector", lambda v, pli=pli: v.tensor_copy(
                    plb[:, pli, :], ps_ln[:, pli, 0:16]), wait=t_plm)
            t_sg1 = p.op("vector", lambda v, pli=pli, b=b: v.tensor_tensor(
                sgc[:, :, :], sin16[:, b, :, :],
                plb[:, pli, :].rearrange("p (h q) -> p h q", q=1).broadcast_to([128, 16, 65]),
                ALU.mult), wait=[t_s16, t_pl])
            t_sg = p.op("vector", lambda v, sli=sli: v.tensor_tensor(
                sgc[:, :, :], sgc[:, :, :], slc[:, sli, :, :], ALU.add), wait=t_sl)
            if ch == 0:
                t_cll = p.op("vector", lambda v: v.tensor_copy(at32[:, 1, :], clb[:, 0:128]),
                             wait=[t_cl] + t_chend)
            else:
                t_cll = p.op("vector", lambda v, ch=ch: v.tensor_scalar(
                    at32[:, 1, :], clb[:, ch * 128:(ch + 1) * 128],
                    clb[:, ch * 128 - 1:ch * 128], None, ALU.subtract), wait=[t_cl] + t_chend)
            t_ae = p.op("scalar", lambda s: s.activation(
                at32[:, 2, :], at32[:, 1, :], AF.Exp), wait=t_cll)
            ttr = p.op("tensor", lambda t: t.transpose(
                ps_w[:, 0, 0:32], at32[:, 2, :], identf[0:32, 0:32]),
                wait=[t_ae, t_idf] + t_chend)
            t_atm = p.op("vector", lambda v: v.tensor_copy(atm[:, :], ps_w[:, 0, 0:32]),
                         wait=ttr)
            t_hlast = None
            for h in range(H if L2_TRUNC >= 2 else 0):
                tq1 = p.op("tensor", lambda t, h=h, ch=ch: t.transpose(
                    ps_tr[0:64, 0, 0:128], qb[:, ch, h * 64:(h + 1) * 64], identb[:, :]),
                    wait=[t_q, t_idb, t_hlast, t_atm])
                tq2 = p.op("scalar", lambda s: s.copy(qT[:, :], ps_tr[0:64, 0, 0:128]),
                           wait=tq1)
                tk1 = p.op("tensor", lambda t, h=h, ch=ch: t.transpose(
                    ps_tr[0:64, 1, 0:128], kb[:, ch, h * 64:(h + 1) * 64], identb[:, :]),
                    wait=[t_k, tq2])
                tk2 = p.op("scalar", lambda s: s.copy(kT[:, :], ps_tr[0:64, 1, 0:128]),
                           wait=tk1)
                tsc = p.op("tensor", lambda t: t.matmul(
                    ps_w[:, 0, 0:128], kT[:, :], qT[:, :], start=True, stop=True),
                    wait=[tk2, t_atm])
                tms = p.op("vector", lambda v: v.tensor_tensor(
                    msc[:, :], ps_w[:, 0, 0:128], lowerf[:, :], ALU.mult),
                    wait=[tsc, t_low])
                for c2 in range(2):
                    p.op("vector", lambda v, h=h, c2=c2, ch=ch: v.tensor_scalar(
                        rd[:, c2, :], lowerf[:, :], lbt[:, ch, 2 * h + c2:2 * h + c2 + 1],
                        None, ALU.mult), wait=[t_lt])
                tdl = p.op("tensor", lambda t: t.matmul(
                    ps_w[:, 1, 0:256], supper16[:, :], rd[:, :, :].rearrange("p a c -> p (a c)"),
                    start=True, stop=True), wait=[p.tok("vector"), t_sup])
                te0 = p.op("scalar", lambda s: s.activation(
                    ee[:, 0, :], ps_w[:, 1, 0:128], AF.Exp), wait=tdl)
                te1 = p.op("scalar", lambda s: s.activation(
                    ee[:, 1, :], ps_w[:, 1, 128:256], AF.Exp))
                tes = p.op("vector", lambda v: v.tensor_tensor(
                    ee[:, 0, :], ee[:, 0, :], ee[:, 1, :], ALU.add), wait=te1)
                tds = p.op("vector", lambda v: v.tensor_tensor(
                    ds[:, :], ee[:, 0, :], msc[:, :], ALU.mult))
                p.op("vector", lambda v, h=h, ch=ch: v.tensor_scalar(
                    qs2[:, 0:64], qb[:, ch, h * 64:(h + 1) * 64], atm[:, 2 * h:2 * h + 1],
                    None, ALU.mult))
                tqs = p.op("vector", lambda v, h=h, ch=ch: v.tensor_scalar(
                    qs2[:, 64:128], qb[:, ch, h * 64:(h + 1) * 64],
                    atm[:, 2 * h + 1:2 * h + 2], None, ALU.mult))
                tq3 = p.op("tensor", lambda t: t.transpose(
                    ps_tr[:, 0, 0:128], qs2[:, :], identb[:, :]), wait=[tqs, tq2])
                tq4 = p.op("scalar", lambda s: s.copy(qsT[:, :], ps_tr[:, 0, 0:128]),
                           wait=tq3)
                ty1 = p.op("tensor", lambda t, h=h, ch=ch: t.matmul(
                    ps_y[:, 0:65], ds[:, :], vb[:, ch, h * 65:(h + 1) * 65],
                    start=True, stop=False), wait=[tds, t_v, tq4])
                ty2 = p.op("tensor", lambda t, h=h: t.matmul(
                    ps_y[:, 0:65], qsT[:, :], sgc[:, h, :], start=False, stop=True),
                    wait=[t_sg])
                tr1 = p.op("vector", lambda v: v.tensor_scalar(
                    rcol[:, 0:1], ps_y[:, 64:65], cst[:, 3:4], None, ALU.add),
                    wait=[ty2, t_cst])
                tr2 = p.op("vector", lambda v: v.reciprocal(rcol[:, 1:2], rcol[:, 0:1]))
                t_hlast = p.op("vector", lambda v, h=h: v.tensor_scalar(
                    ytm[:, h * 64:(h + 1) * 64], ps_y[:, 0:64], rcol[:, 1:2], None,
                    ALU.mult))
            if L2_TRUNC < 3:
                t_chend = [t_sg, p.tok("tensor"), p.tok("vector"), p.tok("scalar"), t_hlast]
                continue
            t_yT = None
            for dt in range(NDT):
                tb = dt % 2
                ty3 = p.op("tensor", lambda t, dt=dt, tb=tb: t.transpose(
                    ps_tr[:, tb, 0:128], ytm[:, dt * 128:(dt + 1) * 128], identb[:, :]),
                    wait=[t_hlast] + ([t_yT] if dt >= 2 else []))
                t_yT = p.op("scalar", lambda s, dt=dt, tb=tb: s.copy(
                    yT[:, dt, :], ps_tr[:, tb, 0:128]), wait=ty3)
            if L2_TRUNC < 4:
                t_chend = [t_sg, t_yT, p.tok("tensor"), p.tok("vector"), p.tok("scalar")]
                continue
            t_h0 = None
            h0_hist = []
            for do in range(NDT):
                wb2 = do % 2
                tokm = None
                for di in range(NDT):
                    tokm = p.op("tensor", lambda t, do=do, di=di, wb2=wb2: t.matmul(
                        ps_w[:, wb2, 0:128], wo16[:, di, do * 128:(do + 1) * 128],
                        yT[:, di, :], start=(di == 0), stop=(di == NDT - 1)),
                        wait=([t_yT, t_wo] + ([h0_hist[-2]] if len(h0_hist) >= 2 else []))
                        if di == 0 else [])
                t_h0 = p.op("vector", lambda v, do=do, wb2=wb2, c=col0: v.scalar_tensor_tensor(
                    h0[:, do, :], ps_w[:, wb2, 0:128], bo8[:, do:do + 1],
                    xnT[:, do, c:c + 128], ALU.add, ALU.add), wait=[tokm, t_bo])
                h0_hist.append(t_h0)
            if L2_TRUNC < 5:
                t_chend = [t_sg, t_h0, p.tok("tensor"), p.tok("vector"), p.tok("scalar")]
                continue
            tsq = p.op("scalar", lambda s: s.activation(
                x1T[:, :, :].rearrange("p a c -> p (a c)"),
                h0[:, :, :].rearrange("p a c -> p (a c)"), AF.Square), wait=[t_h0, t_xnsp])
            tm1 = None
            for dt in range(NDT):
                tm1 = p.op("tensor", lambda t, dt=dt: t.matmul(
                    ps_ln[0:1, 0, 0:128], onesc[:, :], h0[:, dt, :],
                    start=(dt == 0), stop=(dt == NDT - 1)),
                    wait=[t_h0, t_onc] if dt == 0 else [])
            tm2 = None
            for dt in range(NDT):
                tm2 = p.op("tensor", lambda t, dt=dt: t.matmul(
                    ps_ln[0:1, 1, 0:128], onescb[:, :], x1T[:, dt, :],
                    start=(dt == 0), stop=(dt == NDT - 1)),
                    wait=[tsq, t_oncb] if dt == 0 else [])
            tl1 = p.op("vector", lambda v: v.tensor_scalar(
                lnr[:, 0, :], ps_ln[0:1, 0, 0:128], 1.0 / D, None, ALU.mult), wait=tm1)
            tl2 = p.op("vector", lambda v: v.tensor_scalar(
                lnr[:, 1, :], ps_ln[0:1, 1, 0:128], 1.0 / D, None, ALU.mult), wait=tm2)
            tl3 = p.op("vector", lambda v: v.tensor_tensor(
                lnr[:, 2, :], lnr[:, 0, :], lnr[:, 0, :], ALU.mult))
            tl4 = p.op("vector", lambda v: v.tensor_tensor(
                lnr[:, 2, :], lnr[:, 1, :], lnr[:, 2, :], ALU.subtract))
            tl5 = p.op("scalar", lambda s: s.activation(
                lnr[:, 3, :], lnr[:, 2, :], AF.Ln, bias=cst[0:1, 0:1]), wait=tl4)
            tl6 = p.op("scalar", lambda s: s.activation(
                lnr[:, 4, :], lnr[:, 3, :], AF.Exp, scale=cst[0:1, 1:2]))
            if L2_TRUNC < 6:
                t_chend = [t_sg, t_h0, p.tok("tensor"), p.tok("vector"), p.tok("scalar")]
                continue
            tb1 = p.op("tensor", lambda t: t.matmul(
                ps_ln[:, 0, 0:128], onesf[0:1, :], lnr[0:1, 0, :], start=True, stop=True),
                wait=[tl6])
            tb2 = p.op("tensor", lambda t: t.matmul(
                ps_ln[:, 1, 0:128], onesf[0:1, :], lnr[0:1, 4, :], start=True, stop=True))
            t_s = None
            for dt in range(NDT):
                p.op("vector", lambda v, dt=dt: v.tensor_tensor(
                    h0[:, dt, :], h0[:, dt, :], ps_ln[:, 0, 0:128], ALU.subtract),
                    wait=[tb1, tb2] if dt == 0 else [])
                p.op("vector", lambda v, dt=dt: v.tensor_tensor(
                    h0[:, dt, :], h0[:, dt, :], ps_ln[:, 1, 0:128], ALU.mult))
                t_s = p.op("vector", lambda v, dt=dt: v.tensor_scalar(
                    s16[:, dt, :], h0[:, dt, :], gam8[:, dt:dt + 1], lnb8[:, dt:dt + 1],
                    ALU.mult, ALU.add), wait=[t_gm, t_lb] if dt == 0 else [])
            t_stm = None
            for dt in range(NDT):
                tb = dt % 2
                ts1 = p.op("tensor", lambda t, dt=dt, tb=tb: t.transpose(
                    ps_tr[:, tb, 0:128], s16[:, dt, :], identb[:, :]),
                    wait=[t_s] + ([t_stm] if dt >= 2 else []))
                t_stm = p.op("scalar", lambda s, dt=dt, tb=tb: s.copy(
                    stm[:, dt * 128:(dt + 1) * 128], ps_tr[:, tb, 0:128]), wait=ts1)
            if L2_TRUNC < 7:
                t_chend = [t_stm, p.tok("tensor"), p.tok("vector"), p.tok("scalar")]
                continue
            t_ep = None
            for half in range(2):
                wb2 = half % 2
                tokm = None
                for dt in range(NDT):
                    tokm = p.op("tensor", lambda t, half=half, dt=dt, wb2=wb2, c=col0: t.matmul(
                        ps_w[:, wb2, :], xnT[:, dt, c:c + 128],
                        wexp16[:, dt, half * 512:(half + 1) * 512],
                        start=(dt == 0), stop=(dt == NDT - 1)),
                        wait=[t_wx, t_stm] if dt == 0 else [])
                t_ep = p.op("vector", lambda v, half=half, wb2=wb2: v.tensor_tensor(
                    ge1[:, half * 512:(half + 1) * 512], ps_w[:, wb2, :],
                    bexbc[:, half * 512:(half + 1) * 512], ALU.add), wait=[tokm, t_bxc])
            t_ep2 = p.op("vector", lambda v, tt=tt: v.tensor_scalar(
                ge1[:, :], ge1[:, :], kgate[:, tt:tt + 1], None, ALU.mult), wait=t_kg)
            t_gt = p.op("scalar", lambda s: s.activation(gtm[:, :], ge1[:, :], AF.Silu),
                        wait=t_ep2)
            t_x = p.dma("sync", xtl[:, :], io["xs"][b, ch * 128:(ch + 1) * 128, :],
                        wait=t_chend, ch="cx")
            t_m = p.op("vector", lambda v: v.tensor_tensor(
                stm[:, :], stm[:, :], gtm[:, :], ALU.mult), wait=t_gt)
            t_x1 = p.op("vector", lambda v: v.tensor_tensor(
                x1t[:, :], xtl[:, :], stm[:, :], ALU.add), wait=t_x)
            t_x1sp = p.dma("gpsimd", io["x1"][col0:col0 + 128, :], x1t[:, :],
                           wait=t_x1, ch="cx1")
            p.op("vector", lambda v: v.bn_stats(xst[:, 0, 4:10], x1t[:, 0:512]))
            p.op("vector", lambda v: v.bn_stats(xst[:, 0, 10:16], x1t[:, 512:1024]))
            tbn = p.op("vector", lambda v: v.bn_aggr(xst[:, 0, 0:2], xst[:, 0, 4:16]))
            p.op("scalar", lambda s: s.activation(
                xst[:, 0, 2:3], xst[:, 0, 1:2], AF.Ln, bias=cst[:, 0:1]), wait=tbn)
            tx4 = p.op("scalar", lambda s: s.activation(
                xst[:, 0, 3:4], xst[:, 0, 2:3], AF.Exp, scale=cst[:, 1:2]))
            tx5 = p.op("vector", lambda v: v.tensor_scalar(
                xs16[:, :], x1t[:, :], xst[:, 0, 0:1], xst[:, 0, 3:4],
                ALU.subtract, ALU.mult), wait=[tx4, t_x1sp])
            t_xT = None
            for dt in range(NDT):
                tb = dt % 2
                tx6 = p.op("tensor", lambda t, dt=dt, tb=tb: t.transpose(
                    ps_tr[:, tb, 0:128], xs16[:, dt * 128:(dt + 1) * 128], identb[:, :]),
                    wait=[tx5] + ([t_xT] if dt >= 2 else []))
                t_xT = p.op("scalar", lambda s, dt=dt, tb=tb: s.copy(
                    x1T[:, dt, :], ps_tr[:, tb, 0:128]), wait=tx6)
            t_xnsp = p.dma("gpsimd", io["xn1T"][:, :, col0:col0 + 128],
                           x1T[:, :, :], wait=t_xT, ch="cxn1")
            t_chend = [t_xnsp, t_x1sp, p.tok("tensor"), p.tok("vector"), p.tok("scalar")]


def build_l2():
    nc = bass.Bass("TRN2", target_bir_lowering=False, detect_race_conditions=RACE_DETECT)
    io = decl(nc, ["xs", "wo", "bo", "gam", "lnb", "wexp", "bexp",
                   "ident", "supper", "lower", "sel", "hsel", "csel", "wsel",
                   "xnT", "qtm", "ktm", "vtm", "lbtm", "cl", "sloc", "cntq", "gat",
                   "summall", "x1", "xn1T"],
              ext_in=["xnT", "qtm", "ktm", "vtm", "lbtm", "cl", "sloc", "cntq",
                      "gat", "summall"],
              ext_out=["x1", "xn1T"])
    with ExitStack() as stack:
        p = Prog(nc, stack)
        blk = stack.enter_context(nc.Block())
        emit_l2(nc, p, io)
        p.final_waits()
        p.emit(blk)
    return nc




def emit_l3(nc, p, io):
    st = p.stack
    sb = lambda name, shape, dt: st.enter_context(nc.sbuf_tensor(p.tag + name, shape, dt))
    psm = lambda name, shape, dt=F32: st.enter_context(nc.psum_tensor(p.tag + name, shape, dt))

    identb = sb("identb", [128, 128], BF16)
    pairK2 = sb("s_pairK2", [128, 512], BF16)
    caus2 = sb("s_caus2", [32, 8], BF16)
    denbc2 = sb("s_denbc2", [8, 32], F32)
    pairsel2 = sb("s_pairsel2", [32, 2048], BF16)
    bout8 = sb("bout8", [128, NDT], F32)
    cst = sb("cst", [128, 1], F32)
    outT = sb("outT", [128, NDT, B, SL], BF16)

    ps_tr = psm("ps_tr", [128, 2, 1024], BF16)
    ps_mm = psm("ps_mm", [128, 2, 512], F32)
    ps_sc = psm("ps_sc", [32, 2, 512], F32)
    ps_bc = psm("ps_bc", [128, 2, 512], F32)

    t_idb = p.dma("gpsimd", identb[:, :], io["ident"], ch="cst")
    t_pk = p.dma("gpsimd", pairK2[:, :], io["pairK2"], ch="cst")
    t_c2 = p.dma("gpsimd", caus2[:, :], io["caus2"], ch="cst")
    t_db = p.dma("sync", denbc2[:, :], io["denbc2"], ch="cst2")
    t_p2 = p.dma("gpsimd", pairsel2[:, :], io["pairsel2"], ch="cst")
    t_bo = p.dma("sync", bout8[:, :], io["bout"].rearrange("(dt p) -> p dt", p=128),
                 allow_slow_non_contiguous=True, ch="cst2")
    t_cq = p.op("vector", lambda v: v.memset(cst[:, 0:1], 0.125))
    if int(_os.environ.get('L3_HG', '8')) < 8:
        p.op("vector", lambda v: v.memset(outT[:, :, :, :], 0.0))

    with ExitStack() as ph:
        s2 = lambda name, shape, dt: ph.enter_context(nc.sbuf_tensor(p.tag + name, shape, dt))
        xn1T = s2("s_xn1T", [128, NDT, T], BF16)
        wqn = s2("wqn", [128, NDT, 128], BF16)
        wqT = s2("wqT", [128, 3, NDT, 128], BF16)
        qg = s2("qg", [128, B, SL], BF16)
        kg = s2("kg", [128, B, SL], BF16)
        vg = s2("vg", [128, B, SL], BF16)
        prod = s2("prod", [128, SL], BF16)
        Ebuf = s2("Ebuf", [32, SL], BF16)
        rden = s2("rden", [8, SL], F32)
        Enb = s2("Enb", [32, SL], BF16)
        wvt = s2("wvt", [128, SL], BF16)

        t_xn = None
        for dt in range(NDT):
            t_xn = p.dma("gpsimd", xn1T[:, dt, :], io["xn1T"][:, dt, :], ch="cxn")

        t_accw = None
        t_E = None
        t_wq = None
        for hg in range(int(_os.environ.get('L3_HG', '8'))):
            t_wn = None
            for sel3, e0 in ((0, hg * 128), (1, 1024 + hg * 128), (2, 2048 + hg * 128)):
                t_wn = p.dma("gpsimd", wqn[:, :, :],
                             io["wqkv"][e0:e0 + 128, :].rearrange("e (dt c) -> e dt c", c=128),
                             wait=[t_wq], ch="cw")
                for de in range(NDT):
                    tb = de % 2
                    tt1 = p.op("tensor", lambda t, de=de, tb=tb: t.transpose(
                        ps_tr[:, tb, 0:128], wqn[:, de, :], identb[:, :]),
                        wait=[t_wn, t_idb] + ([t_wq] if de >= 2 else []))
                    t_wq = p.op("scalar", lambda s, de=de, sel3=sel3, tb=tb: s.copy(
                        wqT[:, sel3, de, :], ps_tr[:, tb, 0:128]), wait=tt1)
            t_ev = None
            for sel3, dst in ((0, qg), (1, kg), (2, vg)):
                for b in range(B):
                    for half in range(2):
                        pb = (b * 2 + half) % 2
                        tokm = None
                        for dt in range(NDT):
                            tokm = p.op("tensor",
                                        lambda t, sel3=sel3, dt=dt, pb=pb, b=b, half=half: t.matmul(
                                            ps_mm[:, pb, :], wqT[:, sel3, dt, :],
                                            xn1T[:, dt, b * SL + half * 512:b * SL + (half + 1) * 512],
                                            start=(dt == 0), stop=(dt == NDT - 1)),
                                        wait=[t_wq, t_xn, t_ev, t_accw] if dt == 0 else [])
                        t_ev = p.op("vector",
                                    lambda v, dst=dst, b=b, half=half, pb=pb: v.tensor_copy(
                                        dst[:, b, half * 512:(half + 1) * 512], ps_mm[:, pb, :]),
                                    wait=tokm)
            t_scm = None
            for a1 in range(4):
                for a2 in range(4):
                    pp = a1 * 4 + a2
                    t_pr = p.op("vector", lambda v, a1=a1, a2=a2: v.tensor_tensor(
                        prod[:, :], qg[:, a1, :], kg[:, a2, :], ALU.mult),
                        wait=[t_ev] if pp == 0 else [t_scm])
                    for half in range(2):
                        t_scm = p.op("tensor", lambda t, pp=pp, half=half: t.matmul(
                            ps_sc[:, half, :], pairK2[:, pp * 32:(pp + 1) * 32],
                            prod[:, half * 512:(half + 1) * 512],
                            start=(pp == 0), stop=(pp == 15), skip_group_check=True),
                            wait=[t_pr, t_pk] + ([t_accw, t_E] if pp == 0 else []))
            t_E = p.op("scalar", lambda s: s.activation(
                Ebuf[:, :], ps_sc[:, :, :].rearrange("p a c -> p (a c)"), AF.Exp,
                scale=cst[0:32, 0:1]), wait=[t_scm, t_cq])
            t_en = None
            t_rc = None
            for half in range(2):
                t_dn = p.op("tensor", lambda t, half=half: t.matmul(
                    ps_bc[0:8, 0, 0:512], caus2[:, :], Ebuf[:, half * 512:(half + 1) * 512],
                    start=True, stop=True),
                    wait=[t_E, t_c2, t_rc] + ([t_accw] if half == 0 else []))
                t_rc = p.op("vector", lambda v, half=half: v.reciprocal(
                    rden[:, half * 512:(half + 1) * 512], ps_bc[0:8, 0, 0:512]), wait=t_dn)
                t_dx = p.op("tensor", lambda t, half=half: t.matmul(
                    ps_bc[0:32, 1, 0:512], denbc2[:, :], rden[:, half * 512:(half + 1) * 512],
                    start=True, stop=True), wait=[t_rc, t_db])
                t_en = p.op("vector", lambda v, half=half: v.tensor_tensor(
                    Enb[:, half * 512:(half + 1) * 512], Ebuf[:, half * 512:(half + 1) * 512],
                    ps_bc[0:32, 1, 0:512], ALU.mult), wait=t_dx)
            t_accw = None
            for a1 in range(4):
                for a2 in range(a1 + 1):
                    pp = a1 * 4 + a2
                    for half in range(2):
                        bs = (a2 * 2 + half) % 2
                        t_bm = p.op("tensor", lambda t, pp=pp, half=half, bs=bs: t.matmul(
                            ps_bc[:, bs, :], pairsel2[:, pp * 128:(pp + 1) * 128],
                            Enb[:, half * 512:(half + 1) * 512], start=True, stop=True),
                            wait=[t_en, t_p2, t_accw])
                        if a2 == 0:
                            t_accw = p.op("vector",
                                          lambda v, hg=hg, a1=a1, half=half, bs=bs: v.tensor_tensor(
                                              outT[:, hg, a1, half * 512:(half + 1) * 512],
                                              vg[:, 0, half * 512:(half + 1) * 512],
                                              ps_bc[:, bs, :], ALU.mult), wait=t_bm)
                        else:
                            p.op("vector", lambda v, a2=a2, half=half, bs=bs: v.tensor_tensor(
                                wvt[:, half * 512:(half + 1) * 512],
                                vg[:, a2, half * 512:(half + 1) * 512],
                                ps_bc[:, bs, :], ALU.mult), wait=t_bm)
                            t_accw = p.op("vector",
                                          lambda v, hg=hg, a1=a1, half=half: v.tensor_tensor(
                                              outT[:, hg, a1, half * 512:(half + 1) * 512],
                                              outT[:, hg, a1, half * 512:(half + 1) * 512],
                                              wvt[:, half * 512:(half + 1) * 512], ALU.add))
            if "dbg" in io:
                p.dma("gpsimd", io["dbg"][hg % 2, :, :],
                      outT[:, 0, :, :].rearrange("p a c -> p (a c)"), wait=t_accw, ch="cdbg")
        t_attn = [t_accw, p.tok("tensor"), p.tok("scalar")]

    with ExitStack() as ph:
        s2 = lambda name, shape, dt: ph.enter_context(nc.sbuf_tensor(p.tag + name, shape, dt))
        woT = s2("woT", [128, NDT, D], BF16)
        wst = s2("wst", [128, NDT, 128], BF16)
        s1b = s2("s1b", [128, NDT, SL], BF16)
        s1s = s2("s1s", [128, NDT, 128], BF16)

        t_wo = None
        t_wl = None
        for do in range(NDT):
            t_wl = p.dma("gpsimd", wst[:, :, :],
                         io["wout"][do * 128:(do + 1) * 128, :]
                         .rearrange("e (dt c) -> e dt c", c=128),
                         wait=[t_wl, t_wo] + (t_attn if do == 0 else []), ch="cw")
            for de in range(NDT):
                tb = de % 2
                tt1 = p.op("tensor", lambda t, de=de, tb=tb: t.transpose(
                    ps_tr[:, tb, 0:128], wst[:, de, :], identb[:, :]),
                    wait=[t_wl] + ([t_wo] if de >= 2 else []) + (t_attn if do == 0 and de < 2 else []))
                t_wo = p.op("scalar", lambda s, de=de, do=do, tb=tb: s.copy(
                    woT[:, de, do * 128:(do + 1) * 128], ps_tr[:, tb, 0:128]), wait=tt1)
        t_s1 = None
        t_sp = None
        t_cp = None
        for b in range(B):
            for do in range(NDT):
                for half in range(2):
                    pb = (do * 2 + half) % 2
                    tokm = None
                    for de in range(NDT):
                        tokm = p.op("tensor",
                                    lambda t, de=de, do=do, b=b, half=half, pb=pb: t.matmul(
                                        ps_mm[:, pb, :], woT[:, de, do * 128:(do + 1) * 128],
                                        outT[:, de, b, half * 512:(half + 1) * 512],
                                        start=(de == 0), stop=(de == NDT - 1)),
                                    wait=[t_wo, t_s1] if de == 0 else [])
                    t_s1 = p.op("vector", lambda v, do=do, half=half, pb=pb: v.tensor_scalar(
                        s1b[:, do, half * 512:(half + 1) * 512], ps_mm[:, pb, :],
                        bout8[:, do:do + 1], None, ALU.add), wait=[tokm, t_bo])
            for mt in range(NMT):
                for dt in range(NDT):
                    tb = dt % 2
                    tt1 = p.op("tensor", lambda t, dt=dt, mt=mt, tb=tb: t.transpose(
                        ps_tr[:, tb, 0:128], s1b[:, dt, mt * 128:(mt + 1) * 128], identb[:, :]),
                        wait=[t_s1] + ([t_cp] if dt >= 2 else []) + ([t_sp] if dt < 2 else []))
                    t_cp = p.op("scalar", lambda s, dt=dt, tb=tb: s.copy(
                        s1s[:, dt, :], ps_tr[:, tb, 0:128]), wait=tt1)
                t_sp = p.dma("gpsimd",
                             io["s1tm"][b * SL + mt * 128:b * SL + (mt + 1) * 128, :],
                             s1s[:, :, :].rearrange("p dt c -> p (dt c)"),
                             wait=t_cp, ch="cs1")


def build_l3(dbg=False):
    nc = bass.Bass("TRN2", target_bir_lowering=False, detect_race_conditions=RACE_DETECT)
    io = decl(nc, ["wqkv", "wout", "bout", "ident", "pairK2", "caus2", "denbc2",
                   "pairsel2", "xn1T", "s1tm"],
              ext_in=["xn1T"], ext_out=["s1tm"])
    if dbg:
        io["dbg"] = nc.dram_tensor("dbg", [2, 128, 4096], F32, kind="ExternalOutput").ap()
    with ExitStack() as stack:
        p = Prog(nc, stack)
        blk = stack.enter_context(nc.Block())
        emit_l3(nc, p, io)
        p.final_waits()
        p.emit(blk)
    return nc




def emit_l4(nc, p, io):
    st = p.stack
    sb = lambda name, shape, dt: st.enter_context(nc.sbuf_tensor(p.tag + name, shape, dt))
    psm = lambda name, shape, dt=F32: st.enter_context(nc.psum_tensor(p.tag + name, shape, dt))

    identb = sb("identb", [128, 128], BF16)
    identf = sb("identf", [128, 128], F32)
    eselc = sb("eselc", [8, 1024], BF16)
    maskE = sb("maskE4", [8, T], BF16)
    sumb2 = sb("sumb2", [128, NDT], F32)
    b1T = sb("b1T", [128, 32, 8], F32)
    mbc = sb("mbc", [128, 2, 512], BF16)
    g1s = sb("g1s", [128, NDT, 128], BF16)
    g1t = sb("g1t", [128, NDT, 128], BF16)
    s1l = sb("s1l", [128, 2, D], BF16)
    x1l = sb("x1l", [128, 2, D], F32)
    x2t = sb("x2t", [128, 2, D], F32)

    ps_h = psm("ps_h", [128, 2, 512], F32)
    ps_o = psm("ps_o", [128, 2, 512], F32)
    ps_tr = psm("ps_tr", [128, 2, 1024], BF16)
    ps_bc = psm("ps_bc", [128, 2, 512], F32)

    t_idb = p.dma("gpsimd", identb[:, :], io["ident"], ch="cst")
    t_idf = p.dma("sync", identf[:, :], io["ident"], ch="cst2")
    t_es = p.dma("gpsimd", eselc[:, :], io["esel"], ch="cst")

    b2r = sb("b2r", [128, NDT, NE], F32)

    with ExitStack() as ph:
        s2 = lambda name, shape, dt: ph.enter_context(nc.sbuf_tensor(p.tag + name, shape, dt))
        xn1T = s2("s_xn1T", [128, NDT, T], BF16)
        wr16 = s2("wr16", [128, NDT, 8], BF16)
        brc = s2("brc", [8, 1], F32)
        rE = s2("rE", [8, T], F32)
        rtm = s2("rtm", [128, 32, 8], F32)
        mask3 = s2("mask34", [128, 32, 8], F32)
        gw1 = s2("gw14", [128, 32], F32)
        gw2 = s2("gw24", [128, 32], F32)
        gw3 = s2("gw34", [128, 32], F32)

        t_xn = None
        for dt in range(NDT):
            t_xn = p.dma("gpsimd", xn1T[:, dt, :], io["xn1T"][:, dt, :], ch="cxn")
        t_wr = p.dma("gpsimd", wr16[:, :, :], io["Wr"].rearrange("(dt p) c -> p dt c", p=128),
                     ch="cst")
        b1n = s2("b1n", [8, FFH], F32)
        t_b1n = p.dma("gpsimd", b1n[:, :], io["b1"], ch="cst")
        b2n = s2("b2n", [8, D], F32)
        t_b2n = p.dma("gpsimd", b2n[:, :], io["b2"], ch="cst")
        t_b2c = None
        for c in range(NDT):
            tb = c % 2
            tt8 = p.op("tensor", lambda t, c=c, tb=tb: t.transpose(
                ps_h[:, tb, 0:8], b2n[:, c * 128:(c + 1) * 128], identf[0:8, 0:8]),
                wait=[t_b2n, t_idf] + ([t_b2c] if c >= 2 else []))
            t_b2c = p.op("scalar", lambda s, c=c, tb=tb: s.copy(
                b2r[:, c, :], ps_h[:, tb, 0:8]), wait=tt8)
        t_sb2 = p.op("vector", lambda v: v.tensor_reduce(
            sumb2[:, :], b2r[:, :, :], AXX, ALU.add), wait=t_b2c)
        t_b1 = None
        for c in range(32):
            tb = c % 2
            tt9 = p.op("tensor", lambda t, c=c, tb=tb: t.transpose(
                ps_o[:, tb, 0:8], b1n[:, c * 128:(c + 1) * 128], identf[0:8, 0:8]),
                wait=[t_b1n, t_idf] + ([t_b1] if c >= 2 else []))
            t_b1 = p.op("scalar", lambda s, c=c, tb=tb: s.copy(
                b1T[:, c, :], ps_o[:, tb, 0:8]), wait=tt9)
        t_br = p.dma("sync", brc[:, :], io["br"].rearrange("(e o) -> e o", o=1), ch="cst2")
        t_rE = None
        for i in range(8):
            pb = i % 2
            c0 = i * 512
            tokm = None
            for dt in range(NDT):
                tokm = p.op("tensor", lambda t, dt=dt, pb=pb, c0=c0: t.matmul(
                    ps_h[0:8, pb, :], wr16[:, dt, :], xn1T[:, dt, c0:c0 + 512],
                    start=(dt == 0), stop=(dt == NDT - 1)),
                    wait=([t_xn, t_wr] + ([t_rE] if i >= 2 else [])) if dt == 0 else [])
            t_rE = p.op("vector", lambda v, pb=pb, c0=c0: v.tensor_scalar(
                rE[:, c0:c0 + 512], ps_h[0:8, pb, :], brc[:, 0:1], None, ALU.add),
                wait=[tokm, t_br])
        t_ltm = None
        for c in range(32):
            tb = c % 2
            ttr = p.op("tensor", lambda t, c=c, tb=tb: t.transpose(
                ps_bc[:, tb, 0:8], rE[:, c * 128:(c + 1) * 128], identf[0:8, 0:8]),
                wait=[t_rE, t_idf] + ([t_ltm] if c >= 2 else []))
            t_ltm = p.op("scalar", lambda s, c=c, tb=tb: s.copy(
                rtm[:, c, :], ps_bc[:, tb, 0:8]), wait=ttr)
        p.op("vector", lambda v: v.tensor_copy(gw1[:, :], rtm[:, :, 0]), wait=t_ltm)
        for e2 in range(1, NE):
            p.op("vector", lambda v, e2=e2: v.tensor_max(gw1[:, :], gw1[:, :], rtm[:, :, e2]))
        p.op("vector", lambda v: v.memset(gw2[:, :], 0.0))
        t_msk = None
        for e2 in range(NE):
            p.op("vector", lambda v, e2=e2: v.tensor_tensor(
                gw3[:, :], rtm[:, :, e2], gw1[:, :], ALU.is_ge))
            p.op("vector", lambda v, e2=e2: v.tensor_tensor(
                mask3[:, :, e2], gw3[:, :], gw2[:, :], ALU.mult))
            p.op("vector", lambda v, e2=e2: v.tensor_tensor(
                mask3[:, :, e2], gw3[:, :], mask3[:, :, e2], ALU.subtract))
            t_msk = p.op("vector", lambda v, e2=e2: v.tensor_tensor(
                gw2[:, :], gw2[:, :], mask3[:, :, e2], ALU.add))
        t_mE = None
        for c in range(32):
            tb = c % 2
            ttr = p.op("tensor", lambda t, c=c, tb=tb: t.transpose(
                ps_bc[0:8, tb, 0:128], mask3[:, c, :], identf[:, :]),
                wait=[t_msk] + ([t_mE] if c >= 2 else []))
            t_mE = p.op("vector", lambda v, c=c, tb=tb: v.tensor_copy(
                maskE[:, c * 128:(c + 1) * 128], ps_bc[0:8, tb, 0:128]), wait=ttr)
        t_route = [t_mE, p.tok("tensor"), p.tok("scalar"), p.tok("vector")]

    xhal = sb("xhal", [128, NDT, 2048], BF16)
    xi = sb("xi", [128, NDT, 2048], BF16)
    w1q = sb("w1q", [128, 2, NDT, 1024], BF16)
    w2q = sb("w2q", [128, 2, NDT, 1024], BF16)
    outs = sb("outs", [128, NDT, 2048], BF16)
    Ht = sb("Ht", [128, 8, 512], BF16)
    t_old = t_route
    oacc_hist = []
    t_m1_prev = None
    t_g1 = None
    t_sp = None
    t_w1 = [None, None]
    t_w2 = [None, None]
    w1_user = [None, None]
    w2_user = [None, None]
    t_xh = None
    t_xiu = None
    t_ord = None
    for th in range(2):
        tok0 = th * 2048
        t_xh = p.dma("gpsimd", xhal[:, :, :], io["xn1T"][:, :, tok0:tok0 + 2048],
                     wait=[t_xiu] + (t_route if th == 0 else []), ch="cxh")
        t_out0 = p.op("vector", lambda v: v.memset(outs[:, :, :], 0.0),
                      wait=[t_g1] + (t_route if th == 0 else []))
        t_oacc = t_out0
        for e in range(NE):
            t_xiu = None
            for tc in range(4):
                bs = tc % 2
                t_bm = p.op("tensor", lambda t, e=e, tc=tc, bs=bs, tok0=tok0: t.matmul(
                    ps_bc[:, bs, :], eselc[:, e * 128:(e + 1) * 128],
                    maskE[:, tok0 + tc * 512:tok0 + (tc + 1) * 512],
                    start=True, stop=True),
                    wait=[t_mE, t_es, t_xiu if tc >= 2 else t_ord]
                    if True else []) 
                t_mb = p.op("scalar", lambda s, bs=bs: s.copy(
                    mbc[:, bs, :], ps_bc[:, bs, :]), wait=t_bm)
                for dt in range(NDT):
                    t_xiu = p.op("vector", lambda v, dt=dt, tc=tc, bs=bs: v.tensor_tensor(
                        xi[:, dt, tc * 512:(tc + 1) * 512],
                        xhal[:, dt, tc * 512:(tc + 1) * 512], mbc[:, bs, :], ALU.mult),
                        wait=[t_mb, t_xh])
            for q in range(4):
                wb = (e * 4 + q) % 2
                t_w1[wb] = p.dma("gpsimd", w1q[:, wb, :, :],
                                 io["W1"][e, :, :].rearrange("(dt p) h -> p dt h", p=128)
                                 [:, :, q * 1024:(q + 1) * 1024],
                                 wait=[t_w1[wb], w1_user[wb]], ch=f"cw1{wb}")
                t_w2[wb] = p.dma("gpsimd", w2q[:, wb, :, :],
                                 io["W2"][e, q * 1024:(q + 1) * 1024, :]
                                 .rearrange("(ht p) d -> p ht d", p=128),
                                 wait=[t_w2[wb], w2_user[wb]], ch=f"cw2{wb}")
                for tc in range(4):
                    t_relu = None
                    for ht in range(8):
                        hb = ht % 2
                        tokm = None
                        for dt in range(NDT):
                            tokm = p.op("tensor",
                                        lambda t, e=e, q=q, tc=tc, ht=ht, dt=dt, hb=hb, wb=wb: t.matmul(
                                            ps_h[:, hb, :], w1q[:, wb, dt, ht * 128:(ht + 1) * 128],
                                            xi[:, dt, tc * 512:(tc + 1) * 512],
                                            start=(dt == 0), stop=(dt == NDT - 1)),
                                        wait=[t_w1[wb], t_xiu, t_relu] if dt == 0 else [])
                        t_relu = p.op("scalar", lambda s, e=e, q=q, ht=ht, hb=hb: s.activation(
                            Ht[:, ht, :], ps_h[:, hb, :], AF.Relu,
                            bias=b1T[:, q * 8 + ht, e:e + 1]), wait=[tokm, t_b1])
                    for do in range(NDT):
                        ob = do % 2
                        tokm = None
                        for ht in range(8):
                            tokm = p.op("tensor",
                                        lambda t, do=do, ht=ht, ob=ob, wb=wb: t.matmul(
                                            ps_o[:, ob, :], w2q[:, wb, ht, do * 128:(do + 1) * 128],
                                            Ht[:, ht, :], start=(ht == 0), stop=(ht == 7)),
                                        wait=([t_w2[wb], t_relu] +
                                              ([oacc_hist[-2]] if len(oacc_hist) >= 2 else []))
                                        if ht == 0 else [])
                        t_oacc = p.op("vector", lambda v, do=do, tc=tc, ob=ob: v.tensor_tensor(
                            outs[:, do, tc * 512:(tc + 1) * 512],
                            outs[:, do, tc * 512:(tc + 1) * 512], ps_o[:, ob, :], ALU.add),
                            wait=tokm)
                        oacc_hist.append(t_oacc)
                        w2_user[wb] = tokm
                    w1_user[wb] = t_relu
            t_ord = t_oacc
        for mt in range(16):
            r0 = tok0 + mt * 128
            bq, sl0 = divmod(r0, SL)
            li = mt % 2
            t_x1 = p.dma("sync", x1l[:, li, :], io["x1"][r0:r0 + 128, :],
                         wait=t_sp, ch=f"cx{li}")
            t_s1 = p.dma("gpsimd", s1l[:, li, :], io["s1tm"][r0:r0 + 128, :],
                         wait=t_sp, ch=f"cs{li}")
            t_si = None
            for dt in range(NDT):
                t_si = p.op("scalar", lambda s, dt=dt, mt=mt: s.activation(
                    g1s[:, dt, :], outs[:, dt, mt * 128:(mt + 1) * 128], AF.Silu,
                    bias=sumb2[:, dt:dt + 1]), wait=[t_oacc, t_sb2] if dt == 0 else [])
            t_gt = None
            for dt in range(NDT):
                tb = dt % 2
                ttr = p.op("tensor", lambda t, dt=dt, tb=tb: t.transpose(
                    ps_tr[:, tb, 0:128], g1s[:, dt, :], identb[:, :]),
                    wait=[t_si, t_idb] + ([t_gt] if dt >= 2 else []))
                t_gt = p.op("scalar", lambda s, dt=dt, tb=tb: s.copy(
                    g1t[:, dt, :], ps_tr[:, tb, 0:128]),
                    wait=[ttr] + ([t_m1_prev] if dt == 0 else []))
            t_m1 = p.op("vector", lambda v, li=li: v.tensor_tensor(
                s1l[:, li, :], s1l[:, li, :], g1t[:, :, :].rearrange("p a c -> p (a c)"),
                ALU.mult), wait=[t_gt, t_s1])
            t_m1_prev = t_m1
            t_x2 = p.op("vector", lambda v, li=li: v.tensor_tensor(
                x2t[:, li, :], x1l[:, li, :], s1l[:, li, :], ALU.add), wait=t_x1)
            t_sp = p.dma("gpsimd", io["yout"][bq, sl0:sl0 + 128, :], x2t[:, li, :],
                         wait=t_x2, ch="cy")
        t_g1 = p.tok("scalar")
        t_xiu = p.tok("vector")


def build_l4():
    nc = bass.Bass("TRN2", target_bir_lowering=False, detect_race_conditions=RACE_DETECT)
    io = decl(nc, ["W1", "b1", "W2", "b2", "Wr", "br", "ident", "esel",
                   "xn1T", "x1", "s1tm", "yout"],
              ext_in=["xn1T", "x1", "s1tm"], ext_out=["yout"])
    with ExitStack() as stack:
        p = Prog(nc, stack)
        blk = stack.enter_context(nc.Block())
        emit_l4(nc, p, io)
        p.final_waits()
        p.emit(blk)
    return nc


_NC_CACHE = {}


def _get_nc(name, builder):
    if name not in _NC_CACHE:
        _NC_CACHE[name] = builder()
    return _NC_CACHE[name]




def emit_cc(nc, p, io):
    tok = p.op("gpsimd", lambda g: g.collective_compute(
        "AllGather", ALU.bypass, replica_groups=[list(range(NCORES))],
        ins=[io["summ"].opt()], outs=[io["summall"].opt()]), n=1)
    p.opn("gpsimd", lambda g: g.nop(), wait=tok)


ALL_NAMES = ["xs", "wq", "wk", "wv", "wbeta", "bbeta", "wo", "bo", "gam", "lnb",
             "wgate", "wexp", "bexp", "wqkv", "wout", "bout",
             "W1", "b1", "W2", "b2", "Wr", "br",
             "cos8", "sin8", "ident", "supper", "lower", "sel", "hsel", "csel",
             "wsel", "pairK2", "caus2", "denbc2", "pairsel2", "esel",
             "xnT", "qtm", "ktm", "vtm", "lbtm", "cl", "sloc", "cntq", "gat",
             "summ", "summall", "x1", "xn1T", "s1tm", "yout"]


def build_fused():
    nc = bass.Bass("TRN2", target_bir_lowering=False, detect_race_conditions=RACE_DETECT)
    io = decl(nc, ALL_NAMES, ext_out=["yout"])
    with ExitStack() as outer:
        ems = (emit_l1, emit_cc, emit_l2, emit_l3, emit_l4)
        if _os.environ.get("SKIP_CC"):
            ems = (emit_l1, emit_l2, emit_l3, emit_l4)
        for i, em in enumerate(ems):
            with ExitStack() as stg:
                p = Prog(nc, stg, sem_stack=outer, tag=f"s{i}_")
                blk = stg.enter_context(nc.Block())
                em(nc, p, io)
                p.final_waits()
                p.emit(blk)
    return nc


def kernel(**inputs):
    I = inputs
    cores = list(range(NCORES))
    hcs = [host_consts(c) for c in cores]
    nc = _get_nc("fused", build_fused)
    maps = []
    for c in cores:
        hc = hcs[c]
        m = {
            "xs": np.ascontiguousarray(I["L0_attn_x"][:, c * SL:(c + 1) * SL, :]),
            "wq": I["L0_attn_Wq"], "wk": I["L0_attn_Wk"], "wv": I["L0_attn_Wv"],
            "wbeta": I["L0_attn_Wbeta"],
            "bbeta": np.ascontiguousarray(np.asarray(I["L0_attn_bbeta"]).reshape(1, 32)),
            "wo": I["L0_attn_Wo"], "bo": I["L0_attn_bo"],
            "gam": I["L0_attn_ln_gamma"], "lnb": I["L0_attn_ln_beta"],
            "wgate": I["L0_ffn_w_gate"], "wexp": I["L0_ffn_w_expert"],
            "bexp": np.ascontiguousarray(np.asarray(I["L0_ffn_b_expert"]).reshape(1, D)),
            "wqkv": I["L1_attn_w_qkv"], "wout": I["L1_attn_w_out"],
            "bout": I["L1_attn_b_out"],
            "W1": I["L1_ffn_W1"], "b1": I["L1_ffn_b1"], "W2": I["L1_ffn_W2"],
            "b2": I["L1_ffn_b2"], "Wr": I["L1_ffn_Wr"], "br": I["L1_ffn_br"],
        }
        for nm in ("cos8", "sin8", "ident", "supper", "lower", "sel", "hsel",
                   "csel", "wsel", "pairK2", "caus2", "denbc2", "pairsel2", "esel"):
            m[nm] = hc[nm]
        maps.append(m)
    res = run_bass_kernel_spmd(nc, maps, core_ids=cores).results
    out = np.zeros((B, S, D), dtype=np.float32)
    for c in cores:
        out[:, c * SL:(c + 1) * SL, :] = np.asarray(res[c]["yout"])
    return out


def kernel_unfused(**inputs):
    I = inputs
    cores = list(range(NCORES))
    hcs = [host_consts(c) for c in cores]
    xsl = [np.ascontiguousarray(I["L0_attn_x"][:, c * SL:(c + 1) * SL, :]) for c in cores]

    nc1 = _get_nc("l1", build_l1)
    m1 = [{
        "xs": xsl[c], "wq": I["L0_attn_Wq"], "wk": I["L0_attn_Wk"], "wv": I["L0_attn_Wv"],
        "wbeta": I["L0_attn_Wbeta"], "bbeta": np.ascontiguousarray(
            np.asarray(I["L0_attn_bbeta"]).reshape(1, 32)),
        "wgate": I["L0_ffn_w_gate"],
        "cos8": hcs[c]["cos8"], "sin8": hcs[c]["sin8"], "ident": hcs[c]["ident"],
    } for c in cores]
    r1 = run_bass_kernel_spmd(nc1, m1, core_ids=cores).results
    summall = np.concatenate([np.asarray(r1[c]["summ"]) for c in cores], axis=0)

    nc2 = _get_nc("l2", build_l2)
    m2 = [{
        "xs": xsl[c], "wo": I["L0_attn_Wo"], "bo": I["L0_attn_bo"],
        "gam": I["L0_attn_ln_gamma"], "lnb": I["L0_attn_ln_beta"],
        "wexp": I["L0_ffn_w_expert"], "bexp": np.ascontiguousarray(
            np.asarray(I["L0_ffn_b_expert"]).reshape(1, D)),
        "ident": hcs[c]["ident"], "supper": hcs[c]["supper"], "lower": hcs[c]["lower"],
        "sel": hcs[c]["sel"], "hsel": hcs[c]["hsel"], "csel": hcs[c]["csel"],
        "wsel": hcs[c]["wsel"], "summall": summall,
        "xnT": np.asarray(r1[c]["xnT"]), "qtm": np.asarray(r1[c]["qtm"]),
        "ktm": np.asarray(r1[c]["ktm"]), "vtm": np.asarray(r1[c]["vtm"]),
        "lbtm": np.asarray(r1[c]["lbtm"]), "cl": np.asarray(r1[c]["cl"]),
        "sloc": np.asarray(r1[c]["sloc"]), "cntq": np.asarray(r1[c]["cntq"]),
        "gat": np.asarray(r1[c]["gat"]),
    } for c in cores]
    r2 = run_bass_kernel_spmd(nc2, m2, core_ids=cores).results

    nc3 = _get_nc("l3", build_l3)
    m3 = [{
        "wqkv": I["L1_attn_w_qkv"], "wout": I["L1_attn_w_out"], "bout": I["L1_attn_b_out"],
        "ident": hcs[c]["ident"], "pairK2": hcs[c]["pairK2"], "caus2": hcs[c]["caus2"],
        "denbc2": hcs[c]["denbc2"], "pairsel2": hcs[c]["pairsel2"],
        "xn1T": np.asarray(r2[c]["xn1T"]),
    } for c in cores]
    r3 = run_bass_kernel_spmd(nc3, m3, core_ids=cores).results

    nc4 = _get_nc("l4", build_l4)
    m4 = [{
        "W1": I["L1_ffn_W1"], "b1": I["L1_ffn_b1"], "W2": I["L1_ffn_W2"],
        "b2": I["L1_ffn_b2"], "Wr": I["L1_ffn_Wr"], "br": I["L1_ffn_br"],
        "ident": hcs[c]["ident"], "esel": hcs[c]["esel"],
        "xn1T": np.asarray(r2[c]["xn1T"]), "x1": np.asarray(r2[c]["x1"]),
        "s1tm": np.asarray(r3[c]["s1tm"]),
    } for c in cores]
    r4 = run_bass_kernel_spmd(nc4, m4, core_ids=cores).results

    out = np.zeros((B, S, D), dtype=np.float32)
    for c in cores:
        out[:, c * SL:(c + 1) * SL, :] = np.asarray(r4[c]["yout"])
    return out


# version 04
# speedup vs baseline: 1.1223x; 1.1223x over previous
import numpy as np
from contextlib import ExitStack

import concourse.bass as bass
import concourse.mybir as mybir
from concourse.bass_utils import run_bass_kernel_spmd

F32 = mybir.dt.float32
BF16 = mybir.dt.bfloat16
AF = mybir.ActivationFunctionType
ALU = mybir.AluOpType
AXX = mybir.AxisListType.X

NCORES = 8
B = 4
S = 8192
D = 1024
SL = S // NCORES
T = B * SL
NMT = SL // 128
NDT = D // 128
H = 16
DH = 64
NCH = 8
EPS_DN = 1e-6
LN_EPS = 1e-5
NE = 8
CAP = 100
FFH = 4096


class SemT:
    def __init__(self, stack, nc, name):
        self.h = stack.enter_context(nc.semaphore(name))
        self.v = 0


class Prog:
    ENGS = ("sync", "gpsimd", "tensor", "vector", "scalar")

    def __init__(self, nc, stack, sem_stack=None, tag=""):
        self.nc = nc
        self.stack = stack
        self.sem_stack = sem_stack or stack
        self.tag = tag
        self.items = {e: [] for e in self.ENGS}
        self.sems = {e: SemT(self.sem_stack, nc, f"{tag}c_{e}") for e in self.ENGS}

    def _norm(self, wait):
        if wait is None or wait == ():
            return []
        if not isinstance(wait, list):
            wait = [wait]
        out = []
        for w in wait:
            if w is None:
                continue
            out.append((w.h, w.v) if isinstance(w, SemT) else (w[0].h, w[1]))
        return out

    CHAINED = ("vector", "scalar")

    def op(self, eng, fn, wait=(), n=1):
        sem = self.sems[eng]
        wl = self._norm(wait)
        if eng in self.CHAINED and sem.v > 0:
            wl.append((sem.h, sem.v))
        sem.v += n
        tok = (sem, sem.v)

        def g(e, wl=wl, fn=fn, sem=sem, n=n):
            for h, v in wl:
                if v > 0:
                    e.wait_ge(h, v)
            fn(e).then_inc(sem.h, n)
        self.items[eng].append(g)
        return tok

    def opn(self, eng, fn, wait=()):
        if eng in self.CHAINED:
            return self.op(eng, fn, wait=wait)
        wl = self._norm(wait)

        def g(e, wl=wl, fn=fn):
            for h, v in wl:
                if v > 0:
                    e.wait_ge(h, v)
            fn(e)
        self.items[eng].append(g)

    def dma(self, eng, out, in_, wait=(), ch=None, **kw):
        key = "ch_" + (ch or eng)
        if key not in self.sems:
            self.sems[key] = SemT(self.sem_stack, self.nc, self.tag + key)
        sem = self.sems[key]
        wl = self._norm(wait) + ([(sem.h, sem.v)] if sem.v > 0 else [])
        sem.v += 16
        tok = (sem, sem.v)

        def g(e, wl=wl, out=out, in_=in_, sem=sem, kw=kw):
            for h, v in wl:
                if v > 0:
                    e.wait_ge(h, v)
            e.dma_start(out=out, in_=in_, **kw).then_inc(sem.h, 16)
        self.items[eng].append(g)
        return tok

    def tok(self, eng):
        s = self.sems[eng]
        return (s, s.v)

    def final_waits(self):
        for key, s in list(self.sems.items()):
            if key.startswith("ch_") and s.v > 0:
                self.opn("gpsimd", lambda g, s=s, v=s.v: g.wait_ge(s.h, v))

    def emit(self, blk):
        for name in self.ENGS:
            items = self.items[name]
            if not items:
                continue

            def body(e, items=items):
                for g in items:
                    g(e)
            getattr(blk, name)(body)


def host_consts(core):
    out = {}
    inv_freq = 1.0 / 10000.0 ** (np.arange(0, DH, 2, dtype=np.float64) / DH)
    t = np.arange(core * SL, (core + 1) * SL, dtype=np.float64)
    fr = t[:, None] * inv_freq[None, :]
    out["cos8"] = np.tile(np.cos(fr).astype(np.float32), (1, 8))
    out["sin8"] = np.tile(np.sin(fr).astype(np.float32), (1, 8))
    out["ident"] = np.eye(128, dtype=np.float32)
    r = np.arange(128)
    out["supper"] = (r[:, None] > r[None, :]).astype(np.float32)
    out["lower"] = (r[:, None] <= r[None, :]).astype(np.float32)
    sel = np.zeros((32, 4), dtype=np.float32)
    for j in range(NCORES):
        for bp in range(B):
            for b in range(B):
                if bp < b or (bp == b and j < core):
                    sel[j * 4 + bp, b] = 1.0
    out["sel"] = sel
    pairK2 = np.zeros((128, 512), dtype=np.float32)
    for a1 in range(4):
        for a2 in range(4):
            pp = a1 * 4 + a2
            for hh in range(2):
                ks = slice(hh * 64, (hh + 1) * 64)
                if a1 == a2:
                    for a3 in range(a1):
                        pairK2[ks, pp * 32 + hh * 16 + a1 * 4 + a3] = -1.0
                else:
                    pairK2[ks, pp * 32 + hh * 16 + pp] = 1.0
    out["pairK2"] = pairK2
    caus2 = np.zeros((32, 8), dtype=np.float32)
    for hh in range(2):
        for a in range(4):
            for ap_ in range(a + 1):
                caus2[hh * 16 + a * 4 + ap_, hh * 4 + a] = 1.0
    out["caus2"] = caus2
    denbc2 = np.zeros((8, 32), dtype=np.float32)
    for hh in range(2):
        for a in range(4):
            for ap_ in range(a + 1):
                denbc2[hh * 4 + a, hh * 16 + a * 4 + ap_] = 1.0
    out["denbc2"] = denbc2
    ps2 = np.zeros((32, 2048), dtype=np.float32)
    for a1 in range(4):
        for a2 in range(4):
            pp = a1 * 4 + a2
            for hh in range(2):
                ps2[hh * 16 + pp, pp * 128 + hh * 64:pp * 128 + (hh + 1) * 64] = 1.0
    out["pairsel2"] = ps2
    hsel = np.zeros((32, 16), dtype=np.float32)
    for hc in range(32):
        hsel[hc, hc >> 1] = 1.0
    out["hsel"] = hsel
    csel = np.zeros((32, 128), dtype=np.float32)
    for hc in range(32):
        for pp in range(128):
            if (hc & 1) == (pp >> 6):
                csel[hc, pp] = 1.0
    out["csel"] = csel
    out["wsel"] = np.array([[1.0 if j < core else 0.0 for j in range(8)]], np.float32)
    esel = np.zeros((8, 1024), dtype=np.float32)
    for e2 in range(8):
        esel[e2, e2 * 128:(e2 + 1) * 128] = 1.0
    out["esel"] = esel
    return out


IN_SHAPES = {
    "xs": [B, SL, D], "wq": [D, D], "wk": [D, D], "wv": [D, D],
    "wbeta": [D, 32], "bbeta": [1, 32], "wo": [D, D], "bo": [D],
    "gam": [D], "lnb": [D], "wgate": [D, NE], "wexp": [D, D], "bexp": [1, D],
    "wqkv": [3 * D, D], "wout": [D, D], "bout": [D],
    "W1": [NE, D, FFH], "b1": [NE, FFH], "W2": [NE, FFH, D], "b2": [NE, D],
    "Wr": [D, NE], "br": [NE],
    "cos8": [SL, 256], "sin8": [SL, 256], "ident": [128, 128],
    "supper": [128, 128], "lower": [128, 128], "sel": [32, 4],
    "pairK2": [128, 512], "caus2": [32, 8], "denbc2": [8, 32],
    "pairsel2": [32, 2048], "hsel": [32, 16], "csel": [32, 128], "wsel": [1, 8],
    "esel": [8, 1024],
}

INTER_SHAPES = {
    "xnT": ([128, NDT, T], BF16),
    "qtm": ([T, D], BF16),
    "ktm": ([T, D], BF16),
    "vtm": ([T, H * 65], BF16),
    "lbtm": ([T, 32], F32),
    "cl": ([B, 32, SL], F32),
    "sloc": ([B, NCH, 128, H * 65], BF16),
    "cntq": ([128, T], F32),
    "gat": ([128, 32], F32),
    "summ": ([128, 4256], F32),
    "summall": ([NCORES * 128, 4256], F32),
    "x1": ([T, D], F32),
    "xn1T": ([128, NDT, T], BF16),
    "s1tm": ([T, D], BF16),
    "g1tm": ([T, D], BF16),
    "yout": ([B, SL, D], F32),
}


def decl(nc, names, ext_in=(), ext_out=()):
    out = {}
    for n in names:
        if n in IN_SHAPES:
            out[n] = nc.dram_tensor(n, IN_SHAPES[n], F32, kind="ExternalInput").ap()
        else:
            shp, dt = INTER_SHAPES[n]
            k = "ExternalInput" if n in ext_in else (
                "ExternalOutput" if n in ext_out else "Internal")
            out[n] = nc.dram_tensor(n, shp, dt, kind=k).ap()
    return out


def emit_l1(nc, p, io):
    st = p.stack
    sb = lambda name, shape, dt: st.enter_context(nc.sbuf_tensor(p.tag + name, shape, dt))
    psm = lambda name, shape, dt=F32: st.enter_context(nc.psum_tensor(p.tag + name, shape, dt))

    xnT = sb("s_xnT", [128, NDT, T], BF16)
    wcat = sb("wcat", [128, NDT, 3104], BF16)
    cos8 = sb("cos8s", [128, NMT, 256], BF16)
    sin8 = sb("sin8s", [128, NMT, 256], BF16)
    identb = sb("identb", [128, 128], BF16)
    identf = sb("identf", [128, 128], F32)
    onesf = sb("onesf", [128, 128], F32)
    ones32 = sb("ones32", [32, SL], F32)
    bbc = sb("bbc", [128, 32], F32)
    cst = sb("cst", [128, 4], F32)

    ps_TF = psm("ps_TF", [128, 2, 512], F32)
    ps_A = psm("ps_A", [128, 2, 512], F32)

    t_idb = p.dma("gpsimd", identb[:, :], io["ident"], ch="cst")
    t_idf = p.dma("sync", identf[:, :], io["ident"], ch="cst2")
    t_cos = p.dma("gpsimd", cos8[:, :, :], io["cos8"].rearrange("(mt p) c -> p mt c", p=128), ch="cst")
    t_sin = p.dma("gpsimd", sin8[:, :, :], io["sin8"].rearrange("(mt p) c -> p mt c", p=128), ch="cst")
    for wname, c0 in (("wq", 0), ("wk", 1024), ("wv", 2048)):
        for dt in range(NDT):
            t_w = p.dma("gpsimd", wcat[:, dt, c0:c0 + 1024],
                        io[wname][dt * 128:(dt + 1) * 128, :], ch="cst")
    t_w = p.dma("gpsimd", wcat[:, :, 3072:3104],
                io["wbeta"].rearrange("(dt p) c -> p dt c", p=128), ch="cst")
    t_ones = p.op("vector", lambda v: v.memset(onesf[:, :], 1.0))
    p.opn("vector", lambda v: v.memset(cst[:, 0:1], LN_EPS))
    p.opn("vector", lambda v: v.memset(cst[:, 1:2], -0.5))
    p.opn("vector", lambda v: v.memset(cst[:, 2:3], 0.125))
    t_cst = p.op("vector", lambda v: v.memset(cst[:, 3:4], EPS_DN))
    t_o32 = p.op("vector", lambda v: v.memset(ones32[:, :], 1.0))
    bbrow = sb("bbrow", [1, 32], F32)
    t_bb = p.dma("sync", bbrow[:, :], io["bbeta"], ch="cst2")

    with ExitStack() as ph:
        s2 = lambda name, shape, dt: ph.enter_context(nc.sbuf_tensor(p.tag + name, shape, dt))
        ps_T = ph.enter_context(nc.psum_tensor(p.tag + "ps_T", [128, 2, 1024], BF16))
        xt = s2("xt", [128, 2, D], F32)
        xn16 = s2("xn16", [128, 2, D], BF16)
        stat = s2("stat", [128, 2, 16], F32)
        t_xfree = [None, None]
        t_x16free = [None, None]
        tok_cp = None
        for it in range(B * NMT):
            b, mt = divmod(it, NMT)
            bi = it % 2
            col0 = b * SL + mt * 128
            t_x = p.dma("sync", xt[:, bi, :], io["xs"][b, mt * 128:(mt + 1) * 128, :],
                        wait=t_xfree[bi], ch=f"xl{bi}")
            p.opn("vector", lambda v, bi=bi: v.bn_stats(stat[:, bi, 4:10], xt[:, bi, 0:512]),
                  wait=t_x)
            p.opn("vector", lambda v, bi=bi: v.bn_stats(stat[:, bi, 10:16], xt[:, bi, 512:1024]))
            t2 = p.op("vector", lambda v, bi=bi: v.bn_aggr(stat[:, bi, 0:2], stat[:, bi, 4:16]))
            p.opn("scalar", lambda s, bi=bi: s.activation(
                stat[:, bi, 2:3], stat[:, bi, 1:2], AF.Ln, bias=cst[:, 0:1]),
                wait=[t2, t_cst])
            t4 = p.op("scalar", lambda s, bi=bi: s.activation(
                stat[:, bi, 3:4], stat[:, bi, 2:3], AF.Exp, scale=cst[:, 1:2]))
            t5 = p.op("vector", lambda v, bi=bi: v.tensor_scalar(
                xn16[:, bi, :], xt[:, bi, :], stat[:, bi, 0:1], stat[:, bi, 3:4],
                ALU.subtract, ALU.mult), wait=[t4, t_x16free[bi]])
            t_xfree[bi] = t5
            ttr = None
            for dt in range(NDT):
                tb = dt % 2
                ttr = p.op("tensor", lambda t, bi=bi, dt=dt, tb=tb: t.transpose(
                    ps_T[:, tb, 0:128], xn16[:, bi, dt * 128:(dt + 1) * 128], identb[:, :]),
                    wait=[t5, t_idb] + ([tok_cp] if (it > 0 or dt >= 2) else []))
                tok_cp = p.op("scalar", lambda s, dt=dt, tb=tb, c=col0: s.copy(
                    xnT[:, dt, c:c + 128], ps_T[:, tb, 0:128]), wait=ttr)
            t_x16free[bi] = ttr
    t_xnT = tok_cp
    t_s0_end = [p.tok("tensor"), p.tok("scalar"), p.tok("vector")]

    for dt in range(NDT):
        p.dma("gpsimd", io["xnT"][:, dt, :], xnT[:, dt, :], wait=t_xnT, ch="sxnt")

    with ExitStack() as ph:
        s2 = lambda name, shape, dt: ph.enter_context(nc.sbuf_tensor(p.tag + name, shape, dt))
        wg = s2("wg", [128, NDT, 8], BF16)
        lE = s2("lE", [8, T], F32)
        maskE = s2("maskE", [8, T], F32)
        cntE = s2("cntE", [8, T], F32)
        ltm = s2("ltm", [128, 32, 8], F32)
        mask3 = s2("mask3", [128, 32, 8], F32)
        gw1 = s2("gw1", [128, 32], F32)
        gw2 = s2("gw2", [128, 32], F32)
        gw3 = s2("gw3", [128, 32], F32)
        gat = s2("s_gat", [128, 32], F32)
        ltot = s2("ltot", [8, 4], F32)

        t_wg = p.dma("gpsimd", wg[:, :, :], io["wgate"].rearrange("(dt p) c -> p dt c", p=128),
                     wait=t_s0_end, ch="cst")
        t_lE = None
        for i in range(8):
            pb = i % 2
            c0 = i * 512
            tokm = None
            for dt in range(NDT):
                tokm = p.op("tensor", lambda t, dt=dt, pb=pb, c0=c0: t.matmul(
                    ps_A[0:8, pb, :], wg[:, dt, :], xnT[:, dt, c0:c0 + 512],
                    start=(dt == 0), stop=(dt == NDT - 1)),
                    wait=([t_xnT, t_wg] + ([t_lE] if i >= 2 else [])) if dt == 0 else [])
            t_lE = p.op("vector", lambda v, pb=pb, c0=c0: v.tensor_copy(
                lE[:, c0:c0 + 512], ps_A[0:8, pb, :]), wait=tokm)
        t_ltm = None
        for c in range(32):
            tb = c % 2
            ttr = p.op("tensor", lambda t, c=c, tb=tb: t.transpose(
                ps_TF[:, tb, 0:8], lE[:, c * 128:(c + 1) * 128], identf[0:8, 0:8]),
                wait=[t_lE, t_idf] + ([t_ltm] if c >= 2 else []))
            t_ltm = p.op("scalar", lambda s, c=c, tb=tb: s.copy(
                ltm[:, c, :], ps_TF[:, tb, 0:8]), wait=ttr)
        p.opn("vector", lambda v: v.tensor_copy(gw1[:, :], ltm[:, :, 0]), wait=t_ltm)
        for e2 in range(1, NE):
            p.opn("vector", lambda v, e2=e2: v.tensor_max(gw1[:, :], gw1[:, :], ltm[:, :, e2]))
        p.opn("vector", lambda v: v.memset(gw2[:, :], 0.0))
        for e2 in range(NE):
            p.opn("vector", lambda v, e2=e2: v.tensor_tensor(
                gw3[:, :], ltm[:, :, e2], gw1[:, :], ALU.is_ge))
            p.opn("vector", lambda v, e2=e2: v.tensor_tensor(
                mask3[:, :, e2], gw3[:, :], gw2[:, :], ALU.mult))
            p.opn("vector", lambda v, e2=e2: v.tensor_tensor(
                mask3[:, :, e2], gw3[:, :], mask3[:, :, e2], ALU.subtract))
            p.opn("vector", lambda v, e2=e2: v.tensor_tensor(
                gw2[:, :], gw2[:, :], mask3[:, :, e2], ALU.add))
        for e2 in range(NE):
            p.opn("vector", lambda v, e2=e2: v.tensor_tensor(
                ltm[:, :, e2], ltm[:, :, e2], gw1[:, :], ALU.subtract))
        t_sub = p.op("vector", lambda v: v.tensor_copy(gw3[:, :], gw1[:, :]))
        t_exp = p.op("scalar", lambda s: s.activation(
            ltm[:, :, :], ltm[:, :, :], AF.Exp), wait=t_sub)
        p.opn("vector", lambda v: v.tensor_reduce(gw3[:, :], ltm[:, :, :], AXX, ALU.add),
              wait=t_exp)
        t_gat = p.op("vector", lambda v: v.reciprocal(gat[:, :], gw3[:, :]))
        p.dma("gpsimd", io["gat"], gat[:, :], wait=t_gat, ch="sgat")
        t_mE = None
        for c in range(32):
            tb = c % 2
            ttr = p.op("tensor", lambda t, c=c, tb=tb: t.transpose(
                ps_TF[0:8, tb, 0:128], mask3[:, c, :], identf[:, :]),
                wait=[t_gat] + ([t_mE] if c >= 2 else []))
            t_mE = p.op("vector", lambda v, c=c, tb=tb: v.tensor_copy(
                maskE[:, c * 128:(c + 1) * 128], ps_TF[0:8, tb, 0:128]), wait=ttr)
        for b in range(B):
            p.opn("vector", lambda v, b=b: v.tensor_tensor_scan(
                cntE[:, b * SL:(b + 1) * SL], ones32[0:8, :],
                maskE[:, b * SL:(b + 1) * SL], 0.0, ALU.mult, ALU.add),
                wait=[t_mE, t_o32] if b == 0 else [])
            p.opn("vector", lambda v, b=b: v.tensor_reduce(
                ltot[:, b:b + 1], maskE[:, b * SL:(b + 1) * SL], AXX, ALU.add))
        t_cnt = p.op("vector", lambda v: v.tensor_copy(gw3[0:8, 0:4], ltot[:, :]))
        p.dma("gpsimd", io["cntq"][0:8, :], lE[:, :], wait=t_cnt, ch="sgat")
        p.dma("gpsimd", io["cntq"][32:40, :], maskE[:, :], wait=t_cnt, ch="sgat")
        t_g_sp = p.dma("gpsimd", io["cntq"][64:72, :], cntE[:, :], wait=t_cnt, ch="sgat")
        p.dma("gpsimd", io["summ"][0:8, 4224:4228], ltot[:, :], wait=t_cnt)
    t_gate_v = p.tok("vector")

    ps_S = psm("ps_S", [64, 2, 512], F32)
    ps_B = psm("ps_B", [128, 512], F32)
    qtm = sb("s_qtm", [128, NMT, D], BF16)
    ktm = sb("s_ktm", [128, NMT, D], BF16)
    vtm = sb("s_vtm", [128, NMT, H * 65], BF16)
    lbT = sb("s_lbT", [128, NMT, 32], F32)
    lbe = sb("lbe", [32, SL], F32)
    cl = sb("s_cl", [32, SL], F32)
    Sst = sb("Sst", [64, 2, 2, H, 65], F32)
    at32 = sb("at32", [32, 4, 128], F32)
    atm = sb("atm", [128, 32], F32)
    scr32 = sb("scr32", [32, 2, 32], F32)
    albc = sb("albc", [64, 32], F32)
    ptbc = sb("ptbc", [64, 2, 16], F32)
    kst = sb("kst", [128, 2, 64], BF16)
    rtmp = sb("rtmp", [128, 4, 256], BF16)
    btmp = sb("btmp", [128, 2, 32], F32)

    t_m1 = p.op("tensor", lambda t: t.matmul(
        ps_B[:, 0:32], onesf[0:1, :], bbrow[:, :], start=True, stop=True),
        wait=[t_bb, t_ones, t_gate_v])
    t_bbc = p.op("vector", lambda v: v.tensor_copy(bbc[:, :], ps_B[:, 0:32]), wait=t_m1)
    p.opn("vector", lambda v: v.memset(scr32[:, :, :], 0.0), wait=t_g_sp)
    t_vones = p.op("vector", lambda v: v.memset(
        vtm[:, :, :].rearrange("p mt (h q) -> p (mt h) q", q=65)[:, :, 64:65], 1.0),
        wait=t_g_sp)

    t_ech = [t_gate_v, t_gate_v]
    t_bprev = [t_g_sp]
    t_pb_prev = None
    for b in range(B):
        t_lbT = None
        for mt in range(NMT):
            col0 = b * SL + mt * 128
            for ech in range(7):
                i = mt * 7 + ech
                pb = i % 2
                ew = 512 if ech < 6 else 32
                e0 = ech * 512
                tokm = None
                for dt in range(NDT):
                    tokm = p.op("tensor", lambda t, dt=dt, pb=pb, e0=e0, ew=ew, c=col0: t.matmul(
                        ps_A[:, pb, 0:ew], xnT[:, dt, c:c + 128], wcat[:, dt, e0:e0 + ew],
                        start=(dt == 0), stop=(dt == NDT - 1)),
                        wait=([t_xnT, t_w, t_ech[pb]] if dt == 0 else []))
                wext = t_bprev if mt == 0 else []
                if ech < 2:
                    t_ech[pb] = p.op("vector", lambda v, pb=pb, mt=mt, e0=e0: v.tensor_copy(
                        qtm[:, mt, e0:e0 + 512], ps_A[:, pb, 0:512]), wait=[tokm] + wext)
                elif ech < 4:
                    t_ech[pb] = p.op("vector", lambda v, pb=pb, mt=mt, e0=e0: v.tensor_copy(
                        ktm[:, mt, e0 - 1024:e0 - 1024 + 512], ps_A[:, pb, 0:512]),
                        wait=[tokm] + wext)
                elif ech < 6:
                    h0 = (ech - 4) * 8
                    t_ech[pb] = p.op("vector", lambda v, pb=pb, mt=mt, h0=h0: v.tensor_copy(
                        vtm[:, mt, :].rearrange("p (h q) -> p h q", q=65)[:, h0:h0 + 8, 0:64],
                        ps_A[:, pb, 0:512].rearrange("p (h q) -> p h q", q=64)),
                        wait=[tokm, t_vones] + wext)
                else:
                    tb0 = p.op("vector", lambda v, pb=pb: v.tensor_tensor(
                        btmp[:, 0, :], ps_A[:, pb, 0:32], bbc[:, :], ALU.add),
                        wait=[tokm, t_bbc] + wext)
                    tb1 = p.op("scalar", lambda s: s.activation(
                        btmp[:, 1, :], btmp[:, 0, :], AF.Sigmoid), wait=tb0)
                    tb2 = p.op("vector", lambda v: v.tensor_scalar(
                        btmp[:, 0, :], btmp[:, 1, :], 0.999, 0.001, ALU.min, ALU.max),
                        wait=tb1)
                    t_lbT = p.op("scalar", lambda s, mt=mt: s.activation(
                        lbT[:, mt, :], btmp[:, 0, :], AF.Ln), wait=tb2)
                    t_ech[pb] = t_lbT
        t_rope = None
        first = True
        for mt in range(NMT):
            for src in (qtm, ktm):
                for half in range(2):
                    hs = half * 8
                    qv = src[:, mt, :].rearrange("p (h q) -> p h q", q=64)[:, hs:hs + 8, :]
                    q1 = qv[:, :, 0:32]
                    q2 = qv[:, :, 32:64]
                    cv = cos8[:, mt, :].rearrange("p (h q) -> p h q", q=32)
                    sv = sin8[:, mt, :].rearrange("p (h q) -> p h q", q=32)
                    r0 = rtmp[:, 0, :].rearrange("p (h q) -> p h q", q=32)
                    r1 = rtmp[:, 1, :].rearrange("p (h q) -> p h q", q=32)
                    r2 = rtmp[:, 2, :].rearrange("p (h q) -> p h q", q=32)
                    r3 = rtmp[:, 3, :].rearrange("p (h q) -> p h q", q=32)
                    w0 = [t_cos, t_sin, t_lbT] if first else []
                    first = False
                    p.opn("vector", lambda v, a=r0, x=q1, y=cv: v.tensor_mul(a, x, y), wait=w0)
                    p.opn("vector", lambda v, a=r1, x=q2, y=sv: v.tensor_mul(a, x, y))
                    p.opn("vector", lambda v, a=r2, x=q1, y=sv: v.tensor_mul(a, x, y))
                    p.opn("vector", lambda v, a=r3, x=q2, y=cv: v.tensor_mul(a, x, y))
                    p.opn("vector", lambda v, a=q1, x=r0, y=r1: v.tensor_sub(a, x, y))
                    t_rope = p.op("vector", lambda v, a=q2, x=r2, y=r3: v.tensor_add(a, x, y))
            for src in (qtm, ktm):
                uu = src[:, mt, :]
                ft = rtmp[:, :, :].rearrange("p a c -> p (a c)")
                p.opn("vector", lambda v, u=uu, ft=ft: v.tensor_scalar(
                    ft, u, 0.0, None, ALU.min))
                tmn = p.op("vector", lambda v, u=uu: v.tensor_scalar(u, u, 0.0, None, ALU.max))
                tex = p.op("scalar", lambda s, ft=ft: s.activation(ft, ft, AF.Exp), wait=tmn)
                t_rope = p.op("vector", lambda v, u=uu, ft=ft: v.tensor_tensor(
                    u, u, ft, ALU.add), wait=tex)
        t_qspill = p.dma("gpsimd",
                         io["qtm"][b * SL:(b + 1) * SL, :].rearrange("(mt p) e -> p mt e", p=128),
                         qtm[:, :, :], wait=t_rope, ch="sq")
        t_lbe = None
        for mt in range(NMT):
            tb = mt % 2
            ttr = p.op("tensor", lambda t, mt=mt, tb=tb: t.transpose(
                ps_TF[0:32, tb, 0:128], lbT[:, mt, :], identf[:, :]),
                wait=[t_lbT, t_idf] + ([t_lbe] if mt >= 2 else []) + (t_bprev if mt < 2 else []))
            t_lbe = p.op("vector", lambda v, mt=mt, tb=tb: v.tensor_copy(
                lbe[:, mt * 128:(mt + 1) * 128], ps_TF[0:32, tb, 0:128]), wait=ttr)
        t_cl = p.op("vector", lambda v: v.tensor_tensor_scan(
            cl[:, :], ones32[:, :], lbe[:, :], 0.0, ALU.mult, ALU.add), wait=t_lbe)
        p.dma("gpsimd", io["cl"][b, :, :], cl[:, :], wait=t_cl, ch="scl")
        p.dma("gpsimd",
              io["lbtm"][b * SL:(b + 1) * SL, :].rearrange("(mt p) e -> p mt e", p=128),
              lbT[:, :, :], wait=t_lbT, ch="scl")
        t_s0 = p.op("vector", lambda v: v.memset(Sst[:, 0, :, :, :], 0.0), wait=t_bprev)
        t_supd = t_s0
        t_al = t_pb_prev if b > 0 else t_bbc
        supd_hist = []
        t_kst_user = None
        t_atm_user = None
        for ch in range(NCH):
            pg = ch % 2
            c0 = ch * 128
            if ch == 0:
                t_cll = p.op("vector", lambda v, c0=c0: v.tensor_copy(
                    at32[:, 0, :], cl[:, c0:c0 + 128]), wait=t_cl)
            else:
                t_cll = p.op("vector", lambda v, c0=c0: v.tensor_scalar(
                    at32[:, 0, :], cl[:, c0:c0 + 128], cl[:, c0 - 1:c0], None,
                    ALU.subtract))
            t_ae = p.op("scalar", lambda s: s.activation(
                at32[:, 2, :], at32[:, 0, :], AF.Exp), wait=t_cll)
            t_tl = p.op("vector", lambda v: v.tensor_scalar(
                at32[:, 1, :], at32[:, 0, :], -1.0, at32[:, 0, 127:128], ALU.mult, ALU.add))
            t_at = p.op("scalar", lambda s: s.activation(
                at32[:, 3, :], at32[:, 1, :], AF.Exp), wait=t_tl)
            ttr = p.op("tensor", lambda t: t.transpose(
                ps_TF[:, 0, 0:32], at32[:, 3, :], identf[0:32, 0:32]),
                wait=[t_at, t_atm_user, t_lbe])
            t_atm = p.op("vector", lambda v: v.tensor_copy(atm[:, :], ps_TF[:, 0, 0:32]),
                         wait=ttr)
            t_c1 = p.op("vector", lambda v: v.tensor_copy(scr32[:, 0, 0:1], at32[:, 2, 127:128]),
                        wait=t_ae)
            t_c2 = p.op("vector", lambda v: v.transpose(scr32[:, 1, :], scr32[:, 0, :]),
                        wait=t_c1)
            t_c3 = p.op("tensor", lambda t: t.matmul(
                ps_B[0:64, 0:32], onesf[0:1, 0:64], scr32[0:1, 1, :], start=True, stop=True),
                wait=[t_c2, t_ones, t_al])
            t_al = p.op("vector", lambda v: v.tensor_copy(albc[:, :], ps_B[0:64, 0:32]),
                        wait=t_c3)
            t_sp = p.dma("gpsimd", io["sloc"][b, ch, 0:64, :],
                         Sst[:, pg, 0, :, :].rearrange("p h q -> p (h q)"), wait=t_supd,
                         ch="ssl")
            t_sp = p.dma("gpsimd", io["sloc"][b, ch, 64:128, :],
                         Sst[:, pg, 1, :, :].rearrange("p h q -> p (h q)"), wait=t_supd,
                         ch="ssl")
            for h in range(H):
                t_k0 = p.op("vector", lambda v, h=h, ch=ch: v.tensor_scalar(
                    kst[:, 0, :], ktm[:, ch, h * 64:(h + 1) * 64], atm[:, 2 * h:2 * h + 1],
                    None, ALU.mult), wait=[t_atm, t_kst_user])
                t_k1 = p.op("vector", lambda v, h=h, ch=ch: v.tensor_scalar(
                    kst[:, 1, :], ktm[:, ch, h * 64:(h + 1) * 64], atm[:, 2 * h + 1:2 * h + 2],
                    None, ALU.mult))
                for c2 in range(2):
                    pb2 = (h * 2 + c2) % 2
                    wmm = [t_k1, t_sp]
                    if len(supd_hist) >= 2:
                        wmm.append(supd_hist[-2])
                    tokm = p.op("tensor", lambda t, h=h, c2=c2, pb2=pb2, ch=ch: t.matmul(
                        ps_S[:, pb2, 0:65], kst[:, c2, :],
                        vtm[:, ch, h * 65:(h + 1) * 65], start=True, stop=True), wait=wmm)
                    t_kst_user = tokm
                    t_supd = p.op("vector", lambda v, h=h, c2=c2, pb2=pb2, pg=pg: v.scalar_tensor_tensor(
                        Sst[:, 1 - pg, c2, h, :], Sst[:, pg, c2, h, :],
                        albc[:, 2 * h + c2:2 * h + c2 + 1],
                        ps_S[:, pb2, 0:65], ALU.mult, ALU.add), wait=[tokm, t_al])
                    supd_hist.append(t_supd)
            t_atm_user = t_kst_user
        fin = NCH % 2
        t_pe = p.op("scalar", lambda s: s.activation(
            scr32[:, 0, 0:1], cl[:, SL - 1:SL], AF.Exp), wait=[t_cl, t_c2])
        t_pt = p.op("vector", lambda v: v.transpose(scr32[:, 1, :], scr32[:, 0, :]), wait=t_pe)
        t_pm0 = p.op("tensor", lambda t: t.matmul(
            ps_B[0:64, 0:16], onesf[0:1, 0:64],
            scr32[0:1, 1, :].rearrange("o (h c) -> o h c", c=2)[:, :, 0],
            start=True, stop=True), wait=[t_pt, t_ones, t_al])
        t_pb0 = p.op("vector", lambda v: v.tensor_copy(ptbc[:, 0, :], ps_B[0:64, 0:16]),
                     wait=t_pm0)
        t_pm1 = p.op("tensor", lambda t: t.matmul(
            ps_B[0:64, 0:16], onesf[0:1, 0:64],
            scr32[0:1, 1, :].rearrange("o (h c) -> o h c", c=2)[:, :, 1],
            start=True, stop=True), wait=t_pb0)
        t_pb1 = p.op("vector", lambda v: v.tensor_copy(ptbc[:, 1, :], ps_B[0:64, 0:16]),
                     wait=t_pm1)
        t_sd1 = p.dma("gpsimd", io["summ"][0:64, b * 1040:(b + 1) * 1040],
                      Sst[:, fin, 0, :, :].rearrange("p h q -> p (h q)"), wait=t_supd, ch="ssm")
        t_sd1b = p.dma("gpsimd", io["summ"][64:128, b * 1040:(b + 1) * 1040],
                       Sst[:, fin, 1, :, :].rearrange("p h q -> p (h q)"), wait=t_supd, ch="ssm")
        t_sd2 = p.dma("gpsimd", io["summ"][0:64, 4160 + b * 16:4160 + (b + 1) * 16],
                      ptbc[:, 0, :], wait=[t_pb0, t_pb1], ch="ssm")
        t_sd2b = p.dma("gpsimd", io["summ"][64:128, 4160 + b * 16:4160 + (b + 1) * 16],
                       ptbc[:, 1, :], wait=[t_pb0, t_pb1], ch="ssm")
        t_kspill = p.dma("gpsimd",
                         io["ktm"][b * SL:(b + 1) * SL, :].rearrange("(mt p) e -> p mt e", p=128),
                         ktm[:, :, :], wait=t_supd, ch="sq")
        t_vspill = p.dma("gpsimd",
                         io["vtm"][b * SL:(b + 1) * SL, :].rearrange("(mt p) e -> p mt e", p=128),
                         vtm[:, :, :], wait=t_supd, ch="sq")
        t_bprev = [t_qspill, t_kspill, t_vspill, t_sd1, t_sd1b, t_sd2, t_sd2b, t_kst_user]
        t_pb_prev = t_pb1


import os as _os
RACE_DETECT = bool(int(_os.environ.get("RACE_DETECT", "0")))
L2_TRUNC = int(_os.environ.get("L2_TRUNC", "9"))


def build_l1():
    nc = bass.Bass("TRN2", target_bir_lowering=False, detect_race_conditions=RACE_DETECT)
    io = decl(nc, ["xs", "wq", "wk", "wv", "wbeta", "bbeta", "wgate",
                   "cos8", "sin8", "ident",
                   "xnT", "qtm", "ktm", "vtm", "lbtm", "cl", "sloc",
                   "cntq", "gat", "summ"],
              ext_out=["xnT", "qtm", "ktm", "vtm", "lbtm", "cl", "sloc",
                       "cntq", "gat", "summ"])
    with ExitStack() as stack:
        p = Prog(nc, stack)
        blk = stack.enter_context(nc.Block())
        emit_l1(nc, p, io)
        p.final_waits()
        p.emit(blk)
    return nc




def emit_l2(nc, p, io):
    st = p.stack
    sb = lambda name, shape, dt: st.enter_context(nc.sbuf_tensor(p.tag + name, shape, dt))
    psm = lambda name, shape, dt=F32: st.enter_context(nc.psum_tensor(p.tag + name, shape, dt))

    xnT = sb("s_xnT", [128, NDT, T], BF16)
    identb = sb("identb", [128, 128], BF16)
    identf = sb("identf", [128, 128], F32)
    onesf = sb("onesf", [128, 128], F32)
    lowerf = sb("lowerf", [128, 128], F32)
    supper16 = sb("supper16", [128, 128], BF16)
    hselc = sb("hselc", [32, 16], F32)
    cselc = sb("cselc", [32, 128], F32)
    cst = sb("cst", [128, 4], F32)
    wo16 = sb("wo16", [128, NDT, D], BF16)
    wexp16 = sb("wexp16", [128, NDT, D], BF16)
    bo8 = sb("bo8", [128, NDT], F32)
    gam8 = sb("gam8", [128, NDT], F32)
    lnb8 = sb("lnb8", [128, NDT], F32)
    bexbc = sb("bexbc", [128, D], F32)
    gatl = sb("gatl", [128, 32], F32)
    kgate = sb("kgate", [128, 32], F32)
    sin16 = sb("sin16", [128, B, H, 65], BF16)
    onesc = sb("onesc", [128, 1], F32)
    onescb = sb("onescb", [128, 1], BF16)

    ps_tr = psm("ps_tr", [128, 2, 1024], BF16)
    ps_w = psm("ps_w", [128, 2, 512], F32)
    ps_y = psm("ps_y", [128, 512], F32)
    ps_ln = psm("ps_ln", [128, 2, 512], F32)

    t_idb = p.dma("gpsimd", identb[:, :], io["ident"], ch="cst")
    t_idf = p.dma("sync", identf[:, :], io["ident"], ch="cst2")
    t_low = p.dma("sync", lowerf[:, :], io["lower"], ch="cst2")
    t_sup = p.dma("gpsimd", supper16[:, :], io["supper"], ch="cst")
    t_hs = p.dma("sync", hselc[:, :], io["hsel"], ch="cst2")
    t_cs = p.dma("gpsimd", cselc[:, :], io["csel"], ch="cst")
    for dt in range(NDT):
        t_wo = p.dma("gpsimd", wo16[:, dt, :], io["wo"][dt * 128:(dt + 1) * 128, :], ch="cst")
        t_wx = p.dma("gpsimd", wexp16[:, dt, :], io["wexp"][dt * 128:(dt + 1) * 128, :], ch="cst")
    t_bo = p.dma("sync", bo8[:, :], io["bo"].rearrange("(dt p) -> p dt", p=128), allow_slow_non_contiguous=True, ch="cst2")
    t_gm = p.dma("sync", gam8[:, :], io["gam"].rearrange("(dt p) -> p dt", p=128), allow_slow_non_contiguous=True, ch="cst2")
    t_lb = p.dma("sync", lnb8[:, :], io["lnb"].rearrange("(dt p) -> p dt", p=128), allow_slow_non_contiguous=True, ch="cst2")
    t_ga = p.dma("sync", gatl[:, :], io["gat"], ch="cst2")
    for dt in range(NDT):
        t_xnl = p.dma("gpsimd", xnT[:, dt, :], io["xnT"][:, dt, :], ch="cxn")
    t_one = p.op("vector", lambda v: v.memset(onesf[:, :], 1.0))
    t_onc = p.op("vector", lambda v: v.memset(onesc[:, :], 1.0))
    t_oncb = p.op("vector", lambda v: v.memset(onescb[:, :], 1.0))
    p.op("vector", lambda v: v.memset(cst[:, 0:1], LN_EPS))
    p.op("vector", lambda v: v.memset(cst[:, 1:2], -0.5))
    p.op("vector", lambda v: v.memset(cst[:, 2:3], 0.125))
    t_cst = p.op("vector", lambda v: v.memset(cst[:, 3:4], EPS_DN))
    bexr = sb("bexr", [1, D], F32)
    t_bx = p.dma("sync", bexr[:, :], io["bexp"], ch="cst2")
    for half in range(2):
        t_bm = p.op("tensor", lambda t, half=half: t.matmul(
            ps_ln[:, half, :], onesf[0:1, :], bexr[0:1, half * 512:(half + 1) * 512],
            start=True, stop=True), wait=[t_bx, t_one])
        t_bxc = p.op("vector", lambda v, half=half: v.tensor_copy(
            bexbc[:, half * 512:(half + 1) * 512], ps_ln[:, half, :]), wait=t_bm)

    with ExitStack() as ph:
        s2 = lambda name, shape, dt: ph.enter_context(nc.sbuf_tensor(p.tag + name, shape, dt))
        prj = s2("prj", [128, 1, 4256], F32)
        acc = s2("acc", [128, B, H, 65], F32)
        wbc = s2("wbc", [128, 8], F32)
        w1bc = s2("w1bc", [128, 8], F32)
        ptt = s2("ptt", [128, 2, 16], F32)
        wrow = s2("wrow", [1, 8], F32)
        cnta = s2("cnta", [32, 8], F32)
        cnt8 = s2("cnt8", [8, 32], F32)
        selc = s2("selc", [32, 4], F32)
        offs = s2("offs", [8, 8], F32)
        keepE = s2("keepE", [8, T], F32)
        cntE = s2("cntE", [8, T], F32)
        maskE = s2("maskE", [8, T], F32)
        kw = s2("kw", [8, SL], F32)
        kt3 = s2("kt3", [128, 32, 8], F32)

        t_wr = p.dma("sync", wrow[:, :], io["wsel"], ch="cst2")
        t_wm = p.op("tensor", lambda t: t.matmul(
            ps_ln[:, 0, 0:8], onesf[0:1, :], wrow[:, :], start=True, stop=True),
            wait=[t_wr, t_one, t_bxc])
        t_wb = p.op("vector", lambda v: v.tensor_copy(wbc[:, :], ps_ln[:, 0, 0:8]), wait=t_wm)
        t_w1 = p.op("vector", lambda v: v.tensor_scalar(
            w1bc[:, :], wbc[:, :], -1.0, 1.0, ALU.mult, ALU.add))
        t_acc0 = p.op("vector", lambda v: v.memset(acc[:, :, :, :], 0.0))
        t_prfree = [None]
        for j in range(NCORES - 1):
            bj = 0
            t_pr = p.dma("gpsimd", prj[:, bj, :], io["summall"][j * 128:(j + 1) * 128, :],
                         wait=t_prfree[bj], ch="cpr")
            for b in range(B):
                t_p1 = p.op("vector", lambda v, bj=bj, b=b, j=j: v.tensor_scalar(
                    ptt[:, bj, :], prj[:, bj, 4160 + b * 16:4160 + (b + 1) * 16],
                    wbc[:, j:j + 1], w1bc[:, j:j + 1], ALU.mult, ALU.add),
                    wait=[t_pr, t_w1, t_acc0])
                t_p2 = p.op("vector", lambda v, bj=bj, b=b: v.tensor_tensor(
                    acc[:, b, :, :], acc[:, b, :, :],
                    ptt[:, bj, :].rearrange("p (h q) -> p h q", q=1).broadcast_to([128, 16, 65]),
                    ALU.mult))
                t_p3 = p.op("vector", lambda v, bj=bj, b=b, j=j: v.scalar_tensor_tensor(
                    acc[:, b, :, :],
                    prj[:, bj, b * 1040:(b + 1) * 1040].rearrange("p (h q) -> p h q", q=65),
                    wbc[:, j:j + 1], acc[:, b, :, :], ALU.mult, ALU.add))
            t_prfree[0] = t_p3
        t_s16 = p.op("scalar", lambda s: s.copy(sin16[:, :, :, :], acc[:, :, :, :]),
                     wait=t_p3 if NCORES > 1 else t_acc0)
        t_c8 = p.dma("gpsimd", cnt8[:, :].rearrange("p (j b) -> p j b", b=4),
                     io["summall"][:, 4224:4228].rearrange("(j p) b -> p j b", p=128)[0:8, :, :],
                     allow_slow_non_contiguous=True, ch="cpr")
        t_ct = p.op("tensor", lambda t: t.transpose(
            ps_w[0:32, 0, 0:8], cnt8[:, :], identf[0:8, 0:8]), wait=[t_c8, t_idf])
        t_ca = p.op("vector", lambda v: v.tensor_copy(cnta[:, :], ps_w[0:32, 0, 0:8]),
                    wait=t_ct)
        t_se = p.dma("sync", selc[:, :], io["sel"], ch="cst2")
        t_om = p.op("tensor", lambda t: t.matmul(
            ps_ln[0:8, 1, 0:4], cnta[:, :], selc[:, :], start=True, stop=True),
            wait=[t_ca, t_se, t_wb])
        t_of = p.op("vector", lambda v: v.tensor_scalar(
            offs[:, 0:4], ps_ln[0:8, 1, 0:4], -1.0, None, ALU.add), wait=t_om)
        t_ma = p.dma("sync", maskE[:, :], io["cntq"][32:40, :], ch="cst2")
        t_cn = p.dma("sync", cntE[:, :], io["cntq"][64:72, :], ch="cst2")
        t_ke = None
        for b in range(B):
            p.op("vector", lambda v, b=b: v.tensor_scalar(
                kw[:, :], cntE[:, b * SL:(b + 1) * SL], offs[:, b:b + 1], None, ALU.add),
                wait=[t_cn, t_of])
            p.op("vector", lambda v: v.tensor_single_scalar(kw[:, :], kw[:, :], 100.0, ALU.is_lt))
            t_ke = p.op("vector", lambda v, b=b: v.tensor_tensor(
                keepE[:, b * SL:(b + 1) * SL], kw[:, :], maskE[:, b * SL:(b + 1) * SL],
                ALU.mult), wait=t_ma)
        t_kc = None
        for c in range(32):
            tb = c % 2
            ttr = p.op("tensor", lambda t, c=c, tb=tb: t.transpose(
                ps_w[:, tb, 0:8], keepE[:, c * 128:(c + 1) * 128], identf[0:8, 0:8]),
                wait=[t_ke, t_idf] + ([t_kc] if c >= 2 else []))
            t_kc = p.op("scalar", lambda s, c=c, tb=tb: s.copy(
                kt3[:, c, :], ps_w[:, tb, 0:8]), wait=ttr)
        t_ks = p.op("vector", lambda v: v.tensor_reduce(kgate[:, :], kt3[:, :, :], AXX, ALU.add),
                    wait=t_kc)
        t_kg = p.op("vector", lambda v: v.tensor_tensor(
            kgate[:, :], kgate[:, :], gatl[:, :], ALU.mult), wait=t_ga)
    t_stageA = [p.tok("vector"), p.tok("scalar"), p.tok("tensor"), t_kg, t_s16]

    qb = sb("qb", [128, NMT, D], BF16)
    kb = sb("kb", [128, NMT, D], BF16)
    vb = sb("vb", [128, NMT, H * 65], BF16)
    lbt = sb("lbt", [128, NMT, 32], F32)
    clb = sb("clb", [32, SL], F32)
    slc = sb("slc", [128, 1, H, 65], BF16)
    sgc = sb("sgc", [128, H, 65], BF16)
    plb = sb("plb", [128, 2, 16], F32)
    hw = sb("hw", [32, 16], F32)
    at32 = sb("at32", [32, 3, 128], F32)
    atm = sb("atm", [128, 32], F32)
    qT = sb("qT", [64, 128], BF16)
    kT = sb("kT", [64, 128], BF16)
    qs2 = sb("qs2", [128, 128], BF16)
    qsT = sb("qsT", [128, 128], BF16)
    msc = sb("msc", [128, 128], F32)
    rd = sb("rd", [128, 2, 128], BF16)
    ee = sb("ee", [128, 2, 128], BF16)
    ds = sb("ds", [128, 128], BF16)
    rcol = sb("rcol", [128, 2], F32)
    ytm = sb("ytm", [128, D], BF16)
    yT = sb("yT", [128, NDT, 128], BF16)
    h0 = sb("h0", [128, NDT, 128], F32)
    lnr = sb("lnr", [1, 5, 128], F32)
    s16 = sb("s16", [128, NDT, 128], BF16)
    stm = sb("stm", [128, D], BF16)
    gtm = sb("gtm", [128, D], BF16)
    ge1 = sb("ge1", [128, D], BF16)
    xtl = sb("xtl", [128, D], F32)
    x1t = sb("x1t", [128, D], F32)
    xs16 = sb("xs16", [128, D], BF16)
    xst = sb("xst", [128, 1, 16], F32)
    x1T = sb("x1T", [128, NDT, 128], BF16)

    t_bload = [None]
    t_chend = [None]
    t_x1sp = None
    t_xnsp = None
    for b in range(B):
        w_b = t_stageA if b == 0 else t_chend
        t_q = p.dma("gpsimd", qb[:, :, :],
                    io["qtm"][b * SL:(b + 1) * SL, :].rearrange("(mt p) e -> p mt e", p=128),
                    wait=w_b, ch="cq")
        t_k = p.dma("gpsimd", kb[:, :, :],
                    io["ktm"][b * SL:(b + 1) * SL, :].rearrange("(mt p) e -> p mt e", p=128),
                    wait=w_b, ch="ck")
        t_v = p.dma("gpsimd", vb[:, :, :],
                    io["vtm"][b * SL:(b + 1) * SL, :].rearrange("(mt p) e -> p mt e", p=128),
                    wait=w_b, ch="cv")
        t_lt = p.dma("gpsimd", lbt[:, :, :],
                     io["lbtm"][b * SL:(b + 1) * SL, :].rearrange("(mt p) e -> p mt e", p=128),
                     wait=w_b, ch="cq")
        t_cl = p.dma("sync", clb[:, :], io["cl"][b, :, :], wait=w_b, ch="ccl")
        for ch in range(NCH if L2_TRUNC >= 1 else 0):
            tt = b * NMT + ch
            col0 = b * SL + ch * 128
            sli = 0
            t_sl = p.dma("gpsimd", slc[:, sli, :, :].rearrange("p h q -> p (h q)"),
                         io["sloc"][b, ch, :, :], wait=t_chend, ch="csl")
            if ch == 0:
                t_pl = p.op("vector", lambda v: v.memset(plb[:, 0, :], 1.0), wait=t_chend)
                pli = 0
            else:
                pli = ch % 2
                t_pe = p.op("scalar", lambda s, ch=ch: s.activation(
                    at32[:, 0, 0:1], clb[:, ch * 128 - 1:ch * 128], AF.Exp),
                    wait=[t_cl] + t_chend)
                t_hw = p.op("vector", lambda v: v.tensor_scalar(
                    hw[:, :], hselc[:, :], at32[:, 0, 0:1], None, ALU.mult),
                    wait=[t_pe, t_hs])
                t_plm = p.op("tensor", lambda t, pli=pli: t.matmul(
                    ps_ln[:, pli, 0:16], cselc[:, :], hw[:, :], start=True, stop=True),
                    wait=[t_hw, t_cs] + t_chend)
                t_pl = p.op("vector", lambda v, pli=pli: v.tensor_copy(
                    plb[:, pli, :], ps_ln[:, pli, 0:16]), wait=t_plm)
            t_sg1 = p.op("vector", lambda v, pli=pli, b=b: v.tensor_tensor(
                sgc[:, :, :], sin16[:, b, :, :],
                plb[:, pli, :].rearrange("p (h q) -> p h q", q=1).broadcast_to([128, 16, 65]),
                ALU.mult), wait=[t_s16, t_pl])
            t_sg = p.op("vector", lambda v, sli=sli: v.tensor_tensor(
                sgc[:, :, :], sgc[:, :, :], slc[:, sli, :, :], ALU.add), wait=t_sl)
            if ch == 0:
                t_cll = p.op("vector", lambda v: v.tensor_copy(at32[:, 1, :], clb[:, 0:128]),
                             wait=[t_cl] + t_chend)
            else:
                t_cll = p.op("vector", lambda v, ch=ch: v.tensor_scalar(
                    at32[:, 1, :], clb[:, ch * 128:(ch + 1) * 128],
                    clb[:, ch * 128 - 1:ch * 128], None, ALU.subtract), wait=[t_cl] + t_chend)
            t_ae = p.op("scalar", lambda s: s.activation(
                at32[:, 2, :], at32[:, 1, :], AF.Exp), wait=t_cll)
            ttr = p.op("tensor", lambda t: t.transpose(
                ps_w[:, 0, 0:32], at32[:, 2, :], identf[0:32, 0:32]),
                wait=[t_ae, t_idf] + t_chend)
            t_atm = p.op("vector", lambda v: v.tensor_copy(atm[:, :], ps_w[:, 0, 0:32]),
                         wait=ttr)
            t_hlast = None
            for h in range(H if L2_TRUNC >= 2 else 0):
                tq1 = p.op("tensor", lambda t, h=h, ch=ch: t.transpose(
                    ps_tr[0:64, 0, 0:128], qb[:, ch, h * 64:(h + 1) * 64], identb[:, :]),
                    wait=[t_q, t_idb, t_hlast, t_atm])
                tq2 = p.op("scalar", lambda s: s.copy(qT[:, :], ps_tr[0:64, 0, 0:128]),
                           wait=tq1)
                tk1 = p.op("tensor", lambda t, h=h, ch=ch: t.transpose(
                    ps_tr[0:64, 1, 0:128], kb[:, ch, h * 64:(h + 1) * 64], identb[:, :]),
                    wait=[t_k, tq2])
                tk2 = p.op("scalar", lambda s: s.copy(kT[:, :], ps_tr[0:64, 1, 0:128]),
                           wait=tk1)
                tsc = p.op("tensor", lambda t: t.matmul(
                    ps_w[:, 0, 0:128], kT[:, :], qT[:, :], start=True, stop=True),
                    wait=[tk2, t_atm])
                tms = p.op("vector", lambda v: v.tensor_tensor(
                    msc[:, :], ps_w[:, 0, 0:128], lowerf[:, :], ALU.mult),
                    wait=[tsc, t_low])
                for c2 in range(2):
                    p.op("vector", lambda v, h=h, c2=c2, ch=ch: v.tensor_scalar(
                        rd[:, c2, :], lowerf[:, :], lbt[:, ch, 2 * h + c2:2 * h + c2 + 1],
                        None, ALU.mult), wait=[t_lt])
                tdl = p.op("tensor", lambda t: t.matmul(
                    ps_w[:, 1, 0:256], supper16[:, :], rd[:, :, :].rearrange("p a c -> p (a c)"),
                    start=True, stop=True), wait=[p.tok("vector"), t_sup])
                te0 = p.op("scalar", lambda s: s.activation(
                    ee[:, 0, :], ps_w[:, 1, 0:128], AF.Exp), wait=tdl)
                te1 = p.op("scalar", lambda s: s.activation(
                    ee[:, 1, :], ps_w[:, 1, 128:256], AF.Exp))
                tes = p.op("vector", lambda v: v.tensor_tensor(
                    ee[:, 0, :], ee[:, 0, :], ee[:, 1, :], ALU.add), wait=te1)
                tds = p.op("vector", lambda v: v.tensor_tensor(
                    ds[:, :], ee[:, 0, :], msc[:, :], ALU.mult))
                p.op("vector", lambda v, h=h, ch=ch: v.tensor_scalar(
                    qs2[:, 0:64], qb[:, ch, h * 64:(h + 1) * 64], atm[:, 2 * h:2 * h + 1],
                    None, ALU.mult))
                tqs = p.op("vector", lambda v, h=h, ch=ch: v.tensor_scalar(
                    qs2[:, 64:128], qb[:, ch, h * 64:(h + 1) * 64],
                    atm[:, 2 * h + 1:2 * h + 2], None, ALU.mult))
                tq3 = p.op("tensor", lambda t: t.transpose(
                    ps_tr[:, 0, 0:128], qs2[:, :], identb[:, :]), wait=[tqs, tq2])
                tq4 = p.op("scalar", lambda s: s.copy(qsT[:, :], ps_tr[:, 0, 0:128]),
                           wait=tq3)
                ty1 = p.op("tensor", lambda t, h=h, ch=ch: t.matmul(
                    ps_y[:, 0:65], ds[:, :], vb[:, ch, h * 65:(h + 1) * 65],
                    start=True, stop=False), wait=[tds, t_v, tq4])
                ty2 = p.op("tensor", lambda t, h=h: t.matmul(
                    ps_y[:, 0:65], qsT[:, :], sgc[:, h, :], start=False, stop=True),
                    wait=[t_sg])
                tr1 = p.op("vector", lambda v: v.tensor_scalar(
                    rcol[:, 0:1], ps_y[:, 64:65], cst[:, 3:4], None, ALU.add),
                    wait=[ty2, t_cst])
                tr2 = p.op("vector", lambda v: v.reciprocal(rcol[:, 1:2], rcol[:, 0:1]))
                t_hlast = p.op("vector", lambda v, h=h: v.tensor_scalar(
                    ytm[:, h * 64:(h + 1) * 64], ps_y[:, 0:64], rcol[:, 1:2], None,
                    ALU.mult))
            if L2_TRUNC < 3:
                t_chend = [t_sg, p.tok("tensor"), p.tok("vector"), p.tok("scalar"), t_hlast]
                continue
            t_yT = None
            for dt in range(NDT):
                tb = dt % 2
                ty3 = p.op("tensor", lambda t, dt=dt, tb=tb: t.transpose(
                    ps_tr[:, tb, 0:128], ytm[:, dt * 128:(dt + 1) * 128], identb[:, :]),
                    wait=[t_hlast] + ([t_yT] if dt >= 2 else []))
                t_yT = p.op("scalar", lambda s, dt=dt, tb=tb: s.copy(
                    yT[:, dt, :], ps_tr[:, tb, 0:128]), wait=ty3)
            if L2_TRUNC < 4:
                t_chend = [t_sg, t_yT, p.tok("tensor"), p.tok("vector"), p.tok("scalar")]
                continue
            t_h0 = None
            h0_hist = []
            for do in range(NDT):
                wb2 = do % 2
                tokm = None
                for di in range(NDT):
                    tokm = p.op("tensor", lambda t, do=do, di=di, wb2=wb2: t.matmul(
                        ps_w[:, wb2, 0:128], wo16[:, di, do * 128:(do + 1) * 128],
                        yT[:, di, :], start=(di == 0), stop=(di == NDT - 1)),
                        wait=([t_yT, t_wo] + ([h0_hist[-2]] if len(h0_hist) >= 2 else []))
                        if di == 0 else [])
                t_h0 = p.op("vector", lambda v, do=do, wb2=wb2, c=col0: v.scalar_tensor_tensor(
                    h0[:, do, :], ps_w[:, wb2, 0:128], bo8[:, do:do + 1],
                    xnT[:, do, c:c + 128], ALU.add, ALU.add), wait=[tokm, t_bo])
                h0_hist.append(t_h0)
            if L2_TRUNC < 5:
                t_chend = [t_sg, t_h0, p.tok("tensor"), p.tok("vector"), p.tok("scalar")]
                continue
            tsq = p.op("scalar", lambda s: s.activation(
                x1T[:, :, :].rearrange("p a c -> p (a c)"),
                h0[:, :, :].rearrange("p a c -> p (a c)"), AF.Square), wait=[t_h0, t_xnsp])
            tm1 = None
            for dt in range(NDT):
                tm1 = p.op("tensor", lambda t, dt=dt: t.matmul(
                    ps_ln[0:1, 0, 0:128], onesc[:, :], h0[:, dt, :],
                    start=(dt == 0), stop=(dt == NDT - 1)),
                    wait=[t_h0, t_onc] if dt == 0 else [])
            tm2 = None
            for dt in range(NDT):
                tm2 = p.op("tensor", lambda t, dt=dt: t.matmul(
                    ps_ln[0:1, 1, 0:128], onescb[:, :], x1T[:, dt, :],
                    start=(dt == 0), stop=(dt == NDT - 1)),
                    wait=[tsq, t_oncb] if dt == 0 else [])
            tl1 = p.op("vector", lambda v: v.tensor_scalar(
                lnr[:, 0, :], ps_ln[0:1, 0, 0:128], 1.0 / D, None, ALU.mult), wait=tm1)
            tl2 = p.op("vector", lambda v: v.tensor_scalar(
                lnr[:, 1, :], ps_ln[0:1, 1, 0:128], 1.0 / D, None, ALU.mult), wait=tm2)
            tl3 = p.op("vector", lambda v: v.tensor_tensor(
                lnr[:, 2, :], lnr[:, 0, :], lnr[:, 0, :], ALU.mult))
            tl4 = p.op("vector", lambda v: v.tensor_tensor(
                lnr[:, 2, :], lnr[:, 1, :], lnr[:, 2, :], ALU.subtract))
            tl5 = p.op("scalar", lambda s: s.activation(
                lnr[:, 3, :], lnr[:, 2, :], AF.Ln, bias=cst[0:1, 0:1]), wait=tl4)
            tl6 = p.op("scalar", lambda s: s.activation(
                lnr[:, 4, :], lnr[:, 3, :], AF.Exp, scale=cst[0:1, 1:2]))
            if L2_TRUNC < 6:
                t_chend = [t_sg, t_h0, p.tok("tensor"), p.tok("vector"), p.tok("scalar")]
                continue
            tb1 = p.op("tensor", lambda t: t.matmul(
                ps_ln[:, 0, 0:128], onesf[0:1, :], lnr[0:1, 0, :], start=True, stop=True),
                wait=[tl6])
            tb2 = p.op("tensor", lambda t: t.matmul(
                ps_ln[:, 1, 0:128], onesf[0:1, :], lnr[0:1, 4, :], start=True, stop=True))
            t_s = None
            for dt in range(NDT):
                p.op("vector", lambda v, dt=dt: v.tensor_tensor(
                    h0[:, dt, :], h0[:, dt, :], ps_ln[:, 0, 0:128], ALU.subtract),
                    wait=[tb1, tb2] if dt == 0 else [])
                p.op("vector", lambda v, dt=dt: v.tensor_tensor(
                    h0[:, dt, :], h0[:, dt, :], ps_ln[:, 1, 0:128], ALU.mult))
                t_s = p.op("vector", lambda v, dt=dt: v.tensor_scalar(
                    s16[:, dt, :], h0[:, dt, :], gam8[:, dt:dt + 1], lnb8[:, dt:dt + 1],
                    ALU.mult, ALU.add), wait=[t_gm, t_lb] if dt == 0 else [])
            t_stm = None
            for dt in range(NDT):
                tb = dt % 2
                ts1 = p.op("tensor", lambda t, dt=dt, tb=tb: t.transpose(
                    ps_tr[:, tb, 0:128], s16[:, dt, :], identb[:, :]),
                    wait=[t_s] + ([t_stm] if dt >= 2 else []))
                t_stm = p.op("scalar", lambda s, dt=dt, tb=tb: s.copy(
                    stm[:, dt * 128:(dt + 1) * 128], ps_tr[:, tb, 0:128]), wait=ts1)
            if L2_TRUNC < 7:
                t_chend = [t_stm, p.tok("tensor"), p.tok("vector"), p.tok("scalar")]
                continue
            t_ep = None
            for half in range(2):
                wb2 = half % 2
                tokm = None
                for dt in range(NDT):
                    tokm = p.op("tensor", lambda t, half=half, dt=dt, wb2=wb2, c=col0: t.matmul(
                        ps_w[:, wb2, :], xnT[:, dt, c:c + 128],
                        wexp16[:, dt, half * 512:(half + 1) * 512],
                        start=(dt == 0), stop=(dt == NDT - 1)),
                        wait=[t_wx, t_stm] if dt == 0 else [])
                t_ep = p.op("vector", lambda v, half=half, wb2=wb2: v.tensor_tensor(
                    ge1[:, half * 512:(half + 1) * 512], ps_w[:, wb2, :],
                    bexbc[:, half * 512:(half + 1) * 512], ALU.add), wait=[tokm, t_bxc])
            t_ep2 = p.op("vector", lambda v, tt=tt: v.tensor_scalar(
                ge1[:, :], ge1[:, :], kgate[:, tt:tt + 1], None, ALU.mult), wait=t_kg)
            t_gt = p.op("scalar", lambda s: s.activation(gtm[:, :], ge1[:, :], AF.Silu),
                        wait=t_ep2)
            t_x = p.dma("sync", xtl[:, :], io["xs"][b, ch * 128:(ch + 1) * 128, :],
                        wait=t_chend, ch="cx")
            t_m = p.op("vector", lambda v: v.tensor_tensor(
                stm[:, :], stm[:, :], gtm[:, :], ALU.mult), wait=t_gt)
            t_x1 = p.op("vector", lambda v: v.tensor_tensor(
                x1t[:, :], xtl[:, :], stm[:, :], ALU.add), wait=t_x)
            t_x1sp = p.dma("gpsimd", io["x1"][col0:col0 + 128, :], x1t[:, :],
                           wait=t_x1, ch="cx1")
            p.op("vector", lambda v: v.bn_stats(xst[:, 0, 4:10], x1t[:, 0:512]))
            p.op("vector", lambda v: v.bn_stats(xst[:, 0, 10:16], x1t[:, 512:1024]))
            tbn = p.op("vector", lambda v: v.bn_aggr(xst[:, 0, 0:2], xst[:, 0, 4:16]))
            p.op("scalar", lambda s: s.activation(
                xst[:, 0, 2:3], xst[:, 0, 1:2], AF.Ln, bias=cst[:, 0:1]), wait=tbn)
            tx4 = p.op("scalar", lambda s: s.activation(
                xst[:, 0, 3:4], xst[:, 0, 2:3], AF.Exp, scale=cst[:, 1:2]))
            tx5 = p.op("vector", lambda v: v.tensor_scalar(
                xs16[:, :], x1t[:, :], xst[:, 0, 0:1], xst[:, 0, 3:4],
                ALU.subtract, ALU.mult), wait=[tx4, t_x1sp])
            t_xT = None
            for dt in range(NDT):
                tb = dt % 2
                tx6 = p.op("tensor", lambda t, dt=dt, tb=tb: t.transpose(
                    ps_tr[:, tb, 0:128], xs16[:, dt * 128:(dt + 1) * 128], identb[:, :]),
                    wait=[tx5] + ([t_xT] if dt >= 2 else []))
                t_xT = p.op("scalar", lambda s, dt=dt, tb=tb: s.copy(
                    x1T[:, dt, :], ps_tr[:, tb, 0:128]), wait=tx6)
            t_xnsp = p.dma("gpsimd", io["xn1T"][:, :, col0:col0 + 128],
                           x1T[:, :, :], wait=t_xT, ch="cxn1")
            t_chend = [t_xnsp, t_x1sp, p.tok("tensor"), p.tok("vector"), p.tok("scalar")]


def build_l2():
    nc = bass.Bass("TRN2", target_bir_lowering=False, detect_race_conditions=RACE_DETECT)
    io = decl(nc, ["xs", "wo", "bo", "gam", "lnb", "wexp", "bexp",
                   "ident", "supper", "lower", "sel", "hsel", "csel", "wsel",
                   "xnT", "qtm", "ktm", "vtm", "lbtm", "cl", "sloc", "cntq", "gat",
                   "summall", "x1", "xn1T"],
              ext_in=["xnT", "qtm", "ktm", "vtm", "lbtm", "cl", "sloc", "cntq",
                      "gat", "summall"],
              ext_out=["x1", "xn1T"])
    with ExitStack() as stack:
        p = Prog(nc, stack)
        blk = stack.enter_context(nc.Block())
        emit_l2(nc, p, io)
        p.final_waits()
        p.emit(blk)
    return nc




def emit_l3(nc, p, io):
    st = p.stack
    sb = lambda name, shape, dt: st.enter_context(nc.sbuf_tensor(p.tag + name, shape, dt))
    psm = lambda name, shape, dt=F32: st.enter_context(nc.psum_tensor(p.tag + name, shape, dt))

    identb = sb("identb", [128, 128], BF16)
    pairK2 = sb("s_pairK2", [128, 512], BF16)
    caus2 = sb("s_caus2", [32, 8], BF16)
    denbc2 = sb("s_denbc2", [8, 32], F32)
    pairsel2 = sb("s_pairsel2", [32, 2048], BF16)
    bout8 = sb("bout8", [128, NDT], F32)
    cst = sb("cst", [128, 1], F32)
    outT = sb("outT", [128, NDT, B, SL], BF16)

    ps_tr = psm("ps_tr", [128, 2, 1024], BF16)
    ps_mm = psm("ps_mm", [128, 2, 512], F32)
    ps_sc = psm("ps_sc", [32, 2, 512], F32)
    ps_bc = psm("ps_bc", [128, 2, 512], F32)

    t_idb = p.dma("gpsimd", identb[:, :], io["ident"], ch="cst")
    t_pk = p.dma("gpsimd", pairK2[:, :], io["pairK2"], ch="cst")
    t_c2 = p.dma("gpsimd", caus2[:, :], io["caus2"], ch="cst")
    t_db = p.dma("sync", denbc2[:, :], io["denbc2"], ch="cst2")
    t_p2 = p.dma("gpsimd", pairsel2[:, :], io["pairsel2"], ch="cst")
    t_bo = p.dma("sync", bout8[:, :], io["bout"].rearrange("(dt p) -> p dt", p=128),
                 allow_slow_non_contiguous=True, ch="cst2")
    t_cq = p.op("vector", lambda v: v.memset(cst[:, 0:1], 0.125))
    if int(_os.environ.get('L3_HG', '8')) < 8:
        p.op("vector", lambda v: v.memset(outT[:, :, :, :], 0.0))

    with ExitStack() as ph:
        s2 = lambda name, shape, dt: ph.enter_context(nc.sbuf_tensor(p.tag + name, shape, dt))
        xn1T = s2("s_xn1T", [128, NDT, T], BF16)
        wqn = s2("wqn", [128, NDT, 128], BF16)
        wqT = s2("wqT", [128, 3, NDT, 128], BF16)
        qg = s2("qg", [128, B, SL], BF16)
        kg = s2("kg", [128, B, SL], BF16)
        vg = s2("vg", [128, B, SL], BF16)
        prod = s2("prod", [128, SL], BF16)
        Ebuf = s2("Ebuf", [32, SL], BF16)
        rden = s2("rden", [8, SL], F32)
        Enb = s2("Enb", [32, SL], BF16)
        wvt = s2("wvt", [128, SL], BF16)

        t_xn = None
        for dt in range(NDT):
            t_xn = p.dma("gpsimd", xn1T[:, dt, :], io["xn1T"][:, dt, :], ch="cxn")

        t_accw = None
        t_E = None
        t_wq = None
        for hg in range(int(_os.environ.get('L3_HG', '8'))):
            t_wn = None
            for sel3, e0 in ((0, hg * 128), (1, 1024 + hg * 128), (2, 2048 + hg * 128)):
                t_wn = p.dma("gpsimd", wqn[:, :, :],
                             io["wqkv"][e0:e0 + 128, :].rearrange("e (dt c) -> e dt c", c=128),
                             wait=[t_wq], ch="cw")
                for de in range(NDT):
                    tb = de % 2
                    tt1 = p.op("tensor", lambda t, de=de, tb=tb: t.transpose(
                        ps_tr[:, tb, 0:128], wqn[:, de, :], identb[:, :]),
                        wait=[t_wn, t_idb] + ([t_wq] if de >= 2 else []))
                    t_wq = p.op("scalar", lambda s, de=de, sel3=sel3, tb=tb: s.copy(
                        wqT[:, sel3, de, :], ps_tr[:, tb, 0:128]), wait=tt1)
            t_ev = None
            for sel3, dst in ((0, qg), (1, kg), (2, vg)):
                for b in range(B):
                    for half in range(2):
                        pb = (b * 2 + half) % 2
                        tokm = None
                        for dt in range(NDT):
                            tokm = p.op("tensor",
                                        lambda t, sel3=sel3, dt=dt, pb=pb, b=b, half=half: t.matmul(
                                            ps_mm[:, pb, :], wqT[:, sel3, dt, :],
                                            xn1T[:, dt, b * SL + half * 512:b * SL + (half + 1) * 512],
                                            start=(dt == 0), stop=(dt == NDT - 1)),
                                        wait=[t_wq, t_xn, t_ev, t_accw] if dt == 0 else [])
                        t_ev = p.op("vector",
                                    lambda v, dst=dst, b=b, half=half, pb=pb: v.tensor_copy(
                                        dst[:, b, half * 512:(half + 1) * 512], ps_mm[:, pb, :]),
                                    wait=tokm)
            t_scm = None
            for a1 in range(4):
                for a2 in range(4):
                    pp = a1 * 4 + a2
                    t_pr = p.op("vector", lambda v, a1=a1, a2=a2: v.tensor_tensor(
                        prod[:, :], qg[:, a1, :], kg[:, a2, :], ALU.mult),
                        wait=[t_ev] if pp == 0 else [t_scm])
                    for half in range(2):
                        t_scm = p.op("tensor", lambda t, pp=pp, half=half: t.matmul(
                            ps_sc[:, half, :], pairK2[:, pp * 32:(pp + 1) * 32],
                            prod[:, half * 512:(half + 1) * 512],
                            start=(pp == 0), stop=(pp == 15), skip_group_check=True),
                            wait=[t_pr, t_pk] + ([t_accw, t_E] if pp == 0 else []))
            t_E = p.op("scalar", lambda s: s.activation(
                Ebuf[:, :], ps_sc[:, :, :].rearrange("p a c -> p (a c)"), AF.Exp,
                scale=cst[0:32, 0:1]), wait=[t_scm, t_cq])
            t_en = None
            t_rc = None
            for half in range(2):
                t_dn = p.op("tensor", lambda t, half=half: t.matmul(
                    ps_bc[0:8, 0, 0:512], caus2[:, :], Ebuf[:, half * 512:(half + 1) * 512],
                    start=True, stop=True),
                    wait=[t_E, t_c2, t_rc] + ([t_accw] if half == 0 else []))
                t_rc = p.op("vector", lambda v, half=half: v.reciprocal(
                    rden[:, half * 512:(half + 1) * 512], ps_bc[0:8, 0, 0:512]), wait=t_dn)
                t_dx = p.op("tensor", lambda t, half=half: t.matmul(
                    ps_bc[0:32, 1, 0:512], denbc2[:, :], rden[:, half * 512:(half + 1) * 512],
                    start=True, stop=True), wait=[t_rc, t_db])
                t_en = p.op("vector", lambda v, half=half: v.tensor_tensor(
                    Enb[:, half * 512:(half + 1) * 512], Ebuf[:, half * 512:(half + 1) * 512],
                    ps_bc[0:32, 1, 0:512], ALU.mult), wait=t_dx)
            t_accw = None
            for a1 in range(4):
                for a2 in range(a1 + 1):
                    pp = a1 * 4 + a2
                    for half in range(2):
                        bs = (a2 * 2 + half) % 2
                        t_bm = p.op("tensor", lambda t, pp=pp, half=half, bs=bs: t.matmul(
                            ps_bc[:, bs, :], pairsel2[:, pp * 128:(pp + 1) * 128],
                            Enb[:, half * 512:(half + 1) * 512], start=True, stop=True),
                            wait=[t_en, t_p2, t_accw])
                        if a2 == 0:
                            t_accw = p.op("vector",
                                          lambda v, hg=hg, a1=a1, half=half, bs=bs: v.tensor_tensor(
                                              outT[:, hg, a1, half * 512:(half + 1) * 512],
                                              vg[:, 0, half * 512:(half + 1) * 512],
                                              ps_bc[:, bs, :], ALU.mult), wait=t_bm)
                        else:
                            p.op("vector", lambda v, a2=a2, half=half, bs=bs: v.tensor_tensor(
                                wvt[:, half * 512:(half + 1) * 512],
                                vg[:, a2, half * 512:(half + 1) * 512],
                                ps_bc[:, bs, :], ALU.mult), wait=t_bm)
                            t_accw = p.op("vector",
                                          lambda v, hg=hg, a1=a1, half=half: v.tensor_tensor(
                                              outT[:, hg, a1, half * 512:(half + 1) * 512],
                                              outT[:, hg, a1, half * 512:(half + 1) * 512],
                                              wvt[:, half * 512:(half + 1) * 512], ALU.add))
            if "dbg" in io:
                p.dma("gpsimd", io["dbg"][hg % 2, :, :],
                      outT[:, 0, :, :].rearrange("p a c -> p (a c)"), wait=t_accw, ch="cdbg")
        t_attn = [t_accw, p.tok("tensor"), p.tok("scalar")]

    with ExitStack() as ph:
        s2 = lambda name, shape, dt: ph.enter_context(nc.sbuf_tensor(p.tag + name, shape, dt))
        woT = s2("woT", [128, NDT, D], BF16)
        wst = s2("wst", [128, NDT, 128], BF16)
        s1b = s2("s1b", [128, NDT, SL], BF16)
        s1s = s2("s1s", [128, NDT, 128], BF16)

        t_wo = None
        t_wl = None
        for do in range(NDT):
            t_wl = p.dma("gpsimd", wst[:, :, :],
                         io["wout"][do * 128:(do + 1) * 128, :]
                         .rearrange("e (dt c) -> e dt c", c=128),
                         wait=[t_wl, t_wo] + (t_attn if do == 0 else []), ch="cw")
            for de in range(NDT):
                tb = de % 2
                tt1 = p.op("tensor", lambda t, de=de, tb=tb: t.transpose(
                    ps_tr[:, tb, 0:128], wst[:, de, :], identb[:, :]),
                    wait=[t_wl] + ([t_wo] if de >= 2 else []) + (t_attn if do == 0 and de < 2 else []))
                t_wo = p.op("scalar", lambda s, de=de, do=do, tb=tb: s.copy(
                    woT[:, de, do * 128:(do + 1) * 128], ps_tr[:, tb, 0:128]), wait=tt1)
        t_s1 = None
        t_sp = None
        t_cp = None
        for b in range(B):
            for do in range(NDT):
                for half in range(2):
                    pb = (do * 2 + half) % 2
                    tokm = None
                    for de in range(NDT):
                        tokm = p.op("tensor",
                                    lambda t, de=de, do=do, b=b, half=half, pb=pb: t.matmul(
                                        ps_mm[:, pb, :], woT[:, de, do * 128:(do + 1) * 128],
                                        outT[:, de, b, half * 512:(half + 1) * 512],
                                        start=(de == 0), stop=(de == NDT - 1)),
                                    wait=[t_wo, t_s1] if de == 0 else [])
                    t_s1 = p.op("vector", lambda v, do=do, half=half, pb=pb: v.tensor_scalar(
                        s1b[:, do, half * 512:(half + 1) * 512], ps_mm[:, pb, :],
                        bout8[:, do:do + 1], None, ALU.add), wait=[tokm, t_bo])
            for mt in range(NMT):
                for dt in range(NDT):
                    tb = dt % 2
                    tt1 = p.op("tensor", lambda t, dt=dt, mt=mt, tb=tb: t.transpose(
                        ps_tr[:, tb, 0:128], s1b[:, dt, mt * 128:(mt + 1) * 128], identb[:, :]),
                        wait=[t_s1] + ([t_cp] if dt >= 2 else []) + ([t_sp] if dt < 2 else []))
                    t_cp = p.op("scalar", lambda s, dt=dt, tb=tb: s.copy(
                        s1s[:, dt, :], ps_tr[:, tb, 0:128]), wait=tt1)
                t_sp = p.dma("gpsimd",
                             io["s1tm"][b * SL + mt * 128:b * SL + (mt + 1) * 128, :],
                             s1s[:, :, :].rearrange("p dt c -> p (dt c)"),
                             wait=t_cp, ch="cs1")


def build_l3(dbg=False):
    nc = bass.Bass("TRN2", target_bir_lowering=False, detect_race_conditions=RACE_DETECT)
    io = decl(nc, ["wqkv", "wout", "bout", "ident", "pairK2", "caus2", "denbc2",
                   "pairsel2", "xn1T", "s1tm"],
              ext_in=["xn1T"], ext_out=["s1tm"])
    if dbg:
        io["dbg"] = nc.dram_tensor("dbg", [2, 128, 4096], F32, kind="ExternalOutput").ap()
    with ExitStack() as stack:
        p = Prog(nc, stack)
        blk = stack.enter_context(nc.Block())
        emit_l3(nc, p, io)
        p.final_waits()
        p.emit(blk)
    return nc




def emit_l4(nc, p, io):
    st = p.stack
    sb = lambda name, shape, dt: st.enter_context(nc.sbuf_tensor(p.tag + name, shape, dt))
    psm = lambda name, shape, dt=F32: st.enter_context(nc.psum_tensor(p.tag + name, shape, dt))

    identb = sb("identb", [128, 128], BF16)
    identf = sb("identf", [128, 128], F32)
    eselc = sb("eselc", [8, 1024], BF16)
    maskE = sb("maskE4", [8, T], BF16)
    sumb2 = sb("sumb2", [128, NDT], F32)
    b1T = sb("b1T", [128, 32, 8], F32)
    mbc = sb("mbc", [128, 2, 512], BF16)
    g1s = sb("g1s", [128, NDT, 128], BF16)
    g1t = sb("g1t", [128, NDT, 128], BF16)
    s1l = sb("s1l", [128, 2, D], BF16)
    x1l = sb("x1l", [128, 2, D], F32)

    ps_h = psm("ps_h", [128, 2, 512], F32)
    ps_o = psm("ps_o", [128, 2, 512], F32)
    ps_tr = psm("ps_tr", [128, 2, 1024], BF16)
    ps_bc = psm("ps_bc", [128, 2, 512], F32)

    t_idb = p.dma("gpsimd", identb[:, :], io["ident"], ch="cst")
    t_idf = p.dma("sync", identf[:, :], io["ident"], ch="cst2")
    t_es = p.dma("gpsimd", eselc[:, :], io["esel"], ch="cst")

    b2r = sb("b2r", [128, NDT, NE], F32)

    with ExitStack() as ph:
        s2 = lambda name, shape, dt: ph.enter_context(nc.sbuf_tensor(p.tag + name, shape, dt))
        xn1T = s2("s_xn1T", [128, NDT, T], BF16)
        wr16 = s2("wr16", [128, NDT, 8], BF16)
        brc = s2("brc", [8, 1], F32)
        rE = s2("rE", [8, T], F32)
        rtm = s2("rtm", [128, 32, 8], F32)
        mask3 = s2("mask34", [128, 32, 8], F32)
        gw1 = s2("gw14", [128, 32], F32)
        gw2 = s2("gw24", [128, 32], F32)
        gw3 = s2("gw34", [128, 32], F32)

        t_xn = None
        for dt in range(NDT):
            t_xn = p.dma("gpsimd", xn1T[:, dt, :], io["xn1T"][:, dt, :], ch="cxn")
        t_wr = p.dma("gpsimd", wr16[:, :, :], io["Wr"].rearrange("(dt p) c -> p dt c", p=128),
                     ch="cst")
        b1n = s2("b1n", [8, FFH], F32)
        t_b1n = p.dma("gpsimd", b1n[:, :], io["b1"], ch="cst")
        b2n = s2("b2n", [8, D], F32)
        t_b2n = p.dma("gpsimd", b2n[:, :], io["b2"], ch="cst")
        t_b2c = None
        for c in range(NDT):
            tb = c % 2
            tt8 = p.op("tensor", lambda t, c=c, tb=tb: t.transpose(
                ps_h[:, tb, 0:8], b2n[:, c * 128:(c + 1) * 128], identf[0:8, 0:8]),
                wait=[t_b2n, t_idf] + ([t_b2c] if c >= 2 else []))
            t_b2c = p.op("scalar", lambda s, c=c, tb=tb: s.copy(
                b2r[:, c, :], ps_h[:, tb, 0:8]), wait=tt8)
        t_sb2 = p.op("vector", lambda v: v.tensor_reduce(
            sumb2[:, :], b2r[:, :, :], AXX, ALU.add), wait=t_b2c)
        t_b1 = None
        for c in range(32):
            tb = c % 2
            tt9 = p.op("tensor", lambda t, c=c, tb=tb: t.transpose(
                ps_o[:, tb, 0:8], b1n[:, c * 128:(c + 1) * 128], identf[0:8, 0:8]),
                wait=[t_b1n, t_idf] + ([t_b1] if c >= 2 else []))
            t_b1 = p.op("scalar", lambda s, c=c, tb=tb: s.copy(
                b1T[:, c, :], ps_o[:, tb, 0:8]), wait=tt9)
        t_br = p.dma("sync", brc[:, :], io["br"].rearrange("(e o) -> e o", o=1), ch="cst2")
        t_rE = None
        for i in range(8):
            pb = i % 2
            c0 = i * 512
            tokm = None
            for dt in range(NDT):
                tokm = p.op("tensor", lambda t, dt=dt, pb=pb, c0=c0: t.matmul(
                    ps_h[0:8, pb, :], wr16[:, dt, :], xn1T[:, dt, c0:c0 + 512],
                    start=(dt == 0), stop=(dt == NDT - 1)),
                    wait=([t_xn, t_wr] + ([t_rE] if i >= 2 else [])) if dt == 0 else [])
            t_rE = p.op("vector", lambda v, pb=pb, c0=c0: v.tensor_scalar(
                rE[:, c0:c0 + 512], ps_h[0:8, pb, :], brc[:, 0:1], None, ALU.add),
                wait=[tokm, t_br])
        t_ltm = None
        for c in range(32):
            tb = c % 2
            ttr = p.op("tensor", lambda t, c=c, tb=tb: t.transpose(
                ps_bc[:, tb, 0:8], rE[:, c * 128:(c + 1) * 128], identf[0:8, 0:8]),
                wait=[t_rE, t_idf] + ([t_ltm] if c >= 2 else []))
            t_ltm = p.op("scalar", lambda s, c=c, tb=tb: s.copy(
                rtm[:, c, :], ps_bc[:, tb, 0:8]), wait=ttr)
        p.op("vector", lambda v: v.tensor_copy(gw1[:, :], rtm[:, :, 0]), wait=t_ltm)
        for e2 in range(1, NE):
            p.op("vector", lambda v, e2=e2: v.tensor_max(gw1[:, :], gw1[:, :], rtm[:, :, e2]))
        p.op("vector", lambda v: v.memset(gw2[:, :], 0.0))
        t_msk = None
        for e2 in range(NE):
            p.op("vector", lambda v, e2=e2: v.tensor_tensor(
                gw3[:, :], rtm[:, :, e2], gw1[:, :], ALU.is_ge))
            p.op("vector", lambda v, e2=e2: v.tensor_tensor(
                mask3[:, :, e2], gw3[:, :], gw2[:, :], ALU.mult))
            p.op("vector", lambda v, e2=e2: v.tensor_tensor(
                mask3[:, :, e2], gw3[:, :], mask3[:, :, e2], ALU.subtract))
            t_msk = p.op("vector", lambda v, e2=e2: v.tensor_tensor(
                gw2[:, :], gw2[:, :], mask3[:, :, e2], ALU.add))
        t_mE = None
        for c in range(32):
            tb = c % 2
            ttr = p.op("tensor", lambda t, c=c, tb=tb: t.transpose(
                ps_bc[0:8, tb, 0:128], mask3[:, c, :], identf[:, :]),
                wait=[t_msk] + ([t_mE] if c >= 2 else []))
            t_mE = p.op("vector", lambda v, c=c, tb=tb: v.tensor_copy(
                maskE[:, c * 128:(c + 1) * 128], ps_bc[0:8, tb, 0:128]), wait=ttr)
        t_route = [t_mE, p.tok("tensor"), p.tok("scalar"), p.tok("vector")]

    xhal = sb("xhal", [128, NDT, 2048], BF16)
    xi = sb("xi", [128, NDT, 2048], BF16)
    w1q = sb("w1q", [128, 2, NDT, 1024], BF16)
    w2q = sb("w2q", [128, 2, NDT, 1024], BF16)
    outs = sb("outs", [128, NDT, 2048], BF16)
    Ht = sb("Ht", [128, 2, 8, 512], BF16)
    t_old = t_route
    oacc_hist = []
    t_m1_prev = None
    t_g1 = None
    t_sp = None
    t_w1 = [None, None]
    t_w2 = [None, None]
    w1_user = [None, None]
    w2_user = [None, None]
    relu_hist = []
    htuser = [None, None]
    t_xh = None
    t_xiu = None
    t_ord = None
    for th in range(2):
        tok0 = th * 2048
        t_xh = p.dma("gpsimd", xhal[:, :, :], io["xn1T"][:, :, tok0:tok0 + 2048],
                     wait=[t_xiu] + (t_route if th == 0 else []), ch="cxh")
        t_out0 = p.op("vector", lambda v: v.memset(outs[:, :, :], 0.0),
                      wait=[t_g1] + (t_route if th == 0 else []))
        t_oacc = t_out0
        for e in range(NE):
            t_xiu = None
            for tc in range(4):
                bs = tc % 2
                t_bm = p.op("tensor", lambda t, e=e, tc=tc, bs=bs, tok0=tok0: t.matmul(
                    ps_bc[:, bs, :], eselc[:, e * 128:(e + 1) * 128],
                    maskE[:, tok0 + tc * 512:tok0 + (tc + 1) * 512],
                    start=True, stop=True),
                    wait=[t_mE, t_es, t_xiu if tc >= 2 else t_ord]
                    if True else []) 
                t_mb = p.op("scalar", lambda s, bs=bs: s.copy(
                    mbc[:, bs, :], ps_bc[:, bs, :]), wait=t_bm)
                for dt in range(NDT):
                    t_xiu = p.op("vector", lambda v, dt=dt, tc=tc, bs=bs: v.tensor_tensor(
                        xi[:, dt, tc * 512:(tc + 1) * 512],
                        xhal[:, dt, tc * 512:(tc + 1) * 512], mbc[:, bs, :], ALU.mult),
                        wait=[t_mb, t_xh])
            for q in range(4):
                wb = (e * 4 + q) % 2
                t_w1[wb] = p.dma("gpsimd", w1q[:, wb, :, :],
                                 io["W1"][e, :, :].rearrange("(dt p) h -> p dt h", p=128)
                                 [:, :, q * 1024:(q + 1) * 1024],
                                 wait=[t_w1[wb], w1_user[wb]], ch=f"cw1{wb}")
                t_w2[wb] = p.dma("gpsimd", w2q[:, wb, :, :],
                                 io["W2"][e, q * 1024:(q + 1) * 1024, :]
                                 .rearrange("(ht p) d -> p ht d", p=128),
                                 wait=[t_w2[wb], w2_user[wb]], ch=f"cw2{wb}")
                relu_last = [None, None]
                for tcs in range(5):
                    if tcs < 4:
                        tc = tcs
                        hs2 = tc % 2
                        for ht in range(8):
                            tokm = None
                            for dt in range(NDT):
                                tokm = p.op("tensor",
                                            lambda t, tc=tc, ht=ht, dt=dt, wb=wb,
                                            hb=(tc * 8 + ht) % 2: t.matmul(
                                                ps_h[:, hb, :],
                                                w1q[:, wb, dt, ht * 128:(ht + 1) * 128],
                                                xi[:, dt, tc * 512:(tc + 1) * 512],
                                                start=(dt == 0), stop=(dt == NDT - 1)),
                                            wait=([t_w1[wb], t_xiu] +
                                                  ([relu_hist[-2]] if len(relu_hist) >= 2 else []))
                                            if dt == 0 else [])
                            t_relu = p.op("scalar",
                                          lambda s, tc=tc, ht=ht, hb=(tc * 8 + ht) % 2,
                                          hs2=hs2, e=e, q=q: s.activation(
                                              Ht[:, hs2, ht, :], ps_h[:, hb, :], AF.Relu,
                                              bias=b1T[:, q * 8 + ht, e:e + 1]),
                                          wait=[tokm, t_b1, htuser[hs2]])
                            relu_hist.append(t_relu)
                        relu_last[hs2] = t_relu
                    if tcs >= 1:
                        tc = tcs - 1
                        hs2 = tc % 2
                        for do in range(NDT):
                            ob = do % 2
                            tokm = None
                            for ht in range(8):
                                tokm = p.op("tensor",
                                            lambda t, do=do, ht=ht, ob=ob, wb=wb, hs2=hs2: t.matmul(
                                                ps_o[:, ob, :],
                                                w2q[:, wb, ht, do * 128:(do + 1) * 128],
                                                Ht[:, hs2, ht, :], start=(ht == 0), stop=(ht == 7)),
                                            wait=([t_w2[wb], relu_last[hs2]] +
                                                  ([oacc_hist[-2]] if len(oacc_hist) >= 2 else []))
                                            if ht == 0 else [])
                            t_oacc = p.op("vector", lambda v, do=do, tc=tc, ob=ob: v.tensor_tensor(
                                outs[:, do, tc * 512:(tc + 1) * 512],
                                outs[:, do, tc * 512:(tc + 1) * 512], ps_o[:, ob, :], ALU.add),
                                wait=tokm)
                            oacc_hist.append(t_oacc)
                            w2_user[wb] = tokm
                            htuser[hs2] = tokm
                    w1_user[wb] = relu_hist[-1] if relu_hist else None
            t_ord = t_oacc
        for mt in range(16):
            r0 = tok0 + mt * 128
            bq, sl0 = divmod(r0, SL)
            li = mt % 2
            t_x1 = p.dma("sync", x1l[:, li, :], io["x1"][r0:r0 + 128, :],
                         wait=t_sp, ch=f"cx{li}")
            t_s1 = p.dma("gpsimd", s1l[:, li, :], io["s1tm"][r0:r0 + 128, :],
                         wait=t_sp, ch=f"cs{li}")
            t_si = None
            for dt in range(NDT):
                t_si = p.op("scalar", lambda s, dt=dt, mt=mt: s.activation(
                    g1s[:, dt, :], outs[:, dt, mt * 128:(mt + 1) * 128], AF.Silu,
                    bias=sumb2[:, dt:dt + 1]), wait=[t_ord, t_sb2] if dt == 0 else [])
            t_gt = None
            for dt in range(NDT):
                tb = dt % 2
                ttr = p.op("tensor", lambda t, dt=dt, tb=tb: t.transpose(
                    ps_tr[:, tb, 0:128], g1s[:, dt, :], identb[:, :]),
                    wait=[t_si, t_idb] + ([t_gt] if dt >= 2 else []))
                t_gt = p.op("scalar", lambda s, dt=dt, tb=tb: s.copy(
                    g1t[:, dt, :], ps_tr[:, tb, 0:128]),
                    wait=[ttr] + ([t_m1_prev] if dt == 0 else []))
            t_m1 = p.op("vector", lambda v, li=li: v.tensor_tensor(
                s1l[:, li, :], s1l[:, li, :], g1t[:, :, :].rearrange("p a c -> p (a c)"),
                ALU.mult), wait=[t_gt, t_s1])
            t_m1_prev = t_m1
            t_x2 = p.op("vector", lambda v, li=li: v.tensor_tensor(
                x1l[:, li, :], x1l[:, li, :], s1l[:, li, :], ALU.add), wait=t_x1)
            t_sp = p.dma("gpsimd", io["yout"][bq, sl0:sl0 + 128, :], x1l[:, li, :],
                         wait=t_x2, ch="cy")
        t_g1 = p.tok("scalar")
        t_xiu = p.tok("vector")


def build_l4():
    nc = bass.Bass("TRN2", target_bir_lowering=False, detect_race_conditions=RACE_DETECT)
    io = decl(nc, ["W1", "b1", "W2", "b2", "Wr", "br", "ident", "esel",
                   "xn1T", "x1", "s1tm", "yout"],
              ext_in=["xn1T", "x1", "s1tm"], ext_out=["yout"])
    with ExitStack() as stack:
        p = Prog(nc, stack)
        blk = stack.enter_context(nc.Block())
        emit_l4(nc, p, io)
        p.final_waits()
        p.emit(blk)
    return nc


_NC_CACHE = {}


def _get_nc(name, builder):
    if name not in _NC_CACHE:
        _NC_CACHE[name] = builder()
    return _NC_CACHE[name]




def emit_cc(nc, p, io):
    tok = p.op("gpsimd", lambda g: g.collective_compute(
        "AllGather", ALU.bypass, replica_groups=[list(range(NCORES))],
        ins=[io["summ"].opt()], outs=[io["summall"].opt()]), n=1)
    p.opn("gpsimd", lambda g: g.nop(), wait=tok)


ALL_NAMES = ["xs", "wq", "wk", "wv", "wbeta", "bbeta", "wo", "bo", "gam", "lnb",
             "wgate", "wexp", "bexp", "wqkv", "wout", "bout",
             "W1", "b1", "W2", "b2", "Wr", "br",
             "cos8", "sin8", "ident", "supper", "lower", "sel", "hsel", "csel",
             "wsel", "pairK2", "caus2", "denbc2", "pairsel2", "esel",
             "xnT", "qtm", "ktm", "vtm", "lbtm", "cl", "sloc", "cntq", "gat",
             "summ", "summall", "x1", "xn1T", "s1tm", "yout"]


def build_fused():
    nc = bass.Bass("TRN2", target_bir_lowering=False, detect_race_conditions=RACE_DETECT)
    io = decl(nc, ALL_NAMES, ext_out=["yout"])
    with ExitStack() as outer:
        ems = (emit_l1, emit_cc, emit_l2, emit_l3, emit_l4)
        if _os.environ.get("SKIP_CC"):
            ems = (emit_l1, emit_l2, emit_l3, emit_l4)
        for i, em in enumerate(ems):
            with ExitStack() as stg:
                p = Prog(nc, stg, sem_stack=outer, tag=f"s{i}_")
                blk = stg.enter_context(nc.Block())
                em(nc, p, io)
                p.final_waits()
                p.emit(blk)
    return nc


def kernel(**inputs):
    I = inputs
    cores = list(range(NCORES))
    hcs = [host_consts(c) for c in cores]
    nc = _get_nc("fused", build_fused)
    maps = []
    for c in cores:
        hc = hcs[c]
        m = {
            "xs": np.ascontiguousarray(I["L0_attn_x"][:, c * SL:(c + 1) * SL, :]),
            "wq": I["L0_attn_Wq"], "wk": I["L0_attn_Wk"], "wv": I["L0_attn_Wv"],
            "wbeta": I["L0_attn_Wbeta"],
            "bbeta": np.ascontiguousarray(np.asarray(I["L0_attn_bbeta"]).reshape(1, 32)),
            "wo": I["L0_attn_Wo"], "bo": I["L0_attn_bo"],
            "gam": I["L0_attn_ln_gamma"], "lnb": I["L0_attn_ln_beta"],
            "wgate": I["L0_ffn_w_gate"], "wexp": I["L0_ffn_w_expert"],
            "bexp": np.ascontiguousarray(np.asarray(I["L0_ffn_b_expert"]).reshape(1, D)),
            "wqkv": I["L1_attn_w_qkv"], "wout": I["L1_attn_w_out"],
            "bout": I["L1_attn_b_out"],
            "W1": I["L1_ffn_W1"], "b1": I["L1_ffn_b1"], "W2": I["L1_ffn_W2"],
            "b2": I["L1_ffn_b2"], "Wr": I["L1_ffn_Wr"], "br": I["L1_ffn_br"],
        }
        for nm in ("cos8", "sin8", "ident", "supper", "lower", "sel", "hsel",
                   "csel", "wsel", "pairK2", "caus2", "denbc2", "pairsel2", "esel"):
            m[nm] = hc[nm]
        maps.append(m)
    res = run_bass_kernel_spmd(nc, maps, core_ids=cores).results
    out = np.zeros((B, S, D), dtype=np.float32)
    for c in cores:
        out[:, c * SL:(c + 1) * SL, :] = np.asarray(res[c]["yout"])
    return out


def kernel_unfused(**inputs):
    I = inputs
    cores = list(range(NCORES))
    hcs = [host_consts(c) for c in cores]
    xsl = [np.ascontiguousarray(I["L0_attn_x"][:, c * SL:(c + 1) * SL, :]) for c in cores]

    nc1 = _get_nc("l1", build_l1)
    m1 = [{
        "xs": xsl[c], "wq": I["L0_attn_Wq"], "wk": I["L0_attn_Wk"], "wv": I["L0_attn_Wv"],
        "wbeta": I["L0_attn_Wbeta"], "bbeta": np.ascontiguousarray(
            np.asarray(I["L0_attn_bbeta"]).reshape(1, 32)),
        "wgate": I["L0_ffn_w_gate"],
        "cos8": hcs[c]["cos8"], "sin8": hcs[c]["sin8"], "ident": hcs[c]["ident"],
    } for c in cores]
    r1 = run_bass_kernel_spmd(nc1, m1, core_ids=cores).results
    summall = np.concatenate([np.asarray(r1[c]["summ"]) for c in cores], axis=0)

    nc2 = _get_nc("l2", build_l2)
    m2 = [{
        "xs": xsl[c], "wo": I["L0_attn_Wo"], "bo": I["L0_attn_bo"],
        "gam": I["L0_attn_ln_gamma"], "lnb": I["L0_attn_ln_beta"],
        "wexp": I["L0_ffn_w_expert"], "bexp": np.ascontiguousarray(
            np.asarray(I["L0_ffn_b_expert"]).reshape(1, D)),
        "ident": hcs[c]["ident"], "supper": hcs[c]["supper"], "lower": hcs[c]["lower"],
        "sel": hcs[c]["sel"], "hsel": hcs[c]["hsel"], "csel": hcs[c]["csel"],
        "wsel": hcs[c]["wsel"], "summall": summall,
        "xnT": np.asarray(r1[c]["xnT"]), "qtm": np.asarray(r1[c]["qtm"]),
        "ktm": np.asarray(r1[c]["ktm"]), "vtm": np.asarray(r1[c]["vtm"]),
        "lbtm": np.asarray(r1[c]["lbtm"]), "cl": np.asarray(r1[c]["cl"]),
        "sloc": np.asarray(r1[c]["sloc"]), "cntq": np.asarray(r1[c]["cntq"]),
        "gat": np.asarray(r1[c]["gat"]),
    } for c in cores]
    r2 = run_bass_kernel_spmd(nc2, m2, core_ids=cores).results

    nc3 = _get_nc("l3", build_l3)
    m3 = [{
        "wqkv": I["L1_attn_w_qkv"], "wout": I["L1_attn_w_out"], "bout": I["L1_attn_b_out"],
        "ident": hcs[c]["ident"], "pairK2": hcs[c]["pairK2"], "caus2": hcs[c]["caus2"],
        "denbc2": hcs[c]["denbc2"], "pairsel2": hcs[c]["pairsel2"],
        "xn1T": np.asarray(r2[c]["xn1T"]),
    } for c in cores]
    r3 = run_bass_kernel_spmd(nc3, m3, core_ids=cores).results

    nc4 = _get_nc("l4", build_l4)
    m4 = [{
        "W1": I["L1_ffn_W1"], "b1": I["L1_ffn_b1"], "W2": I["L1_ffn_W2"],
        "b2": I["L1_ffn_b2"], "Wr": I["L1_ffn_Wr"], "br": I["L1_ffn_br"],
        "ident": hcs[c]["ident"], "esel": hcs[c]["esel"],
        "xn1T": np.asarray(r2[c]["xn1T"]), "x1": np.asarray(r2[c]["x1"]),
        "s1tm": np.asarray(r3[c]["s1tm"]),
    } for c in cores]
    r4 = run_bass_kernel_spmd(nc4, m4, core_ids=cores).results

    out = np.zeros((B, S, D), dtype=np.float32)
    for c in cores:
        out[:, c * SL:(c + 1) * SL, :] = np.asarray(r4[c]["yout"])
    return out


# version 05 (final)
# speedup vs baseline: 1.2958x; 1.1546x over previous
import numpy as np
from contextlib import ExitStack

import concourse.bass as bass
import concourse.mybir as mybir
from concourse.bass_utils import run_bass_kernel_spmd

F32 = mybir.dt.float32
BF16 = mybir.dt.bfloat16
AF = mybir.ActivationFunctionType
ALU = mybir.AluOpType
AXX = mybir.AxisListType.X

NCORES = 8
B = 4
S = 8192
D = 1024
SL = S // NCORES
T = B * SL
NMT = SL // 128
NDT = D // 128
H = 16
DH = 64
NCH = 8
EPS_DN = 1e-6
LN_EPS = 1e-5
NE = 8
CAP = 100
FFH = 4096


class SemT:
    def __init__(self, stack, nc, name):
        self.h = stack.enter_context(nc.semaphore(name))
        self.v = 0


class Prog:
    ENGS = ("sync", "gpsimd", "tensor", "vector", "scalar")

    def __init__(self, nc, stack, sem_stack=None, tag=""):
        self.nc = nc
        self.stack = stack
        self.sem_stack = sem_stack or stack
        self.tag = tag
        self.items = {e: [] for e in self.ENGS}
        self.sems = {e: SemT(self.sem_stack, nc, f"{tag}c_{e}") for e in self.ENGS}

    def _norm(self, wait):
        if wait is None or wait == ():
            return []
        if not isinstance(wait, list):
            wait = [wait]
        out = []
        for w in wait:
            if w is None:
                continue
            out.append((w.h, w.v) if isinstance(w, SemT) else (w[0].h, w[1]))
        return out

    CHAINED = ("vector", "scalar")

    def op(self, eng, fn, wait=(), n=1):
        sem = self.sems[eng]
        wl = self._norm(wait)
        if eng in self.CHAINED and sem.v > 0:
            wl.append((sem.h, sem.v))
        sem.v += n
        tok = (sem, sem.v)

        def g(e, wl=wl, fn=fn, sem=sem, n=n):
            for h, v in wl:
                if v > 0:
                    e.wait_ge(h, v)
            fn(e).then_inc(sem.h, n)
        self.items[eng].append(g)
        return tok

    def opn(self, eng, fn, wait=()):
        if eng in self.CHAINED:
            return self.op(eng, fn, wait=wait)
        wl = self._norm(wait)

        def g(e, wl=wl, fn=fn):
            for h, v in wl:
                if v > 0:
                    e.wait_ge(h, v)
            fn(e)
        self.items[eng].append(g)

    def dma(self, eng, out, in_, wait=(), ch=None, **kw):
        key = "ch_" + (ch or eng)
        if key not in self.sems:
            self.sems[key] = SemT(self.sem_stack, self.nc, self.tag + key)
        sem = self.sems[key]
        wl = self._norm(wait) + ([(sem.h, sem.v)] if sem.v > 0 else [])
        sem.v += 16
        tok = (sem, sem.v)

        def g(e, wl=wl, out=out, in_=in_, sem=sem, kw=kw):
            for h, v in wl:
                if v > 0:
                    e.wait_ge(h, v)
            e.dma_start(out=out, in_=in_, **kw).then_inc(sem.h, 16)
        self.items[eng].append(g)
        return tok

    def tok(self, eng):
        s = self.sems[eng]
        return (s, s.v)

    def final_waits(self):
        for key, s in list(self.sems.items()):
            if key.startswith("ch_") and s.v > 0:
                self.opn("gpsimd", lambda g, s=s, v=s.v: g.wait_ge(s.h, v))

    def emit(self, blk):
        for name in self.ENGS:
            items = self.items[name]
            if not items:
                continue

            def body(e, items=items):
                for g in items:
                    g(e)
            getattr(blk, name)(body)


def host_consts(core):
    out = {}
    inv_freq = 1.0 / 10000.0 ** (np.arange(0, DH, 2, dtype=np.float64) / DH)
    t = np.arange(core * SL, (core + 1) * SL, dtype=np.float64)
    fr = t[:, None] * inv_freq[None, :]
    out["cos8"] = np.tile(np.cos(fr).astype(np.float32), (1, 8))
    out["sin8"] = np.tile(np.sin(fr).astype(np.float32), (1, 8))
    out["ident"] = np.eye(128, dtype=np.float32)
    r = np.arange(128)
    out["supper"] = (r[:, None] > r[None, :]).astype(np.float32)
    out["lower"] = (r[:, None] <= r[None, :]).astype(np.float32)
    sel = np.zeros((32, 4), dtype=np.float32)
    for j in range(NCORES):
        for bp in range(B):
            for b in range(B):
                if bp < b or (bp == b and j < core):
                    sel[j * 4 + bp, b] = 1.0
    out["sel"] = sel
    pairK2 = np.zeros((128, 512), dtype=np.float32)
    for a1 in range(4):
        for a2 in range(4):
            pp = a1 * 4 + a2
            for hh in range(2):
                ks = slice(hh * 64, (hh + 1) * 64)
                if a1 == a2:
                    for a3 in range(a1):
                        pairK2[ks, pp * 32 + hh * 16 + a1 * 4 + a3] = -1.0
                else:
                    pairK2[ks, pp * 32 + hh * 16 + pp] = 1.0
    out["pairK2"] = pairK2
    caus2 = np.zeros((32, 8), dtype=np.float32)
    for hh in range(2):
        for a in range(4):
            for ap_ in range(a + 1):
                caus2[hh * 16 + a * 4 + ap_, hh * 4 + a] = 1.0
    out["caus2"] = caus2
    denbc2 = np.zeros((8, 32), dtype=np.float32)
    for hh in range(2):
        for a in range(4):
            for ap_ in range(a + 1):
                denbc2[hh * 4 + a, hh * 16 + a * 4 + ap_] = 1.0
    out["denbc2"] = denbc2
    ps2 = np.zeros((32, 2048), dtype=np.float32)
    for a1 in range(4):
        for a2 in range(4):
            pp = a1 * 4 + a2
            for hh in range(2):
                ps2[hh * 16 + pp, pp * 128 + hh * 64:pp * 128 + (hh + 1) * 64] = 1.0
    out["pairsel2"] = ps2
    hsel = np.zeros((32, 16), dtype=np.float32)
    for hc in range(32):
        hsel[hc, hc >> 1] = 1.0
    out["hsel"] = hsel
    csel = np.zeros((32, 128), dtype=np.float32)
    for hc in range(32):
        for pp in range(128):
            if (hc & 1) == (pp >> 6):
                csel[hc, pp] = 1.0
    out["csel"] = csel
    out["wsel"] = np.array([[1.0 if j < core else 0.0 for j in range(8)]], np.float32)
    esel = np.zeros((8, 1024), dtype=np.float32)
    for e2 in range(8):
        esel[e2, e2 * 128:(e2 + 1) * 128] = 1.0
    out["esel"] = esel
    return out


IN_SHAPES = {
    "xs": [B, SL, D], "wq": [D, D], "wk": [D, D], "wv": [D, D],
    "wbeta": [D, 32], "bbeta": [1, 32], "wo": [D, D], "bo": [D],
    "gam": [D], "lnb": [D], "wgate": [D, NE], "wexp": [D, D], "bexp": [1, D],
    "wqkv": [3 * D, D], "wout": [D, D], "bout": [D],
    "W1": [NE, D, FFH], "b1": [NE, FFH], "W2": [NE, FFH, D], "b2": [NE, D],
    "Wr": [D, NE], "br": [NE],
    "cos8": [SL, 256], "sin8": [SL, 256], "ident": [128, 128],
    "supper": [128, 128], "lower": [128, 128], "sel": [32, 4],
    "pairK2": [128, 512], "caus2": [32, 8], "denbc2": [8, 32],
    "pairsel2": [32, 2048], "hsel": [32, 16], "csel": [32, 128], "wsel": [1, 8],
    "esel": [8, 1024],
}

INTER_SHAPES = {
    "xnT": ([128, NDT, T], BF16),
    "qtm": ([T, D], BF16),
    "ktm": ([T, D], BF16),
    "vtm": ([T, H * 65], BF16),
    "lbtm": ([T, 32], F32),
    "cl": ([B, 32, SL], F32),
    "sloc": ([B, NCH, 128, H * 65], BF16),
    "cntq": ([128, T], F32),
    "gat": ([128, 32], F32),
    "summ": ([128, 4256], F32),
    "summall": ([NCORES * 128, 4256], F32),
    "x1": ([T, D], F32),
    "xn1T": ([128, NDT, T], BF16),
    "s1tm": ([T, D], BF16),
    "g1tm": ([T, D], BF16),
    "yout": ([B, SL, D], F32),
}


def decl(nc, names, ext_in=(), ext_out=()):
    out = {}
    for n in names:
        if n in IN_SHAPES:
            out[n] = nc.dram_tensor(n, IN_SHAPES[n], F32, kind="ExternalInput").ap()
        else:
            shp, dt = INTER_SHAPES[n]
            k = "ExternalInput" if n in ext_in else (
                "ExternalOutput" if n in ext_out else "Internal")
            out[n] = nc.dram_tensor(n, shp, dt, kind=k).ap()
    return out


def emit_l1(nc, p, io):
    st = p.stack
    sb = lambda name, shape, dt: st.enter_context(nc.sbuf_tensor(p.tag + name, shape, dt))
    psm = lambda name, shape, dt=F32: st.enter_context(nc.psum_tensor(p.tag + name, shape, dt))

    xnT = sb("s_xnT", [128, NDT, T], BF16)
    wcat = sb("wcat", [128, NDT, 3104], BF16)
    cos8 = sb("cos8s", [128, NMT, 256], BF16)
    sin8 = sb("sin8s", [128, NMT, 256], BF16)
    identb = sb("identb", [128, 128], BF16)
    identf = sb("identf", [128, 128], F32)
    onesf = sb("onesf", [128, 128], F32)
    ones32 = sb("ones32", [32, SL], F32)
    bbc = sb("bbc", [128, 32], F32)
    cst = sb("cst", [128, 4], F32)

    ps_TF = psm("ps_TF", [128, 2, 512], F32)
    ps_A = psm("ps_A", [128, 2, 512], F32)

    t_idb = p.dma("gpsimd", identb[:, :], io["ident"], ch="cst")
    t_idf = p.dma("sync", identf[:, :], io["ident"], ch="cst2")
    t_cos = p.dma("gpsimd", cos8[:, :, :], io["cos8"].rearrange("(mt p) c -> p mt c", p=128), ch="cst")
    t_sin = p.dma("gpsimd", sin8[:, :, :], io["sin8"].rearrange("(mt p) c -> p mt c", p=128), ch="cst")
    for wname, c0 in (("wq", 0), ("wk", 1024), ("wv", 2048)):
        for dt in range(NDT):
            t_w = p.dma("gpsimd", wcat[:, dt, c0:c0 + 1024],
                        io[wname][dt * 128:(dt + 1) * 128, :], ch="cst")
    t_w = p.dma("gpsimd", wcat[:, :, 3072:3104],
                io["wbeta"].rearrange("(dt p) c -> p dt c", p=128), ch="cst")
    t_ones = p.op("vector", lambda v: v.memset(onesf[:, :], 1.0))
    p.opn("vector", lambda v: v.memset(cst[:, 0:1], LN_EPS))
    p.opn("vector", lambda v: v.memset(cst[:, 1:2], -0.5))
    p.opn("vector", lambda v: v.memset(cst[:, 2:3], 0.125))
    t_cst = p.op("vector", lambda v: v.memset(cst[:, 3:4], EPS_DN))
    t_o32 = p.op("vector", lambda v: v.memset(ones32[:, :], 1.0))
    bbrow = sb("bbrow", [1, 32], F32)
    t_bb = p.dma("sync", bbrow[:, :], io["bbeta"], ch="cst2")

    with ExitStack() as ph:
        s2 = lambda name, shape, dt: ph.enter_context(nc.sbuf_tensor(p.tag + name, shape, dt))
        ps_T = ph.enter_context(nc.psum_tensor(p.tag + "ps_T", [128, 2, 1024], BF16))
        xt = s2("xt", [128, 2, D], F32)
        xn16 = s2("xn16", [128, 2, D], BF16)
        stat = s2("stat", [128, 2, 16], F32)
        t_xfree = [None, None]
        t_x16free = [None, None]
        tok_cp = None
        for it in range(B * NMT):
            b, mt = divmod(it, NMT)
            bi = it % 2
            col0 = b * SL + mt * 128
            t_x = p.dma("sync", xt[:, bi, :], io["xs"][b, mt * 128:(mt + 1) * 128, :],
                        wait=t_xfree[bi], ch=f"xl{bi}")
            p.opn("vector", lambda v, bi=bi: v.bn_stats(stat[:, bi, 4:10], xt[:, bi, 0:512]),
                  wait=t_x)
            p.opn("vector", lambda v, bi=bi: v.bn_stats(stat[:, bi, 10:16], xt[:, bi, 512:1024]))
            t2 = p.op("vector", lambda v, bi=bi: v.bn_aggr(stat[:, bi, 0:2], stat[:, bi, 4:16]))
            p.opn("scalar", lambda s, bi=bi: s.activation(
                stat[:, bi, 2:3], stat[:, bi, 1:2], AF.Ln, bias=cst[:, 0:1]),
                wait=[t2, t_cst])
            t4 = p.op("scalar", lambda s, bi=bi: s.activation(
                stat[:, bi, 3:4], stat[:, bi, 2:3], AF.Exp, scale=cst[:, 1:2]))
            t5 = p.op("vector", lambda v, bi=bi: v.tensor_scalar(
                xn16[:, bi, :], xt[:, bi, :], stat[:, bi, 0:1], stat[:, bi, 3:4],
                ALU.subtract, ALU.mult), wait=[t4, t_x16free[bi]])
            t_xfree[bi] = t5
            ttr = None
            for dt in range(NDT):
                tb = dt % 2
                ttr = p.op("tensor", lambda t, bi=bi, dt=dt, tb=tb: t.transpose(
                    ps_T[:, tb, 0:128], xn16[:, bi, dt * 128:(dt + 1) * 128], identb[:, :]),
                    wait=[t5, t_idb] + ([tok_cp] if (it > 0 or dt >= 2) else []))
                tok_cp = p.op("scalar", lambda s, dt=dt, tb=tb, c=col0: s.copy(
                    xnT[:, dt, c:c + 128], ps_T[:, tb, 0:128]), wait=ttr)
            t_x16free[bi] = ttr
    t_xnT = tok_cp
    t_s0_end = [p.tok("tensor"), p.tok("scalar"), p.tok("vector")]

    for dt in range(NDT):
        p.dma("gpsimd", io["xnT"][:, dt, :], xnT[:, dt, :], wait=t_xnT, ch="sxnt")

    with ExitStack() as ph:
        s2 = lambda name, shape, dt: ph.enter_context(nc.sbuf_tensor(p.tag + name, shape, dt))
        wg = s2("wg", [128, NDT, 8], BF16)
        lE = s2("lE", [8, T], F32)
        maskE = s2("maskE", [8, T], F32)
        cntE = s2("cntE", [8, T], F32)
        ltm = s2("ltm", [128, 32, 8], F32)
        mask3 = s2("mask3", [128, 32, 8], F32)
        gw1 = s2("gw1", [128, 32], F32)
        gw2 = s2("gw2", [128, 32], F32)
        gw3 = s2("gw3", [128, 32], F32)
        gat = s2("s_gat", [128, 32], F32)
        ltot = s2("ltot", [8, 4], F32)

        t_wg = p.dma("gpsimd", wg[:, :, :], io["wgate"].rearrange("(dt p) c -> p dt c", p=128),
                     wait=t_s0_end, ch="cst")
        t_lE = None
        for i in range(8):
            pb = i % 2
            c0 = i * 512
            tokm = None
            for dt in range(NDT):
                tokm = p.op("tensor", lambda t, dt=dt, pb=pb, c0=c0: t.matmul(
                    ps_A[0:8, pb, :], wg[:, dt, :], xnT[:, dt, c0:c0 + 512],
                    start=(dt == 0), stop=(dt == NDT - 1)),
                    wait=([t_xnT, t_wg] + ([t_lE] if i >= 2 else [])) if dt == 0 else [])
            t_lE = p.op("vector", lambda v, pb=pb, c0=c0: v.tensor_copy(
                lE[:, c0:c0 + 512], ps_A[0:8, pb, :]), wait=tokm)
        t_ltm = None
        for c in range(32):
            tb = c % 2
            ttr = p.op("tensor", lambda t, c=c, tb=tb: t.transpose(
                ps_TF[:, tb, 0:8], lE[:, c * 128:(c + 1) * 128], identf[0:8, 0:8]),
                wait=[t_lE, t_idf] + ([t_ltm] if c >= 2 else []))
            t_ltm = p.op("scalar", lambda s, c=c, tb=tb: s.copy(
                ltm[:, c, :], ps_TF[:, tb, 0:8]), wait=ttr)
        p.opn("vector", lambda v: v.tensor_copy(gw1[:, :], ltm[:, :, 0]), wait=t_ltm)
        for e2 in range(1, NE):
            p.opn("vector", lambda v, e2=e2: v.tensor_max(gw1[:, :], gw1[:, :], ltm[:, :, e2]))
        p.opn("vector", lambda v: v.memset(gw2[:, :], 0.0))
        for e2 in range(NE):
            p.opn("vector", lambda v, e2=e2: v.tensor_tensor(
                gw3[:, :], ltm[:, :, e2], gw1[:, :], ALU.is_ge))
            p.opn("vector", lambda v, e2=e2: v.tensor_tensor(
                mask3[:, :, e2], gw3[:, :], gw2[:, :], ALU.mult))
            p.opn("vector", lambda v, e2=e2: v.tensor_tensor(
                mask3[:, :, e2], gw3[:, :], mask3[:, :, e2], ALU.subtract))
            p.opn("vector", lambda v, e2=e2: v.tensor_tensor(
                gw2[:, :], gw2[:, :], mask3[:, :, e2], ALU.add))
        for e2 in range(NE):
            p.opn("vector", lambda v, e2=e2: v.tensor_tensor(
                ltm[:, :, e2], ltm[:, :, e2], gw1[:, :], ALU.subtract))
        t_sub = p.op("vector", lambda v: v.tensor_copy(gw3[:, :], gw1[:, :]))
        t_exp = p.op("scalar", lambda s: s.activation(
            ltm[:, :, :], ltm[:, :, :], AF.Exp), wait=t_sub)
        p.opn("vector", lambda v: v.tensor_reduce(gw3[:, :], ltm[:, :, :], AXX, ALU.add),
              wait=t_exp)
        t_gat = p.op("vector", lambda v: v.reciprocal(gat[:, :], gw3[:, :]))
        p.dma("gpsimd", io["gat"], gat[:, :], wait=t_gat, ch="sgat")
        t_mE = None
        for c in range(32):
            tb = c % 2
            ttr = p.op("tensor", lambda t, c=c, tb=tb: t.transpose(
                ps_TF[0:8, tb, 0:128], mask3[:, c, :], identf[:, :]),
                wait=[t_gat] + ([t_mE] if c >= 2 else []))
            t_mE = p.op("vector", lambda v, c=c, tb=tb: v.tensor_copy(
                maskE[:, c * 128:(c + 1) * 128], ps_TF[0:8, tb, 0:128]), wait=ttr)
        for b in range(B):
            p.opn("vector", lambda v, b=b: v.tensor_tensor_scan(
                cntE[:, b * SL:(b + 1) * SL], ones32[0:8, :],
                maskE[:, b * SL:(b + 1) * SL], 0.0, ALU.mult, ALU.add),
                wait=[t_mE, t_o32] if b == 0 else [])
            p.opn("vector", lambda v, b=b: v.tensor_reduce(
                ltot[:, b:b + 1], maskE[:, b * SL:(b + 1) * SL], AXX, ALU.add))
        t_cnt = p.op("vector", lambda v: v.tensor_copy(gw3[0:8, 0:4], ltot[:, :]))
        p.dma("gpsimd", io["cntq"][0:8, :], lE[:, :], wait=t_cnt, ch="sgat")
        p.dma("gpsimd", io["cntq"][32:40, :], maskE[:, :], wait=t_cnt, ch="sgat")
        t_g_sp = p.dma("gpsimd", io["cntq"][64:72, :], cntE[:, :], wait=t_cnt, ch="sgat")
        p.dma("gpsimd", io["summ"][0:8, 4224:4228], ltot[:, :], wait=t_cnt)
    t_gate_v = p.tok("vector")

    ps_S = psm("ps_S", [64, 2, 512], F32)
    ps_B = psm("ps_B", [128, 512], F32)
    qtm = sb("s_qtm", [128, NMT, D], BF16)
    ktm = sb("s_ktm", [128, NMT, D], BF16)
    vtm = sb("s_vtm", [128, NMT, H * 65], BF16)
    lbT = sb("s_lbT", [128, NMT, 32], F32)
    lbe = sb("lbe", [32, SL], F32)
    cl = sb("s_cl", [32, SL], F32)
    Sst = sb("Sst", [64, 2, 2, H, 65], F32)
    at32 = sb("at32", [32, 4, 128], F32)
    atm = sb("atm", [128, 32], F32)
    scr32 = sb("scr32", [32, 2, 32], F32)
    albc = sb("albc", [64, 32], F32)
    ptbc = sb("ptbc", [64, 2, 16], F32)
    kst = sb("kst", [128, 2, 64], BF16)
    rtmp = sb("rtmp", [128, 4, 256], BF16)
    btmp = sb("btmp", [128, 2, 32], F32)

    t_m1 = p.op("tensor", lambda t: t.matmul(
        ps_B[:, 0:32], onesf[0:1, :], bbrow[:, :], start=True, stop=True),
        wait=[t_bb, t_ones, t_gate_v])
    t_bbc = p.op("vector", lambda v: v.tensor_copy(bbc[:, :], ps_B[:, 0:32]), wait=t_m1)
    p.opn("vector", lambda v: v.memset(scr32[:, :, :], 0.0), wait=t_g_sp)
    t_vones = p.op("vector", lambda v: v.memset(
        vtm[:, :, :].rearrange("p mt (h q) -> p (mt h) q", q=65)[:, :, 64:65], 1.0),
        wait=t_g_sp)

    t_ech = [t_gate_v, t_gate_v]
    t_bprev = [t_g_sp]
    t_pb_prev = None
    for b in range(B):
        t_lbT = None
        for mt in range(NMT):
            col0 = b * SL + mt * 128
            for ech in range(7):
                i = mt * 7 + ech
                pb = i % 2
                ew = 512 if ech < 6 else 32
                e0 = ech * 512
                tokm = None
                for dt in range(NDT):
                    tokm = p.op("tensor", lambda t, dt=dt, pb=pb, e0=e0, ew=ew, c=col0: t.matmul(
                        ps_A[:, pb, 0:ew], xnT[:, dt, c:c + 128], wcat[:, dt, e0:e0 + ew],
                        start=(dt == 0), stop=(dt == NDT - 1)),
                        wait=([t_xnT, t_w, t_ech[pb]] if dt == 0 else []))
                wext = t_bprev if mt == 0 else []
                if ech < 2:
                    t_ech[pb] = p.op("vector", lambda v, pb=pb, mt=mt, e0=e0: v.tensor_copy(
                        qtm[:, mt, e0:e0 + 512], ps_A[:, pb, 0:512]), wait=[tokm] + wext)
                elif ech < 4:
                    t_ech[pb] = p.op("vector", lambda v, pb=pb, mt=mt, e0=e0: v.tensor_copy(
                        ktm[:, mt, e0 - 1024:e0 - 1024 + 512], ps_A[:, pb, 0:512]),
                        wait=[tokm] + wext)
                elif ech < 6:
                    h0 = (ech - 4) * 8
                    t_ech[pb] = p.op("vector", lambda v, pb=pb, mt=mt, h0=h0: v.tensor_copy(
                        vtm[:, mt, :].rearrange("p (h q) -> p h q", q=65)[:, h0:h0 + 8, 0:64],
                        ps_A[:, pb, 0:512].rearrange("p (h q) -> p h q", q=64)),
                        wait=[tokm, t_vones] + wext)
                else:
                    tb0 = p.op("vector", lambda v, pb=pb: v.tensor_tensor(
                        btmp[:, 0, :], ps_A[:, pb, 0:32], bbc[:, :], ALU.add),
                        wait=[tokm, t_bbc] + wext)
                    tb1 = p.op("scalar", lambda s: s.activation(
                        btmp[:, 1, :], btmp[:, 0, :], AF.Sigmoid), wait=tb0)
                    tb2 = p.op("vector", lambda v: v.tensor_scalar(
                        btmp[:, 0, :], btmp[:, 1, :], 0.999, 0.001, ALU.min, ALU.max),
                        wait=tb1)
                    t_lbT = p.op("scalar", lambda s, mt=mt: s.activation(
                        lbT[:, mt, :], btmp[:, 0, :], AF.Ln), wait=tb2)
                    t_ech[pb] = t_lbT
        t_rope = None
        first = True
        for mt in range(NMT):
            for src in (qtm, ktm):
                for half in range(2):
                    hs = half * 8
                    qv = src[:, mt, :].rearrange("p (h q) -> p h q", q=64)[:, hs:hs + 8, :]
                    q1 = qv[:, :, 0:32]
                    q2 = qv[:, :, 32:64]
                    cv = cos8[:, mt, :].rearrange("p (h q) -> p h q", q=32)
                    sv = sin8[:, mt, :].rearrange("p (h q) -> p h q", q=32)
                    r0 = rtmp[:, 0, :].rearrange("p (h q) -> p h q", q=32)
                    r1 = rtmp[:, 1, :].rearrange("p (h q) -> p h q", q=32)
                    r2 = rtmp[:, 2, :].rearrange("p (h q) -> p h q", q=32)
                    r3 = rtmp[:, 3, :].rearrange("p (h q) -> p h q", q=32)
                    w0 = [t_cos, t_sin, t_lbT] if first else []
                    first = False
                    p.opn("vector", lambda v, a=r0, x=q1, y=cv: v.tensor_mul(a, x, y), wait=w0)
                    p.opn("vector", lambda v, a=r1, x=q2, y=sv: v.tensor_mul(a, x, y))
                    p.opn("vector", lambda v, a=r2, x=q1, y=sv: v.tensor_mul(a, x, y))
                    p.opn("vector", lambda v, a=r3, x=q2, y=cv: v.tensor_mul(a, x, y))
                    p.opn("vector", lambda v, a=q1, x=r0, y=r1: v.tensor_sub(a, x, y))
                    t_rope = p.op("vector", lambda v, a=q2, x=r2, y=r3: v.tensor_add(a, x, y))
            for src in (qtm, ktm):
                uu = src[:, mt, :]
                ft = rtmp[:, :, :].rearrange("p a c -> p (a c)")
                p.opn("vector", lambda v, u=uu, ft=ft: v.tensor_scalar(
                    ft, u, 0.0, None, ALU.min))
                tmn = p.op("vector", lambda v, u=uu: v.tensor_scalar(u, u, 0.0, None, ALU.max))
                tex = p.op("scalar", lambda s, ft=ft: s.activation(ft, ft, AF.Exp), wait=tmn)
                t_rope = p.op("vector", lambda v, u=uu, ft=ft: v.tensor_tensor(
                    u, u, ft, ALU.add), wait=tex)
        t_qspill = p.dma("gpsimd",
                         io["qtm"][b * SL:(b + 1) * SL, :].rearrange("(mt p) e -> p mt e", p=128),
                         qtm[:, :, :], wait=t_rope, ch="sq")
        t_lbe = None
        for mt in range(NMT):
            tb = mt % 2
            ttr = p.op("tensor", lambda t, mt=mt, tb=tb: t.transpose(
                ps_TF[0:32, tb, 0:128], lbT[:, mt, :], identf[:, :]),
                wait=[t_lbT, t_idf] + ([t_lbe] if mt >= 2 else []) + (t_bprev if mt < 2 else []))
            t_lbe = p.op("vector", lambda v, mt=mt, tb=tb: v.tensor_copy(
                lbe[:, mt * 128:(mt + 1) * 128], ps_TF[0:32, tb, 0:128]), wait=ttr)
        t_cl = p.op("vector", lambda v: v.tensor_tensor_scan(
            cl[:, :], ones32[:, :], lbe[:, :], 0.0, ALU.mult, ALU.add), wait=t_lbe)
        p.dma("gpsimd", io["cl"][b, :, :], cl[:, :], wait=t_cl, ch="scl")
        p.dma("gpsimd",
              io["lbtm"][b * SL:(b + 1) * SL, :].rearrange("(mt p) e -> p mt e", p=128),
              lbT[:, :, :], wait=t_lbT, ch="scl")
        t_s0 = p.op("vector", lambda v: v.memset(Sst[:, 0, :, :, :], 0.0), wait=t_bprev)
        t_supd = t_s0
        t_al = t_pb_prev if b > 0 else t_bbc
        supd_hist = []
        t_kst_user = None
        t_atm_user = None
        for ch in range(NCH):
            pg = ch % 2
            c0 = ch * 128
            if ch == 0:
                t_cll = p.op("vector", lambda v, c0=c0: v.tensor_copy(
                    at32[:, 0, :], cl[:, c0:c0 + 128]), wait=t_cl)
            else:
                t_cll = p.op("vector", lambda v, c0=c0: v.tensor_scalar(
                    at32[:, 0, :], cl[:, c0:c0 + 128], cl[:, c0 - 1:c0], None,
                    ALU.subtract))
            t_ae = p.op("scalar", lambda s: s.activation(
                at32[:, 2, :], at32[:, 0, :], AF.Exp), wait=t_cll)
            t_tl = p.op("vector", lambda v: v.tensor_scalar(
                at32[:, 1, :], at32[:, 0, :], -1.0, at32[:, 0, 127:128], ALU.mult, ALU.add))
            t_at = p.op("scalar", lambda s: s.activation(
                at32[:, 3, :], at32[:, 1, :], AF.Exp), wait=t_tl)
            ttr = p.op("tensor", lambda t: t.transpose(
                ps_TF[:, 0, 0:32], at32[:, 3, :], identf[0:32, 0:32]),
                wait=[t_at, t_atm_user, t_lbe])
            t_atm = p.op("vector", lambda v: v.tensor_copy(atm[:, :], ps_TF[:, 0, 0:32]),
                         wait=ttr)
            t_c1 = p.op("vector", lambda v: v.tensor_copy(scr32[:, 0, 0:1], at32[:, 2, 127:128]),
                        wait=t_ae)
            t_c2 = p.op("vector", lambda v: v.transpose(scr32[:, 1, :], scr32[:, 0, :]),
                        wait=t_c1)
            t_c3 = p.op("tensor", lambda t: t.matmul(
                ps_B[0:64, 0:32], onesf[0:1, 0:64], scr32[0:1, 1, :], start=True, stop=True),
                wait=[t_c2, t_ones, t_al])
            t_al = p.op("vector", lambda v: v.tensor_copy(albc[:, :], ps_B[0:64, 0:32]),
                        wait=t_c3)
            t_sp = p.dma("gpsimd", io["sloc"][b, ch, 0:64, :],
                         Sst[:, pg, 0, :, :].rearrange("p h q -> p (h q)"), wait=t_supd,
                         ch="ssl")
            t_sp = p.dma("gpsimd", io["sloc"][b, ch, 64:128, :],
                         Sst[:, pg, 1, :, :].rearrange("p h q -> p (h q)"), wait=t_supd,
                         ch="ssl")
            for h in range(H):
                t_k0 = p.op("vector", lambda v, h=h, ch=ch: v.tensor_scalar(
                    kst[:, 0, :], ktm[:, ch, h * 64:(h + 1) * 64], atm[:, 2 * h:2 * h + 1],
                    None, ALU.mult), wait=[t_atm, t_kst_user])
                t_k1 = p.op("vector", lambda v, h=h, ch=ch: v.tensor_scalar(
                    kst[:, 1, :], ktm[:, ch, h * 64:(h + 1) * 64], atm[:, 2 * h + 1:2 * h + 2],
                    None, ALU.mult))
                for c2 in range(2):
                    pb2 = (h * 2 + c2) % 2
                    wmm = [t_k1, t_sp]
                    if len(supd_hist) >= 2:
                        wmm.append(supd_hist[-2])
                    tokm = p.op("tensor", lambda t, h=h, c2=c2, pb2=pb2, ch=ch: t.matmul(
                        ps_S[:, pb2, 0:65], kst[:, c2, :],
                        vtm[:, ch, h * 65:(h + 1) * 65], start=True, stop=True), wait=wmm)
                    t_kst_user = tokm
                    t_supd = p.op("vector", lambda v, h=h, c2=c2, pb2=pb2, pg=pg: v.scalar_tensor_tensor(
                        Sst[:, 1 - pg, c2, h, :], Sst[:, pg, c2, h, :],
                        albc[:, 2 * h + c2:2 * h + c2 + 1],
                        ps_S[:, pb2, 0:65], ALU.mult, ALU.add), wait=[tokm, t_al])
                    supd_hist.append(t_supd)
            t_atm_user = t_kst_user
        fin = NCH % 2
        t_pe = p.op("scalar", lambda s: s.activation(
            scr32[:, 0, 0:1], cl[:, SL - 1:SL], AF.Exp), wait=[t_cl, t_c2])
        t_pt = p.op("vector", lambda v: v.transpose(scr32[:, 1, :], scr32[:, 0, :]), wait=t_pe)
        t_pm0 = p.op("tensor", lambda t: t.matmul(
            ps_B[0:64, 0:16], onesf[0:1, 0:64],
            scr32[0:1, 1, :].rearrange("o (h c) -> o h c", c=2)[:, :, 0],
            start=True, stop=True), wait=[t_pt, t_ones, t_al])
        t_pb0 = p.op("vector", lambda v: v.tensor_copy(ptbc[:, 0, :], ps_B[0:64, 0:16]),
                     wait=t_pm0)
        t_pm1 = p.op("tensor", lambda t: t.matmul(
            ps_B[0:64, 0:16], onesf[0:1, 0:64],
            scr32[0:1, 1, :].rearrange("o (h c) -> o h c", c=2)[:, :, 1],
            start=True, stop=True), wait=t_pb0)
        t_pb1 = p.op("vector", lambda v: v.tensor_copy(ptbc[:, 1, :], ps_B[0:64, 0:16]),
                     wait=t_pm1)
        t_sd1 = p.dma("gpsimd", io["summ"][0:64, b * 1040:(b + 1) * 1040],
                      Sst[:, fin, 0, :, :].rearrange("p h q -> p (h q)"), wait=t_supd, ch="ssm")
        t_sd1b = p.dma("gpsimd", io["summ"][64:128, b * 1040:(b + 1) * 1040],
                       Sst[:, fin, 1, :, :].rearrange("p h q -> p (h q)"), wait=t_supd, ch="ssm")
        t_sd2 = p.dma("gpsimd", io["summ"][0:64, 4160 + b * 16:4160 + (b + 1) * 16],
                      ptbc[:, 0, :], wait=[t_pb0, t_pb1], ch="ssm")
        t_sd2b = p.dma("gpsimd", io["summ"][64:128, 4160 + b * 16:4160 + (b + 1) * 16],
                       ptbc[:, 1, :], wait=[t_pb0, t_pb1], ch="ssm")
        t_kspill = p.dma("gpsimd",
                         io["ktm"][b * SL:(b + 1) * SL, :].rearrange("(mt p) e -> p mt e", p=128),
                         ktm[:, :, :], wait=t_supd, ch="sq")
        t_vspill = p.dma("gpsimd",
                         io["vtm"][b * SL:(b + 1) * SL, :].rearrange("(mt p) e -> p mt e", p=128),
                         vtm[:, :, :], wait=t_supd, ch="sq")
        t_bprev = [t_qspill, t_kspill, t_vspill, t_sd1, t_sd1b, t_sd2, t_sd2b, t_kst_user]
        t_pb_prev = t_pb1


import os as _os
RACE_DETECT = bool(int(_os.environ.get("RACE_DETECT", "0")))
L2_TRUNC = int(_os.environ.get("L2_TRUNC", "9"))


def build_l1():
    nc = bass.Bass("TRN2", target_bir_lowering=False, detect_race_conditions=RACE_DETECT)
    io = decl(nc, ["xs", "wq", "wk", "wv", "wbeta", "bbeta", "wgate",
                   "cos8", "sin8", "ident",
                   "xnT", "qtm", "ktm", "vtm", "lbtm", "cl", "sloc",
                   "cntq", "gat", "summ"],
              ext_out=["xnT", "qtm", "ktm", "vtm", "lbtm", "cl", "sloc",
                       "cntq", "gat", "summ"])
    with ExitStack() as stack:
        p = Prog(nc, stack)
        blk = stack.enter_context(nc.Block())
        emit_l1(nc, p, io)
        p.final_waits()
        p.emit(blk)
    return nc




def emit_l2(nc, p, io):
    st = p.stack
    sb = lambda name, shape, dt: st.enter_context(nc.sbuf_tensor(p.tag + name, shape, dt))
    psm = lambda name, shape, dt=F32: st.enter_context(nc.psum_tensor(p.tag + name, shape, dt))

    xnT = sb("s_xnT", [128, NDT, SL], BF16)
    identb = sb("identb", [128, 128], BF16)
    identf = sb("identf", [128, 128], F32)
    onesf = sb("onesf", [128, 128], F32)
    lowerf = sb("lowerf", [128, 128], F32)
    lower2 = sb("lower2", [128, 2, 128], F32)
    supper16 = sb("supper16", [128, 128], BF16)
    hselc = sb("hselc", [32, 16], F32)
    cselc = sb("cselc", [32, 128], F32)
    cst = sb("cst", [128, 4], F32)
    wo16 = sb("wo16", [128, NDT, D], BF16)
    wexp16 = sb("wexp16", [128, NDT, D], BF16)
    bo8 = sb("bo8", [128, NDT], F32)
    gam8 = sb("gam8", [128, NDT], F32)
    lnb8 = sb("lnb8", [128, NDT], F32)
    bexbc = sb("bexbc", [128, D], F32)
    gatl = sb("gatl", [128, 32], F32)
    kgate = sb("kgate", [128, 32], F32)
    sin16 = sb("sin16", [128, B, H, 65], BF16)
    onesc = sb("onesc", [128, 1], F32)
    onescb = sb("onescb", [128, 1], BF16)

    ps_tr = psm("ps_tr", [128, 2, 1024], BF16)
    ps_tr3 = psm("ps_tr3", [128, 1024], BF16)
    ps_w = psm("ps_w", [128, 2, 512], F32)
    ps_y = psm("ps_y", [128, 512], F32)
    ps_ln = psm("ps_ln", [128, 2, 512], F32)

    t_idb = p.dma("gpsimd", identb[:, :], io["ident"], ch="cst")
    t_idf = p.dma("sync", identf[:, :], io["ident"], ch="cst2")
    t_low = p.dma("sync", lowerf[:, :], io["lower"], ch="cst2")
    for c2 in range(2):
        t_lo2 = p.dma("sync", lower2[:, c2, :], io["lower"][:, 0:128], ch="cst2")
    t_sup = p.dma("gpsimd", supper16[:, :], io["supper"], ch="cst")
    t_hs = p.dma("sync", hselc[:, :], io["hsel"], ch="cst2")
    t_cs = p.dma("gpsimd", cselc[:, :], io["csel"], ch="cst")
    for dt in range(NDT):
        t_wo = p.dma("gpsimd", wo16[:, dt, :], io["wo"][dt * 128:(dt + 1) * 128, :], ch="cst")
        t_wx = p.dma("gpsimd", wexp16[:, dt, :], io["wexp"][dt * 128:(dt + 1) * 128, :], ch="cst")
    t_bo = p.dma("sync", bo8[:, :], io["bo"].rearrange("(dt p) -> p dt", p=128), allow_slow_non_contiguous=True, ch="cst2")
    t_gm = p.dma("sync", gam8[:, :], io["gam"].rearrange("(dt p) -> p dt", p=128), allow_slow_non_contiguous=True, ch="cst2")
    t_lb = p.dma("sync", lnb8[:, :], io["lnb"].rearrange("(dt p) -> p dt", p=128), allow_slow_non_contiguous=True, ch="cst2")
    t_ga = p.dma("sync", gatl[:, :], io["gat"], ch="cst2")
    t_one = p.op("vector", lambda v: v.memset(onesf[:, :], 1.0))
    t_onc = p.op("vector", lambda v: v.memset(onesc[:, :], 1.0))
    t_oncb = p.op("vector", lambda v: v.memset(onescb[:, :], 1.0))
    p.op("vector", lambda v: v.memset(cst[:, 0:1], LN_EPS))
    p.op("vector", lambda v: v.memset(cst[:, 1:2], -0.5))
    p.op("vector", lambda v: v.memset(cst[:, 2:3], 0.125))
    t_cst = p.op("vector", lambda v: v.memset(cst[:, 3:4], EPS_DN))
    bexr = sb("bexr", [1, D], F32)
    t_bx = p.dma("sync", bexr[:, :], io["bexp"], ch="cst2")
    for half in range(2):
        t_bm = p.op("tensor", lambda t, half=half: t.matmul(
            ps_ln[:, half, :], onesf[0:1, :], bexr[0:1, half * 512:(half + 1) * 512],
            start=True, stop=True), wait=[t_bx, t_one])
        t_bxc = p.op("vector", lambda v, half=half: v.tensor_copy(
            bexbc[:, half * 512:(half + 1) * 512], ps_ln[:, half, :]), wait=t_bm)

    with ExitStack() as ph:
        s2 = lambda name, shape, dt: ph.enter_context(nc.sbuf_tensor(p.tag + name, shape, dt))
        prj = s2("prj", [128, 1, 4256], F32)
        acc = s2("acc", [128, B, H, 65], F32)
        wbc = s2("wbc", [128, 8], F32)
        w1bc = s2("w1bc", [128, 8], F32)
        ptt = s2("ptt", [128, 2, 16], F32)
        wrow = s2("wrow", [1, 8], F32)
        cnta = s2("cnta", [32, 8], F32)
        cnt8 = s2("cnt8", [8, 32], F32)
        selc = s2("selc", [32, 4], F32)
        offs = s2("offs", [8, 8], F32)
        keepE = s2("keepE", [8, T], F32)
        cntE = s2("cntE", [8, T], F32)
        maskE = s2("maskE", [8, T], F32)
        kw = s2("kw", [8, SL], F32)
        kt3 = s2("kt3", [128, 32, 8], F32)

        t_wr = p.dma("sync", wrow[:, :], io["wsel"], ch="cst2")
        t_wm = p.op("tensor", lambda t: t.matmul(
            ps_ln[:, 0, 0:8], onesf[0:1, :], wrow[:, :], start=True, stop=True),
            wait=[t_wr, t_one, t_bxc])
        t_wb = p.op("vector", lambda v: v.tensor_copy(wbc[:, :], ps_ln[:, 0, 0:8]), wait=t_wm)
        t_w1 = p.op("vector", lambda v: v.tensor_scalar(
            w1bc[:, :], wbc[:, :], -1.0, 1.0, ALU.mult, ALU.add))
        t_acc0 = p.op("vector", lambda v: v.memset(acc[:, :, :, :], 0.0))
        t_prfree = [None]
        for j in range(NCORES - 1):
            bj = 0
            t_pr = p.dma("gpsimd", prj[:, bj, :], io["summall"][j * 128:(j + 1) * 128, :],
                         wait=t_prfree[bj], ch="cpr")
            for b in range(B):
                t_p1 = p.op("vector", lambda v, bj=bj, b=b, j=j: v.tensor_scalar(
                    ptt[:, bj, :], prj[:, bj, 4160 + b * 16:4160 + (b + 1) * 16],
                    wbc[:, j:j + 1], w1bc[:, j:j + 1], ALU.mult, ALU.add),
                    wait=[t_pr, t_w1, t_acc0])
                t_p2 = p.op("vector", lambda v, bj=bj, b=b: v.tensor_tensor(
                    acc[:, b, :, :], acc[:, b, :, :],
                    ptt[:, bj, :].rearrange("p (h q) -> p h q", q=1).broadcast_to([128, 16, 65]),
                    ALU.mult))
                t_p3 = p.op("vector", lambda v, bj=bj, b=b, j=j: v.scalar_tensor_tensor(
                    acc[:, b, :, :],
                    prj[:, bj, b * 1040:(b + 1) * 1040].rearrange("p (h q) -> p h q", q=65),
                    wbc[:, j:j + 1], acc[:, b, :, :], ALU.mult, ALU.add))
            t_prfree[0] = t_p3
        t_s16 = p.op("scalar", lambda s: s.copy(sin16[:, :, :, :], acc[:, :, :, :]),
                     wait=t_p3 if NCORES > 1 else t_acc0)
        t_c8 = p.dma("gpsimd", cnt8[:, :].rearrange("p (j b) -> p j b", b=4),
                     io["summall"][:, 4224:4228].rearrange("(j p) b -> p j b", p=128)[0:8, :, :],
                     allow_slow_non_contiguous=True, ch="cpr")
        t_ct = p.op("tensor", lambda t: t.transpose(
            ps_w[0:32, 0, 0:8], cnt8[:, :], identf[0:8, 0:8]), wait=[t_c8, t_idf])
        t_ca = p.op("vector", lambda v: v.tensor_copy(cnta[:, :], ps_w[0:32, 0, 0:8]),
                    wait=t_ct)
        t_se = p.dma("sync", selc[:, :], io["sel"], ch="cst2")
        t_om = p.op("tensor", lambda t: t.matmul(
            ps_ln[0:8, 1, 0:4], cnta[:, :], selc[:, :], start=True, stop=True),
            wait=[t_ca, t_se, t_wb])
        t_of = p.op("vector", lambda v: v.tensor_scalar(
            offs[:, 0:4], ps_ln[0:8, 1, 0:4], -1.0, None, ALU.add), wait=t_om)
        t_ma = p.dma("sync", maskE[:, :], io["cntq"][32:40, :], ch="cst2")
        t_cn = p.dma("sync", cntE[:, :], io["cntq"][64:72, :], ch="cst2")
        t_ke = None
        for b in range(B):
            p.op("vector", lambda v, b=b: v.tensor_scalar(
                kw[:, :], cntE[:, b * SL:(b + 1) * SL], offs[:, b:b + 1], None, ALU.add),
                wait=[t_cn, t_of])
            p.op("vector", lambda v: v.tensor_single_scalar(kw[:, :], kw[:, :], 100.0, ALU.is_lt))
            t_ke = p.op("vector", lambda v, b=b: v.tensor_tensor(
                keepE[:, b * SL:(b + 1) * SL], kw[:, :], maskE[:, b * SL:(b + 1) * SL],
                ALU.mult), wait=t_ma)
        t_kc = None
        for c in range(32):
            tb = c % 2
            ttr = p.op("tensor", lambda t, c=c, tb=tb: t.transpose(
                ps_w[:, tb, 0:8], keepE[:, c * 128:(c + 1) * 128], identf[0:8, 0:8]),
                wait=[t_ke, t_idf] + ([t_kc] if c >= 2 else []))
            t_kc = p.op("scalar", lambda s, c=c, tb=tb: s.copy(
                kt3[:, c, :], ps_w[:, tb, 0:8]), wait=ttr)
        t_ks = p.op("vector", lambda v: v.tensor_reduce(kgate[:, :], kt3[:, :, :], AXX, ALU.add),
                    wait=t_kc)
        t_kg = p.op("vector", lambda v: v.tensor_tensor(
            kgate[:, :], kgate[:, :], gatl[:, :], ALU.mult), wait=t_ga)
    t_stageA = [p.tok("vector"), p.tok("scalar"), p.tok("tensor"), t_kg, t_s16]

    qb = sb("qb", [128, NMT, D], BF16)
    kb = sb("kb", [128, NMT, D], BF16)
    vb = sb("vb", [128, NMT, H * 65], BF16)
    lbt = sb("lbt", [128, NMT, 32], F32)
    clb = sb("clb", [32, SL], F32)
    slc = sb("slc", [128, 1, H, 65], BF16)
    sgc = sb("sgc", [128, H, 65], BF16)
    plb = sb("plb", [128, 2, 16], F32)
    hw = sb("hw", [32, 16], F32)
    at32 = sb("at32", [32, 3, 128], F32)
    atm = sb("atm", [128, 32], F32)
    qkTb = sb("qkTb", [64, 2, 2, 128], BF16)
    qs2b = sb("qs2b", [128, 2, 128], BF16)
    qsTb = sb("qsTb", [128, 2, 128], BF16)
    mscb = sb("mscb", [128, 2, 128], BF16)
    rdb = sb("rdb", [128, 2, 2, 128], BF16)
    eeb = sb("eeb", [128, 2, 2, 128], BF16)
    dsb = sb("dsb", [128, 2, 128], BF16)
    rcolb = sb("rcolb", [128, 2, 1], F32)
    dcolb = sb("dcolb", [128, 2, 1], F32)
    ytm = sb("ytm", [128, D], BF16)
    yT = sb("yT", [128, NDT, 128], BF16)
    h0 = sb("h0", [128, NDT, 128], F32)
    lnr = sb("lnr", [1, 5, 128], F32)
    s16 = sb("s16", [128, NDT, 128], BF16)
    stm = sb("stm", [128, D], BF16)
    gtm = sb("gtm", [128, D], BF16)
    ge1 = sb("ge1", [128, D], BF16)
    xtl = sb("xtl", [128, D], F32)
    x1t = sb("x1t", [128, D], F32)
    xs16 = sb("xs16", [128, D], BF16)
    xst = sb("xst", [128, 1, 16], F32)
    x1T = sb("x1T", [128, NDT, 128], BF16)

    t_bload = [None]
    t_chend = [None]
    t_x1sp = None
    t_xnsp = None
    u_b0 = None
    u_b1 = None
    u_b2 = None
    u_b4 = None
    u_y5 = None
    u_qT = [None, None]
    u_kT = [None, None]
    u_qs2 = [None, None]
    u_qsT = [None, None]
    u_rd = [None, None]
    u_ee = [None, None]
    u_ds = [None, None]
    u_y = [None, None]
    for b in range(B):
        w_b = t_stageA if b == 0 else t_chend
        t_q = p.dma("gpsimd", qb[:, :, :],
                    io["qtm"][b * SL:(b + 1) * SL, :].rearrange("(mt p) e -> p mt e", p=128),
                    wait=w_b, ch="cq")
        t_k = p.dma("gpsimd", kb[:, :, :],
                    io["ktm"][b * SL:(b + 1) * SL, :].rearrange("(mt p) e -> p mt e", p=128),
                    wait=w_b, ch="ck")
        t_v = p.dma("gpsimd", vb[:, :, :],
                    io["vtm"][b * SL:(b + 1) * SL, :].rearrange("(mt p) e -> p mt e", p=128),
                    wait=w_b, ch="cv")
        t_lt = p.dma("gpsimd", lbt[:, :, :],
                     io["lbtm"][b * SL:(b + 1) * SL, :].rearrange("(mt p) e -> p mt e", p=128),
                     wait=w_b, ch="cq")
        t_cl = p.dma("sync", clb[:, :], io["cl"][b, :, :], wait=w_b, ch="ccl")
        t_xnl = None
        for dt in range(NDT):
            t_xnl = p.dma("gpsimd", xnT[:, dt, :], io["xnT"][:, dt, b * SL:(b + 1) * SL],
                          wait=w_b if dt == 0 else [], ch="cxn")
        for ch in range(NCH if L2_TRUNC >= 1 else 0):
            tt = b * NMT + ch
            col0 = b * SL + ch * 128
            sli = 0
            t_sl = p.dma("gpsimd", slc[:, sli, :, :].rearrange("p h q -> p (h q)"),
                         io["sloc"][b, ch, :, :], wait=t_chend, ch="csl")
            if ch == 0:
                t_pl = p.op("vector", lambda v: v.memset(plb[:, 0, :], 1.0), wait=t_chend)
                pli = 0
            else:
                pli = ch % 2
                t_pe = p.op("scalar", lambda s, ch=ch: s.activation(
                    at32[:, 0, 0:1], clb[:, ch * 128 - 1:ch * 128], AF.Exp),
                    wait=[t_cl] + t_chend)
                t_hw = p.op("vector", lambda v: v.tensor_scalar(
                    hw[:, :], hselc[:, :], at32[:, 0, 0:1], None, ALU.mult),
                    wait=[t_pe, t_hs])
                t_plm = p.op("tensor", lambda t, pli=pli: t.matmul(
                    ps_ln[:, pli, 0:16], cselc[:, :], hw[:, :], start=True, stop=True),
                    wait=[t_hw, t_cs] + t_chend)
                t_pl = p.op("vector", lambda v, pli=pli: v.tensor_copy(
                    plb[:, pli, :], ps_ln[:, pli, 0:16]), wait=t_plm)
            t_sg1 = p.op("vector", lambda v, pli=pli, b=b: v.tensor_tensor(
                sgc[:, :, :], sin16[:, b, :, :],
                plb[:, pli, :].rearrange("p (h q) -> p h q", q=1).broadcast_to([128, 16, 65]),
                ALU.mult), wait=[t_s16, t_pl])
            t_sg = p.op("vector", lambda v, sli=sli: v.tensor_tensor(
                sgc[:, :, :], sgc[:, :, :], slc[:, sli, :, :], ALU.add), wait=t_sl)
            if ch == 0:
                t_cll = p.op("vector", lambda v: v.tensor_copy(at32[:, 1, :], clb[:, 0:128]),
                             wait=[t_cl] + t_chend)
            else:
                t_cll = p.op("vector", lambda v, ch=ch: v.tensor_scalar(
                    at32[:, 1, :], clb[:, ch * 128:(ch + 1) * 128],
                    clb[:, ch * 128 - 1:ch * 128], None, ALU.subtract), wait=[t_cl] + t_chend)
            t_ae = p.op("scalar", lambda s: s.activation(
                at32[:, 2, :], at32[:, 1, :], AF.Exp), wait=t_cll)
            ttr = p.op("tensor", lambda t: t.transpose(
                ps_w[:, 0, 0:32], at32[:, 2, :], identf[0:32, 0:32]),
                wait=[t_ae, t_idf] + t_chend)
            t_atm = p.op("vector", lambda v: v.tensor_copy(atm[:, :], ps_w[:, 0, 0:32]),
                         wait=ttr)
            t_hlast = None
            u_b3 = t_atm
            NH = H if L2_TRUNC >= 2 else 0
            pp = {}
            for i in range(NH + 1 if NH else 0):
                if i < NH:
                    h = i
                    hb = h % 2
                    ent = t_chend if h < 2 else []
                    trd = p.op("vector", lambda v, h=h, ch=ch, hb=hb: v.tensor_tensor(
                        rdb[:, hb, :, :], lower2[:, :, :],
                        lbt[:, ch, 2 * h:2 * h + 2].rearrange("p (a q) -> p a q", q=1)
                        .broadcast_to([128, 2, 128]), ALU.mult),
                        wait=[t_lt, t_lo2, u_rd[hb]])
                    tqs = p.op("vector", lambda v, h=h, ch=ch, hb=hb: v.tensor_tensor(
                        qs2b[:, hb, :].rearrange("p (a q) -> p a q", q=64),
                        qb[:, ch, h * 64:(h + 1) * 64].rearrange("p (a q) -> p a q", a=1)
                        .broadcast_to([128, 2, 64]),
                        atm[:, 2 * h:2 * h + 2].rearrange("p (a q) -> p a q", q=1)
                        .broadcast_to([128, 2, 64]), ALU.mult),
                        wait=[u_qs2[hb], t_atm, t_q])
                    tq1 = p.op("tensor", lambda t, h=h, ch=ch: t.transpose(
                        ps_tr[0:64, 0, 0:128], qb[:, ch, h * 64:(h + 1) * 64], identb[:, :]),
                        wait=[t_q, t_idb, u_b0] + ent)
                    tk1 = p.op("tensor", lambda t, h=h, ch=ch: t.transpose(
                        ps_tr[0:64, 1, 0:128], kb[:, ch, h * 64:(h + 1) * 64], identb[:, :]),
                        wait=[t_k, u_b1] + ent)
                    tqk2 = p.op("scalar", lambda s, hb=hb: s.copy(
                        qkTb[:, hb, :, :], ps_tr[0:64, :, 0:128]), wait=[tq1, tk1, u_qT[hb]])
                    u_b0 = tqk2
                    u_b1 = tqk2
                    tdl = p.op("tensor", lambda t, hb=hb: t.matmul(
                        ps_w[:, 1, 0:256], supper16[:, :],
                        rdb[:, hb, :, :].rearrange("p a c -> p (a c)"),
                        start=True, stop=True), wait=[trd, t_sup, u_b4])
                    u_rd[hb] = tdl
                    te0 = p.op("scalar", lambda s, hb=hb: s.activation(
                        eeb[:, hb, 0, :], ps_w[:, 1, 0:128], AF.Exp), wait=[tdl, u_ee[hb]])
                    te1 = p.op("scalar", lambda s, hb=hb: s.activation(
                        eeb[:, hb, 1, :], ps_w[:, 1, 128:256], AF.Exp))
                    u_b4 = te1
                    tq3 = p.op("tensor", lambda t, hb=hb: t.transpose(
                        ps_tr3[:, 0:128], qs2b[:, hb, :], identb[:, :]), wait=[tqs, u_b2])
                    u_qs2[hb] = tq3
                    tq4 = p.op("scalar", lambda s, hb=hb: s.copy(
                        qsTb[:, hb, :], ps_tr3[:, 0:128]), wait=[tq3, u_qsT[hb]])
                    u_b2 = tq4
                    tsc = p.op("tensor", lambda t, hb=hb: t.matmul(
                        ps_w[:, 0, 0:128], qkTb[:, hb, 1, :], qkTb[:, hb, 0, :],
                        start=True, stop=True), wait=[tqk2, u_b3])
                    u_qT[hb] = tsc
                if i >= 1:
                    h2 = i - 1
                    hb2 = h2 % 2
                    ty1 = p.op("tensor", lambda t, h=h2, ch=ch, hb=hb2: t.matmul(
                        ps_y[:, 0:65], dsb[:, hb, :], vb[:, ch, h * 65:(h + 1) * 65],
                        start=True, stop=False), wait=[pp["tds"], t_v, u_y5])
                    u_ds[hb2] = ty1
                    ty2 = p.op("tensor", lambda t, h=h2, hb=hb2: t.matmul(
                        ps_y[:, 0:65], qsTb[:, hb, :], sgc[:, h, :], start=False, stop=True),
                        wait=[pp["tq4"], t_sg])
                    u_qsT[hb2] = ty2
                if i < NH:
                    tms = p.op("vector", lambda v, hb=hb: v.tensor_tensor(
                        mscb[:, hb, :], ps_w[:, 0, 0:128], lowerf[:, :], ALU.mult),
                        wait=[tsc, t_low])
                    u_b3 = tms
                    tes = p.op("vector", lambda v, hb=hb: v.tensor_tensor(
                        eeb[:, hb, 0, :], eeb[:, hb, 0, :], eeb[:, hb, 1, :], ALU.add),
                        wait=te1)
                    tds = p.op("vector", lambda v, hb=hb: v.tensor_tensor(
                        dsb[:, hb, :], eeb[:, hb, 0, :], mscb[:, hb, :], ALU.mult),
                        wait=[u_ds[hb]])
                    u_ee[hb] = tds
                if i >= 1:
                    tr1 = p.op("vector", lambda v, hb=hb2: v.tensor_scalar(
                        dcolb[:, hb, :], ps_y[:, 64:65], cst[:, 3:4], None, ALU.add),
                        wait=[ty2, t_cst])
                    tr2 = p.op("vector", lambda v, hb=hb2: v.reciprocal(
                        rcolb[:, hb, :], dcolb[:, hb, :]))
                    t_hlast = p.op("scalar", lambda s, h=h2, hb=hb2: s.activation(
                        ytm[:, h * 64:(h + 1) * 64], ps_y[:, 0:64], AF.Copy,
                        scale=rcolb[:, hb, 0:1]), wait=[tr2])
                    u_y5 = t_hlast
                if i < NH:
                    pp = {"tds": tds, "tq4": tq4}
            if L2_TRUNC < 3:
                t_chend = [t_sg, p.tok("tensor"), p.tok("vector"), p.tok("scalar"), t_hlast]
                continue
            t_yT = None
            for dt in range(NDT):
                tb = dt % 2
                ty3 = p.op("tensor", lambda t, dt=dt, tb=tb: t.transpose(
                    ps_tr[:, tb, 0:128], ytm[:, dt * 128:(dt + 1) * 128], identb[:, :]),
                    wait=[t_hlast] + ([t_yT] if dt >= 2 else []))
                t_yT = p.op("scalar", lambda s, dt=dt, tb=tb: s.copy(
                    yT[:, dt, :], ps_tr[:, tb, 0:128]), wait=ty3)
            if L2_TRUNC < 4:
                t_chend = [t_sg, t_yT, p.tok("tensor"), p.tok("vector"), p.tok("scalar")]
                continue
            t_h0 = None
            h0_hist = []
            for do in range(NDT):
                wb2 = do % 2
                tokm = None
                for di in range(NDT):
                    tokm = p.op("tensor", lambda t, do=do, di=di, wb2=wb2: t.matmul(
                        ps_w[:, wb2, 0:128], wo16[:, di, do * 128:(do + 1) * 128],
                        yT[:, di, :], start=(di == 0), stop=(di == NDT - 1)),
                        wait=([t_yT, t_wo, u_b3] + ([h0_hist[-2]] if len(h0_hist) >= 2 else []))
                        if di == 0 else [])
                t_h0 = p.op("vector", lambda v, do=do, wb2=wb2, c=ch * 128: v.scalar_tensor_tensor(
                    h0[:, do, :], ps_w[:, wb2, 0:128], bo8[:, do:do + 1],
                    xnT[:, do, c:c + 128], ALU.add, ALU.add), wait=[tokm, t_bo, t_xnl])
                h0_hist.append(t_h0)
            if L2_TRUNC < 5:
                t_chend = [t_sg, t_h0, p.tok("tensor"), p.tok("vector"), p.tok("scalar")]
                continue
            tsq = p.op("scalar", lambda s: s.activation(
                x1T[:, :, :].rearrange("p a c -> p (a c)"),
                h0[:, :, :].rearrange("p a c -> p (a c)"), AF.Square), wait=[t_h0, t_xnsp])
            tm1 = None
            for dt in range(NDT):
                tm1 = p.op("tensor", lambda t, dt=dt: t.matmul(
                    ps_ln[0:1, 0, 0:128], onesc[:, :], h0[:, dt, :],
                    start=(dt == 0), stop=(dt == NDT - 1)),
                    wait=[t_h0, t_onc] if dt == 0 else [])
            tm2 = None
            for dt in range(NDT):
                tm2 = p.op("tensor", lambda t, dt=dt: t.matmul(
                    ps_ln[0:1, 1, 0:128], onescb[:, :], x1T[:, dt, :],
                    start=(dt == 0), stop=(dt == NDT - 1)),
                    wait=[tsq, t_oncb] if dt == 0 else [])
            tl1 = p.op("vector", lambda v: v.tensor_scalar(
                lnr[:, 0, :], ps_ln[0:1, 0, 0:128], 1.0 / D, None, ALU.mult), wait=tm1)
            tl2 = p.op("vector", lambda v: v.tensor_scalar(
                lnr[:, 1, :], ps_ln[0:1, 1, 0:128], 1.0 / D, None, ALU.mult), wait=tm2)
            tl3 = p.op("vector", lambda v: v.tensor_tensor(
                lnr[:, 2, :], lnr[:, 0, :], lnr[:, 0, :], ALU.mult))
            tl4 = p.op("vector", lambda v: v.tensor_tensor(
                lnr[:, 2, :], lnr[:, 1, :], lnr[:, 2, :], ALU.subtract))
            tl5 = p.op("scalar", lambda s: s.activation(
                lnr[:, 3, :], lnr[:, 2, :], AF.Ln, bias=cst[0:1, 0:1]), wait=tl4)
            tl6 = p.op("scalar", lambda s: s.activation(
                lnr[:, 4, :], lnr[:, 3, :], AF.Exp, scale=cst[0:1, 1:2]))
            if L2_TRUNC < 6:
                t_chend = [t_sg, t_h0, p.tok("tensor"), p.tok("vector"), p.tok("scalar")]
                continue
            tb1 = p.op("tensor", lambda t: t.matmul(
                ps_ln[:, 0, 0:128], onesf[0:1, :], lnr[0:1, 0, :], start=True, stop=True),
                wait=[tl6])
            tb2 = p.op("tensor", lambda t: t.matmul(
                ps_ln[:, 1, 0:128], onesf[0:1, :], lnr[0:1, 4, :], start=True, stop=True))
            t_s = None
            for dt in range(NDT):
                p.op("vector", lambda v, dt=dt: v.tensor_tensor(
                    h0[:, dt, :], h0[:, dt, :], ps_ln[:, 0, 0:128], ALU.subtract),
                    wait=[tb1, tb2] if dt == 0 else [])
                p.op("vector", lambda v, dt=dt: v.tensor_tensor(
                    h0[:, dt, :], h0[:, dt, :], ps_ln[:, 1, 0:128], ALU.mult))
                t_s = p.op("vector", lambda v, dt=dt: v.tensor_scalar(
                    s16[:, dt, :], h0[:, dt, :], gam8[:, dt:dt + 1], lnb8[:, dt:dt + 1],
                    ALU.mult, ALU.add), wait=[t_gm, t_lb] if dt == 0 else [])
            t_stm = None
            for dt in range(NDT):
                tb = dt % 2
                ts1 = p.op("tensor", lambda t, dt=dt, tb=tb: t.transpose(
                    ps_tr[:, tb, 0:128], s16[:, dt, :], identb[:, :]),
                    wait=[t_s] + ([t_stm] if dt >= 2 else []))
                t_stm = p.op("scalar", lambda s, dt=dt, tb=tb: s.copy(
                    stm[:, dt * 128:(dt + 1) * 128], ps_tr[:, tb, 0:128]), wait=ts1)
            if L2_TRUNC < 7:
                t_chend = [t_stm, p.tok("tensor"), p.tok("vector"), p.tok("scalar")]
                continue
            t_ep = None
            for half in range(2):
                wb2 = half % 2
                tokm = None
                for dt in range(NDT):
                    tokm = p.op("tensor", lambda t, half=half, dt=dt, wb2=wb2, c=ch * 128: t.matmul(
                        ps_w[:, wb2, :], xnT[:, dt, c:c + 128],
                        wexp16[:, dt, half * 512:(half + 1) * 512],
                        start=(dt == 0), stop=(dt == NDT - 1)),
                        wait=[t_wx, t_stm, t_xnl] if dt == 0 else [])
                t_ep = p.op("vector", lambda v, half=half, wb2=wb2: v.tensor_tensor(
                    ge1[:, half * 512:(half + 1) * 512], ps_w[:, wb2, :],
                    bexbc[:, half * 512:(half + 1) * 512], ALU.add), wait=[tokm, t_bxc])
            t_ep2 = p.op("vector", lambda v, tt=tt: v.tensor_scalar(
                ge1[:, :], ge1[:, :], kgate[:, tt:tt + 1], None, ALU.mult), wait=t_kg)
            t_gt = p.op("scalar", lambda s: s.activation(gtm[:, :], ge1[:, :], AF.Silu),
                        wait=t_ep2)
            t_x = p.dma("sync", xtl[:, :], io["xs"][b, ch * 128:(ch + 1) * 128, :],
                        wait=t_chend, ch="cx")
            t_m = p.op("vector", lambda v: v.tensor_tensor(
                stm[:, :], stm[:, :], gtm[:, :], ALU.mult), wait=t_gt)
            t_x1 = p.op("vector", lambda v: v.tensor_tensor(
                x1t[:, :], xtl[:, :], stm[:, :], ALU.add), wait=t_x)
            t_x1sp = p.dma("gpsimd", io["x1"][col0:col0 + 128, :], x1t[:, :],
                           wait=t_x1, ch="cx1")
            p.op("vector", lambda v: v.bn_stats(xst[:, 0, 4:10], x1t[:, 0:512]))
            p.op("vector", lambda v: v.bn_stats(xst[:, 0, 10:16], x1t[:, 512:1024]))
            tbn = p.op("vector", lambda v: v.bn_aggr(xst[:, 0, 0:2], xst[:, 0, 4:16]))
            p.op("scalar", lambda s: s.activation(
                xst[:, 0, 2:3], xst[:, 0, 1:2], AF.Ln, bias=cst[:, 0:1]), wait=tbn)
            tx4 = p.op("scalar", lambda s: s.activation(
                xst[:, 0, 3:4], xst[:, 0, 2:3], AF.Exp, scale=cst[:, 1:2]))
            tx5 = p.op("vector", lambda v: v.tensor_scalar(
                xs16[:, :], x1t[:, :], xst[:, 0, 0:1], xst[:, 0, 3:4],
                ALU.subtract, ALU.mult), wait=[tx4, t_x1sp])
            t_xT = None
            for dt in range(NDT):
                tb = dt % 2
                tx6 = p.op("tensor", lambda t, dt=dt, tb=tb: t.transpose(
                    ps_tr[:, tb, 0:128], xs16[:, dt * 128:(dt + 1) * 128], identb[:, :]),
                    wait=[tx5] + ([t_xT] if dt >= 2 else []))
                t_xT = p.op("scalar", lambda s, dt=dt, tb=tb: s.copy(
                    x1T[:, dt, :], ps_tr[:, tb, 0:128]), wait=tx6)
            t_xnsp = p.dma("gpsimd", io["xn1T"][:, :, col0:col0 + 128],
                           x1T[:, :, :], wait=t_xT, ch="cxn1")
            u_b0 = t_xT
            u_b1 = t_xT
            u_b4 = t_ep
            t_chend = [t_xnsp, t_x1sp, p.tok("tensor"), p.tok("vector"), p.tok("scalar")]


def build_l2():
    nc = bass.Bass("TRN2", target_bir_lowering=False, detect_race_conditions=RACE_DETECT)
    io = decl(nc, ["xs", "wo", "bo", "gam", "lnb", "wexp", "bexp",
                   "ident", "supper", "lower", "sel", "hsel", "csel", "wsel",
                   "xnT", "qtm", "ktm", "vtm", "lbtm", "cl", "sloc", "cntq", "gat",
                   "summall", "x1", "xn1T"],
              ext_in=["xnT", "qtm", "ktm", "vtm", "lbtm", "cl", "sloc", "cntq",
                      "gat", "summall"],
              ext_out=["x1", "xn1T"])
    with ExitStack() as stack:
        p = Prog(nc, stack)
        blk = stack.enter_context(nc.Block())
        emit_l2(nc, p, io)
        p.final_waits()
        p.emit(blk)
    return nc




def emit_l3(nc, p, io):
    st = p.stack
    sb = lambda name, shape, dt: st.enter_context(nc.sbuf_tensor(p.tag + name, shape, dt))
    psm = lambda name, shape, dt=F32: st.enter_context(nc.psum_tensor(p.tag + name, shape, dt))

    identb = sb("identb", [128, 128], BF16)
    pairK2 = sb("s_pairK2", [128, 512], BF16)
    caus2 = sb("s_caus2", [32, 8], BF16)
    denbc2 = sb("s_denbc2", [8, 32], F32)
    pairsel2 = sb("s_pairsel2", [32, 2048], BF16)
    bout8 = sb("bout8", [128, NDT], F32)
    cst = sb("cst", [128, 1], F32)
    outT = sb("outT", [128, NDT, B, SL], BF16)

    ps_tr = psm("ps_tr", [128, 2, 1024], BF16)
    ps_mm = psm("ps_mm", [128, 2, 512], F32)
    ps_sc = psm("ps_sc", [32, 2, 512], F32)
    ps_bc = psm("ps_bc", [128, 2, 512], F32)

    t_idb = p.dma("gpsimd", identb[:, :], io["ident"], ch="cst")
    t_pk = p.dma("gpsimd", pairK2[:, :], io["pairK2"], ch="cst")
    t_c2 = p.dma("gpsimd", caus2[:, :], io["caus2"], ch="cst")
    t_db = p.dma("sync", denbc2[:, :], io["denbc2"], ch="cst2")
    t_p2 = p.dma("gpsimd", pairsel2[:, :], io["pairsel2"], ch="cst")
    t_bo = p.dma("sync", bout8[:, :], io["bout"].rearrange("(dt p) -> p dt", p=128),
                 allow_slow_non_contiguous=True, ch="cst2")
    t_cq = p.op("vector", lambda v: v.memset(cst[:, 0:1], 0.125))
    if int(_os.environ.get('L3_HG', '8')) < 8:
        p.op("vector", lambda v: v.memset(outT[:, :, :, :], 0.0))

    with ExitStack() as ph:
        s2 = lambda name, shape, dt: ph.enter_context(nc.sbuf_tensor(p.tag + name, shape, dt))
        xn1T = s2("s_xn1T", [128, NDT, T], BF16)
        wqn = s2("wqn", [128, NDT, 128], BF16)
        wqT = s2("wqT", [128, 3, NDT, 128], BF16)
        qg = s2("qg", [128, B, SL], BF16)
        kg = s2("kg", [128, B, SL], BF16)
        vg = s2("vg", [128, B, SL], BF16)
        prod = s2("prod", [128, SL], BF16)
        Ebuf = s2("Ebuf", [32, SL], BF16)
        rden = s2("rden", [8, SL], F32)
        Enb = s2("Enb", [32, SL], BF16)
        wvt = s2("wvt", [128, SL], BF16)

        t_xn = None
        for dt in range(NDT):
            t_xn = p.dma("gpsimd", xn1T[:, dt, :], io["xn1T"][:, dt, :], ch="cxn")

        t_accw = None
        t_E = None
        t_wq = None
        for hg in range(int(_os.environ.get('L3_HG', '8'))):
            t_wn = None
            for sel3, e0 in ((0, hg * 128), (1, 1024 + hg * 128), (2, 2048 + hg * 128)):
                t_wn = p.dma("gpsimd", wqn[:, :, :],
                             io["wqkv"][e0:e0 + 128, :].rearrange("e (dt c) -> e dt c", c=128),
                             wait=[t_wq], ch="cw")
                for de in range(NDT):
                    tb = de % 2
                    tt1 = p.op("tensor", lambda t, de=de, tb=tb: t.transpose(
                        ps_tr[:, tb, 0:128], wqn[:, de, :], identb[:, :]),
                        wait=[t_wn, t_idb] + ([t_wq] if de >= 2 else []))
                    t_wq = p.op("scalar", lambda s, de=de, sel3=sel3, tb=tb: s.copy(
                        wqT[:, sel3, de, :], ps_tr[:, tb, 0:128]), wait=tt1)
            t_ev = None
            for sel3, dst in ((0, qg), (1, kg), (2, vg)):
                for b in range(B):
                    for half in range(2):
                        pb = (b * 2 + half) % 2
                        tokm = None
                        for dt in range(NDT):
                            tokm = p.op("tensor",
                                        lambda t, sel3=sel3, dt=dt, pb=pb, b=b, half=half: t.matmul(
                                            ps_mm[:, pb, :], wqT[:, sel3, dt, :],
                                            xn1T[:, dt, b * SL + half * 512:b * SL + (half + 1) * 512],
                                            start=(dt == 0), stop=(dt == NDT - 1)),
                                        wait=[t_wq, t_xn, t_ev, t_accw] if dt == 0 else [])
                        t_ev = p.op("vector",
                                    lambda v, dst=dst, b=b, half=half, pb=pb: v.tensor_copy(
                                        dst[:, b, half * 512:(half + 1) * 512], ps_mm[:, pb, :]),
                                    wait=tokm)
            t_scm = None
            for a1 in range(4):
                for a2 in range(4):
                    pp = a1 * 4 + a2
                    t_pr = p.op("vector", lambda v, a1=a1, a2=a2: v.tensor_tensor(
                        prod[:, :], qg[:, a1, :], kg[:, a2, :], ALU.mult),
                        wait=[t_ev] if pp == 0 else [t_scm])
                    for half in range(2):
                        t_scm = p.op("tensor", lambda t, pp=pp, half=half: t.matmul(
                            ps_sc[:, half, :], pairK2[:, pp * 32:(pp + 1) * 32],
                            prod[:, half * 512:(half + 1) * 512],
                            start=(pp == 0), stop=(pp == 15), skip_group_check=True),
                            wait=[t_pr, t_pk] + ([t_accw, t_E] if pp == 0 else []))
            t_E = p.op("scalar", lambda s: s.activation(
                Ebuf[:, :], ps_sc[:, :, :].rearrange("p a c -> p (a c)"), AF.Exp,
                scale=cst[0:32, 0:1]), wait=[t_scm, t_cq])
            t_en = None
            t_rc = None
            for half in range(2):
                t_dn = p.op("tensor", lambda t, half=half: t.matmul(
                    ps_bc[0:8, 0, 0:512], caus2[:, :], Ebuf[:, half * 512:(half + 1) * 512],
                    start=True, stop=True),
                    wait=[t_E, t_c2, t_rc] + ([t_accw] if half == 0 else []))
                t_rc = p.op("vector", lambda v, half=half: v.reciprocal(
                    rden[:, half * 512:(half + 1) * 512], ps_bc[0:8, 0, 0:512]), wait=t_dn)
                t_dx = p.op("tensor", lambda t, half=half: t.matmul(
                    ps_bc[0:32, 1, 0:512], denbc2[:, :], rden[:, half * 512:(half + 1) * 512],
                    start=True, stop=True), wait=[t_rc, t_db])
                t_en = p.op("vector", lambda v, half=half: v.tensor_tensor(
                    Enb[:, half * 512:(half + 1) * 512], Ebuf[:, half * 512:(half + 1) * 512],
                    ps_bc[0:32, 1, 0:512], ALU.mult), wait=t_dx)
            t_accw = None
            for a1 in range(4):
                for a2 in range(a1 + 1):
                    pp = a1 * 4 + a2
                    for half in range(2):
                        bs = (a2 * 2 + half) % 2
                        t_bm = p.op("tensor", lambda t, pp=pp, half=half, bs=bs: t.matmul(
                            ps_bc[:, bs, :], pairsel2[:, pp * 128:(pp + 1) * 128],
                            Enb[:, half * 512:(half + 1) * 512], start=True, stop=True),
                            wait=[t_en, t_p2, t_accw])
                        if a2 == 0:
                            t_accw = p.op("vector",
                                          lambda v, hg=hg, a1=a1, half=half, bs=bs: v.tensor_tensor(
                                              outT[:, hg, a1, half * 512:(half + 1) * 512],
                                              vg[:, 0, half * 512:(half + 1) * 512],
                                              ps_bc[:, bs, :], ALU.mult), wait=t_bm)
                        else:
                            p.op("vector", lambda v, a2=a2, half=half, bs=bs: v.tensor_tensor(
                                wvt[:, half * 512:(half + 1) * 512],
                                vg[:, a2, half * 512:(half + 1) * 512],
                                ps_bc[:, bs, :], ALU.mult), wait=t_bm)
                            t_accw = p.op("vector",
                                          lambda v, hg=hg, a1=a1, half=half: v.tensor_tensor(
                                              outT[:, hg, a1, half * 512:(half + 1) * 512],
                                              outT[:, hg, a1, half * 512:(half + 1) * 512],
                                              wvt[:, half * 512:(half + 1) * 512], ALU.add))
            if "dbg" in io:
                p.dma("gpsimd", io["dbg"][hg % 2, :, :],
                      outT[:, 0, :, :].rearrange("p a c -> p (a c)"), wait=t_accw, ch="cdbg")
        t_attn = [t_accw, p.tok("tensor"), p.tok("scalar")]

    with ExitStack() as ph:
        s2 = lambda name, shape, dt: ph.enter_context(nc.sbuf_tensor(p.tag + name, shape, dt))
        woT = s2("woT", [128, NDT, D], BF16)
        wst = s2("wst", [128, NDT, 128], BF16)
        s1b = s2("s1b", [128, NDT, SL], BF16)
        s1s = s2("s1s", [128, NDT, 128], BF16)

        t_wo = None
        t_wl = None
        for do in range(NDT):
            t_wl = p.dma("gpsimd", wst[:, :, :],
                         io["wout"][do * 128:(do + 1) * 128, :]
                         .rearrange("e (dt c) -> e dt c", c=128),
                         wait=[t_wl, t_wo] + (t_attn if do == 0 else []), ch="cw")
            for de in range(NDT):
                tb = de % 2
                tt1 = p.op("tensor", lambda t, de=de, tb=tb: t.transpose(
                    ps_tr[:, tb, 0:128], wst[:, de, :], identb[:, :]),
                    wait=[t_wl] + ([t_wo] if de >= 2 else []) + (t_attn if do == 0 and de < 2 else []))
                t_wo = p.op("scalar", lambda s, de=de, do=do, tb=tb: s.copy(
                    woT[:, de, do * 128:(do + 1) * 128], ps_tr[:, tb, 0:128]), wait=tt1)
        t_s1 = None
        t_sp = None
        t_cp = None
        for b in range(B):
            for do in range(NDT):
                for half in range(2):
                    pb = (do * 2 + half) % 2
                    tokm = None
                    for de in range(NDT):
                        tokm = p.op("tensor",
                                    lambda t, de=de, do=do, b=b, half=half, pb=pb: t.matmul(
                                        ps_mm[:, pb, :], woT[:, de, do * 128:(do + 1) * 128],
                                        outT[:, de, b, half * 512:(half + 1) * 512],
                                        start=(de == 0), stop=(de == NDT - 1)),
                                    wait=[t_wo, t_s1] if de == 0 else [])
                    t_s1 = p.op("vector", lambda v, do=do, half=half, pb=pb: v.tensor_scalar(
                        s1b[:, do, half * 512:(half + 1) * 512], ps_mm[:, pb, :],
                        bout8[:, do:do + 1], None, ALU.add), wait=[tokm, t_bo])
            for mt in range(NMT):
                for dt in range(NDT):
                    tb = dt % 2
                    tt1 = p.op("tensor", lambda t, dt=dt, mt=mt, tb=tb: t.transpose(
                        ps_tr[:, tb, 0:128], s1b[:, dt, mt * 128:(mt + 1) * 128], identb[:, :]),
                        wait=[t_s1] + ([t_cp] if dt >= 2 else []) + ([t_sp] if dt < 2 else []))
                    t_cp = p.op("scalar", lambda s, dt=dt, tb=tb: s.copy(
                        s1s[:, dt, :], ps_tr[:, tb, 0:128]), wait=tt1)
                t_sp = p.dma("gpsimd",
                             io["s1tm"][b * SL + mt * 128:b * SL + (mt + 1) * 128, :],
                             s1s[:, :, :].rearrange("p dt c -> p (dt c)"),
                             wait=t_cp, ch="cs1")


def build_l3(dbg=False):
    nc = bass.Bass("TRN2", target_bir_lowering=False, detect_race_conditions=RACE_DETECT)
    io = decl(nc, ["wqkv", "wout", "bout", "ident", "pairK2", "caus2", "denbc2",
                   "pairsel2", "xn1T", "s1tm"],
              ext_in=["xn1T"], ext_out=["s1tm"])
    if dbg:
        io["dbg"] = nc.dram_tensor("dbg", [2, 128, 4096], F32, kind="ExternalOutput").ap()
    with ExitStack() as stack:
        p = Prog(nc, stack)
        blk = stack.enter_context(nc.Block())
        emit_l3(nc, p, io)
        p.final_waits()
        p.emit(blk)
    return nc




def emit_l4(nc, p, io):
    st = p.stack
    sb = lambda name, shape, dt: st.enter_context(nc.sbuf_tensor(p.tag + name, shape, dt))
    psm = lambda name, shape, dt=F32: st.enter_context(nc.psum_tensor(p.tag + name, shape, dt))

    identb = sb("identb", [128, 128], BF16)
    identf = sb("identf", [128, 128], F32)
    eselc = sb("eselc", [8, 1024], BF16)
    maskE = sb("maskE4", [8, T], BF16)
    sumb2 = sb("sumb2", [128, NDT], F32)
    b1T = sb("b1T", [128, 32, 8], F32)
    mbc = sb("mbc", [128, 2, 512], BF16)
    g1s = sb("g1s", [128, NDT, 128], BF16)
    g1t = sb("g1t", [128, NDT, 128], BF16)
    s1l = sb("s1l", [128, 2, D], BF16)
    x1l = sb("x1l", [128, 2, D], F32)

    ps_h = psm("ps_h", [128, 2, 512], F32)
    ps_o = psm("ps_o", [128, 2, 512], F32)
    ps_tr = psm("ps_tr", [128, 2, 1024], BF16)
    ps_bc = psm("ps_bc", [128, 2, 512], F32)

    t_idb = p.dma("gpsimd", identb[:, :], io["ident"], ch="cst")
    t_idf = p.dma("sync", identf[:, :], io["ident"], ch="cst2")
    t_es = p.dma("gpsimd", eselc[:, :], io["esel"], ch="cst")

    b2r = sb("b2r", [128, NDT, NE], F32)

    with ExitStack() as ph:
        s2 = lambda name, shape, dt: ph.enter_context(nc.sbuf_tensor(p.tag + name, shape, dt))
        xn1T = s2("s_xn1T", [128, NDT, T], BF16)
        wr16 = s2("wr16", [128, NDT, 8], BF16)
        brc = s2("brc", [8, 1], F32)
        rE = s2("rE", [8, T], F32)
        rtm = s2("rtm", [128, 32, 8], F32)
        mask3 = s2("mask34", [128, 32, 8], F32)
        gw1 = s2("gw14", [128, 32], F32)
        gw2 = s2("gw24", [128, 32], F32)
        gw3 = s2("gw34", [128, 32], F32)

        t_xn = None
        for dt in range(NDT):
            t_xn = p.dma("gpsimd", xn1T[:, dt, :], io["xn1T"][:, dt, :], ch="cxn")
        t_wr = p.dma("gpsimd", wr16[:, :, :], io["Wr"].rearrange("(dt p) c -> p dt c", p=128),
                     ch="cst")
        b1n = s2("b1n", [8, FFH], F32)
        t_b1n = p.dma("gpsimd", b1n[:, :], io["b1"], ch="cst")
        b2n = s2("b2n", [8, D], F32)
        t_b2n = p.dma("gpsimd", b2n[:, :], io["b2"], ch="cst")
        t_b2c = None
        for c in range(NDT):
            tb = c % 2
            tt8 = p.op("tensor", lambda t, c=c, tb=tb: t.transpose(
                ps_h[:, tb, 0:8], b2n[:, c * 128:(c + 1) * 128], identf[0:8, 0:8]),
                wait=[t_b2n, t_idf] + ([t_b2c] if c >= 2 else []))
            t_b2c = p.op("scalar", lambda s, c=c, tb=tb: s.copy(
                b2r[:, c, :], ps_h[:, tb, 0:8]), wait=tt8)
        t_sb2 = p.op("vector", lambda v: v.tensor_reduce(
            sumb2[:, :], b2r[:, :, :], AXX, ALU.add), wait=t_b2c)
        t_b1 = None
        for c in range(32):
            tb = c % 2
            tt9 = p.op("tensor", lambda t, c=c, tb=tb: t.transpose(
                ps_o[:, tb, 0:8], b1n[:, c * 128:(c + 1) * 128], identf[0:8, 0:8]),
                wait=[t_b1n, t_idf] + ([t_b1] if c >= 2 else []))
            t_b1 = p.op("scalar", lambda s, c=c, tb=tb: s.copy(
                b1T[:, c, :], ps_o[:, tb, 0:8]), wait=tt9)
        t_br = p.dma("sync", brc[:, :], io["br"].rearrange("(e o) -> e o", o=1), ch="cst2")
        t_rE = None
        for i in range(8):
            pb = i % 2
            c0 = i * 512
            tokm = None
            for dt in range(NDT):
                tokm = p.op("tensor", lambda t, dt=dt, pb=pb, c0=c0: t.matmul(
                    ps_h[0:8, pb, :], wr16[:, dt, :], xn1T[:, dt, c0:c0 + 512],
                    start=(dt == 0), stop=(dt == NDT - 1)),
                    wait=([t_xn, t_wr] + ([t_rE] if i >= 2 else [])) if dt == 0 else [])
            t_rE = p.op("vector", lambda v, pb=pb, c0=c0: v.tensor_scalar(
                rE[:, c0:c0 + 512], ps_h[0:8, pb, :], brc[:, 0:1], None, ALU.add),
                wait=[tokm, t_br])
        t_ltm = None
        for c in range(32):
            tb = c % 2
            ttr = p.op("tensor", lambda t, c=c, tb=tb: t.transpose(
                ps_bc[:, tb, 0:8], rE[:, c * 128:(c + 1) * 128], identf[0:8, 0:8]),
                wait=[t_rE, t_idf] + ([t_ltm] if c >= 2 else []))
            t_ltm = p.op("scalar", lambda s, c=c, tb=tb: s.copy(
                rtm[:, c, :], ps_bc[:, tb, 0:8]), wait=ttr)
        p.op("vector", lambda v: v.tensor_copy(gw1[:, :], rtm[:, :, 0]), wait=t_ltm)
        for e2 in range(1, NE):
            p.op("vector", lambda v, e2=e2: v.tensor_max(gw1[:, :], gw1[:, :], rtm[:, :, e2]))
        p.op("vector", lambda v: v.memset(gw2[:, :], 0.0))
        t_msk = None
        for e2 in range(NE):
            p.op("vector", lambda v, e2=e2: v.tensor_tensor(
                gw3[:, :], rtm[:, :, e2], gw1[:, :], ALU.is_ge))
            p.op("vector", lambda v, e2=e2: v.tensor_tensor(
                mask3[:, :, e2], gw3[:, :], gw2[:, :], ALU.mult))
            p.op("vector", lambda v, e2=e2: v.tensor_tensor(
                mask3[:, :, e2], gw3[:, :], mask3[:, :, e2], ALU.subtract))
            t_msk = p.op("vector", lambda v, e2=e2: v.tensor_tensor(
                gw2[:, :], gw2[:, :], mask3[:, :, e2], ALU.add))
        t_mE = None
        for c in range(32):
            tb = c % 2
            ttr = p.op("tensor", lambda t, c=c, tb=tb: t.transpose(
                ps_bc[0:8, tb, 0:128], mask3[:, c, :], identf[:, :]),
                wait=[t_msk] + ([t_mE] if c >= 2 else []))
            t_mE = p.op("vector", lambda v, c=c, tb=tb: v.tensor_copy(
                maskE[:, c * 128:(c + 1) * 128], ps_bc[0:8, tb, 0:128]), wait=ttr)
        t_route = [t_mE, p.tok("tensor"), p.tok("scalar"), p.tok("vector")]

    xhal = sb("xhal", [128, NDT, 2048], BF16)
    xi = sb("xi", [128, NDT, 2048], BF16)
    w1q = sb("w1q", [128, 2, NDT, 1024], BF16)
    w2q = sb("w2q", [128, 2, NDT, 1024], BF16)
    outs = sb("outs", [128, NDT, 2048], BF16)
    Ht = sb("Ht", [128, 2, 8, 512], BF16)
    t_old = t_route
    oacc_hist = []
    t_m1_prev = None
    t_g1 = None
    t_sp = None
    t_w1 = [None, None]
    t_w2 = [None, None]
    w1_user = [None, None]
    w2_user = [None, None]
    relu_hist = []
    htuser = [None, None]
    t_xh = None
    t_xiu = None
    t_ord = None
    for th in range(2):
        tok0 = th * 2048
        t_xh = p.dma("gpsimd", xhal[:, :, :], io["xn1T"][:, :, tok0:tok0 + 2048],
                     wait=[t_xiu] + (t_route if th == 0 else []), ch="cxh")
        t_out0 = p.op("vector", lambda v: v.memset(outs[:, :, :], 0.0),
                      wait=[t_g1] + (t_route if th == 0 else []))
        t_oacc = t_out0
        for e in range(NE):
            t_xiu = None
            for tc in range(4):
                bs = tc % 2
                t_bm = p.op("tensor", lambda t, e=e, tc=tc, bs=bs, tok0=tok0: t.matmul(
                    ps_bc[:, bs, :], eselc[:, e * 128:(e + 1) * 128],
                    maskE[:, tok0 + tc * 512:tok0 + (tc + 1) * 512],
                    start=True, stop=True),
                    wait=[t_mE, t_es, t_xiu if tc >= 2 else t_ord]
                    if True else []) 
                t_mb = p.op("scalar", lambda s, bs=bs: s.copy(
                    mbc[:, bs, :], ps_bc[:, bs, :]), wait=t_bm)
                for dt in range(NDT):
                    t_xiu = p.op("vector", lambda v, dt=dt, tc=tc, bs=bs: v.tensor_tensor(
                        xi[:, dt, tc * 512:(tc + 1) * 512],
                        xhal[:, dt, tc * 512:(tc + 1) * 512], mbc[:, bs, :], ALU.mult),
                        wait=[t_mb, t_xh])
            for q in range(4):
                wb = (e * 4 + q) % 2
                t_w1[wb] = p.dma("gpsimd", w1q[:, wb, :, :],
                                 io["W1"][e, :, :].rearrange("(dt p) h -> p dt h", p=128)
                                 [:, :, q * 1024:(q + 1) * 1024],
                                 wait=[t_w1[wb], w1_user[wb]], ch=f"cw1{wb}")
                t_w2[wb] = p.dma("gpsimd", w2q[:, wb, :, :],
                                 io["W2"][e, q * 1024:(q + 1) * 1024, :]
                                 .rearrange("(ht p) d -> p ht d", p=128),
                                 wait=[t_w2[wb], w2_user[wb]], ch=f"cw2{wb}")
                relu_last = [None, None]
                for tcs in range(5):
                    if tcs < 4:
                        tc = tcs
                        hs2 = tc % 2
                        for ht in range(8):
                            tokm = None
                            for dt in range(NDT):
                                tokm = p.op("tensor",
                                            lambda t, tc=tc, ht=ht, dt=dt, wb=wb,
                                            hb=(tc * 8 + ht) % 2: t.matmul(
                                                ps_h[:, hb, :],
                                                w1q[:, wb, dt, ht * 128:(ht + 1) * 128],
                                                xi[:, dt, tc * 512:(tc + 1) * 512],
                                                start=(dt == 0), stop=(dt == NDT - 1)),
                                            wait=([t_w1[wb], t_xiu] +
                                                  ([relu_hist[-2]] if len(relu_hist) >= 2 else []))
                                            if dt == 0 else [])
                            t_relu = p.op("scalar",
                                          lambda s, tc=tc, ht=ht, hb=(tc * 8 + ht) % 2,
                                          hs2=hs2, e=e, q=q: s.activation(
                                              Ht[:, hs2, ht, :], ps_h[:, hb, :], AF.Relu,
                                              bias=b1T[:, q * 8 + ht, e:e + 1]),
                                          wait=[tokm, t_b1, htuser[hs2]])
                            relu_hist.append(t_relu)
                        relu_last[hs2] = t_relu
                    if tcs >= 1:
                        tc = tcs - 1
                        hs2 = tc % 2
                        for do in range(NDT):
                            ob = do % 2
                            tokm = None
                            for ht in range(8):
                                tokm = p.op("tensor",
                                            lambda t, do=do, ht=ht, ob=ob, wb=wb, hs2=hs2: t.matmul(
                                                ps_o[:, ob, :],
                                                w2q[:, wb, ht, do * 128:(do + 1) * 128],
                                                Ht[:, hs2, ht, :], start=(ht == 0), stop=(ht == 7)),
                                            wait=([t_w2[wb], relu_last[hs2]] +
                                                  ([oacc_hist[-2]] if len(oacc_hist) >= 2 else []))
                                            if ht == 0 else [])
                            t_oacc = p.op("vector", lambda v, do=do, tc=tc, ob=ob: v.tensor_tensor(
                                outs[:, do, tc * 512:(tc + 1) * 512],
                                outs[:, do, tc * 512:(tc + 1) * 512], ps_o[:, ob, :], ALU.add),
                                wait=tokm)
                            oacc_hist.append(t_oacc)
                            w2_user[wb] = tokm
                            htuser[hs2] = tokm
                    w1_user[wb] = relu_hist[-1] if relu_hist else None
            t_ord = t_oacc
        for mt in range(16):
            r0 = tok0 + mt * 128
            bq, sl0 = divmod(r0, SL)
            li = mt % 2
            t_x1 = p.dma("sync", x1l[:, li, :], io["x1"][r0:r0 + 128, :],
                         wait=t_sp, ch=f"cx{li}")
            t_s1 = p.dma("gpsimd", s1l[:, li, :], io["s1tm"][r0:r0 + 128, :],
                         wait=t_sp, ch=f"cs{li}")
            t_si = None
            for dt in range(NDT):
                t_si = p.op("scalar", lambda s, dt=dt, mt=mt: s.activation(
                    g1s[:, dt, :], outs[:, dt, mt * 128:(mt + 1) * 128], AF.Silu,
                    bias=sumb2[:, dt:dt + 1]), wait=[t_ord, t_sb2] if dt == 0 else [])
            t_gt = None
            for dt in range(NDT):
                tb = dt % 2
                ttr = p.op("tensor", lambda t, dt=dt, tb=tb: t.transpose(
                    ps_tr[:, tb, 0:128], g1s[:, dt, :], identb[:, :]),
                    wait=[t_si, t_idb] + ([t_gt] if dt >= 2 else []))
                t_gt = p.op("scalar", lambda s, dt=dt, tb=tb: s.copy(
                    g1t[:, dt, :], ps_tr[:, tb, 0:128]),
                    wait=[ttr] + ([t_m1_prev] if dt == 0 else []))
            t_m1 = p.op("vector", lambda v, li=li: v.tensor_tensor(
                s1l[:, li, :], s1l[:, li, :], g1t[:, :, :].rearrange("p a c -> p (a c)"),
                ALU.mult), wait=[t_gt, t_s1])
            t_m1_prev = t_m1
            t_x2 = p.op("vector", lambda v, li=li: v.tensor_tensor(
                x1l[:, li, :], x1l[:, li, :], s1l[:, li, :], ALU.add), wait=t_x1)
            t_sp = p.dma("gpsimd", io["yout"][bq, sl0:sl0 + 128, :], x1l[:, li, :],
                         wait=t_x2, ch="cy")
        t_g1 = p.tok("scalar")
        t_xiu = p.tok("vector")


def build_l4():
    nc = bass.Bass("TRN2", target_bir_lowering=False, detect_race_conditions=RACE_DETECT)
    io = decl(nc, ["W1", "b1", "W2", "b2", "Wr", "br", "ident", "esel",
                   "xn1T", "x1", "s1tm", "yout"],
              ext_in=["xn1T", "x1", "s1tm"], ext_out=["yout"])
    with ExitStack() as stack:
        p = Prog(nc, stack)
        blk = stack.enter_context(nc.Block())
        emit_l4(nc, p, io)
        p.final_waits()
        p.emit(blk)
    return nc


_NC_CACHE = {}


def _get_nc(name, builder):
    if name not in _NC_CACHE:
        _NC_CACHE[name] = builder()
    return _NC_CACHE[name]




def emit_cc(nc, p, io):
    tok = p.op("gpsimd", lambda g: g.collective_compute(
        "AllGather", ALU.bypass, replica_groups=[list(range(NCORES))],
        ins=[io["summ"].opt()], outs=[io["summall"].opt()]), n=1)
    p.opn("gpsimd", lambda g: g.nop(), wait=tok)


ALL_NAMES = ["xs", "wq", "wk", "wv", "wbeta", "bbeta", "wo", "bo", "gam", "lnb",
             "wgate", "wexp", "bexp", "wqkv", "wout", "bout",
             "W1", "b1", "W2", "b2", "Wr", "br",
             "cos8", "sin8", "ident", "supper", "lower", "sel", "hsel", "csel",
             "wsel", "pairK2", "caus2", "denbc2", "pairsel2", "esel",
             "xnT", "qtm", "ktm", "vtm", "lbtm", "cl", "sloc", "cntq", "gat",
             "summ", "summall", "x1", "xn1T", "s1tm", "yout"]


def build_fused():
    nc = bass.Bass("TRN2", target_bir_lowering=False, detect_race_conditions=RACE_DETECT)
    io = decl(nc, ALL_NAMES, ext_out=["yout"])
    with ExitStack() as outer:
        ems = (emit_l1, emit_cc, emit_l2, emit_l3, emit_l4)
        if _os.environ.get("SKIP_CC"):
            ems = (emit_l1, emit_l2, emit_l3, emit_l4)
        for i, em in enumerate(ems):
            with ExitStack() as stg:
                p = Prog(nc, stg, sem_stack=outer, tag=f"s{i}_")
                blk = stg.enter_context(nc.Block())
                em(nc, p, io)
                p.final_waits()
                p.emit(blk)
    return nc


def kernel(**inputs):
    I = inputs
    cores = list(range(NCORES))
    hcs = [host_consts(c) for c in cores]
    nc = _get_nc("fused", build_fused)
    maps = []
    for c in cores:
        hc = hcs[c]
        m = {
            "xs": np.ascontiguousarray(I["L0_attn_x"][:, c * SL:(c + 1) * SL, :]),
            "wq": I["L0_attn_Wq"], "wk": I["L0_attn_Wk"], "wv": I["L0_attn_Wv"],
            "wbeta": I["L0_attn_Wbeta"],
            "bbeta": np.ascontiguousarray(np.asarray(I["L0_attn_bbeta"]).reshape(1, 32)),
            "wo": I["L0_attn_Wo"], "bo": I["L0_attn_bo"],
            "gam": I["L0_attn_ln_gamma"], "lnb": I["L0_attn_ln_beta"],
            "wgate": I["L0_ffn_w_gate"], "wexp": I["L0_ffn_w_expert"],
            "bexp": np.ascontiguousarray(np.asarray(I["L0_ffn_b_expert"]).reshape(1, D)),
            "wqkv": I["L1_attn_w_qkv"], "wout": I["L1_attn_w_out"],
            "bout": I["L1_attn_b_out"],
            "W1": I["L1_ffn_W1"], "b1": I["L1_ffn_b1"], "W2": I["L1_ffn_W2"],
            "b2": I["L1_ffn_b2"], "Wr": I["L1_ffn_Wr"], "br": I["L1_ffn_br"],
        }
        for nm in ("cos8", "sin8", "ident", "supper", "lower", "sel", "hsel",
                   "csel", "wsel", "pairK2", "caus2", "denbc2", "pairsel2", "esel"):
            m[nm] = hc[nm]
        maps.append(m)
    res = run_bass_kernel_spmd(nc, maps, core_ids=cores).results
    out = np.zeros((B, S, D), dtype=np.float32)
    for c in cores:
        out[:, c * SL:(c + 1) * SL, :] = np.asarray(res[c]["yout"])
    return out


def kernel_unfused(**inputs):
    I = inputs
    cores = list(range(NCORES))
    hcs = [host_consts(c) for c in cores]
    xsl = [np.ascontiguousarray(I["L0_attn_x"][:, c * SL:(c + 1) * SL, :]) for c in cores]

    nc1 = _get_nc("l1", build_l1)
    m1 = [{
        "xs": xsl[c], "wq": I["L0_attn_Wq"], "wk": I["L0_attn_Wk"], "wv": I["L0_attn_Wv"],
        "wbeta": I["L0_attn_Wbeta"], "bbeta": np.ascontiguousarray(
            np.asarray(I["L0_attn_bbeta"]).reshape(1, 32)),
        "wgate": I["L0_ffn_w_gate"],
        "cos8": hcs[c]["cos8"], "sin8": hcs[c]["sin8"], "ident": hcs[c]["ident"],
    } for c in cores]
    r1 = run_bass_kernel_spmd(nc1, m1, core_ids=cores).results
    summall = np.concatenate([np.asarray(r1[c]["summ"]) for c in cores], axis=0)

    nc2 = _get_nc("l2", build_l2)
    m2 = [{
        "xs": xsl[c], "wo": I["L0_attn_Wo"], "bo": I["L0_attn_bo"],
        "gam": I["L0_attn_ln_gamma"], "lnb": I["L0_attn_ln_beta"],
        "wexp": I["L0_ffn_w_expert"], "bexp": np.ascontiguousarray(
            np.asarray(I["L0_ffn_b_expert"]).reshape(1, D)),
        "ident": hcs[c]["ident"], "supper": hcs[c]["supper"], "lower": hcs[c]["lower"],
        "sel": hcs[c]["sel"], "hsel": hcs[c]["hsel"], "csel": hcs[c]["csel"],
        "wsel": hcs[c]["wsel"], "summall": summall,
        "xnT": np.asarray(r1[c]["xnT"]), "qtm": np.asarray(r1[c]["qtm"]),
        "ktm": np.asarray(r1[c]["ktm"]), "vtm": np.asarray(r1[c]["vtm"]),
        "lbtm": np.asarray(r1[c]["lbtm"]), "cl": np.asarray(r1[c]["cl"]),
        "sloc": np.asarray(r1[c]["sloc"]), "cntq": np.asarray(r1[c]["cntq"]),
        "gat": np.asarray(r1[c]["gat"]),
    } for c in cores]
    r2 = run_bass_kernel_spmd(nc2, m2, core_ids=cores).results

    nc3 = _get_nc("l3", build_l3)
    m3 = [{
        "wqkv": I["L1_attn_w_qkv"], "wout": I["L1_attn_w_out"], "bout": I["L1_attn_b_out"],
        "ident": hcs[c]["ident"], "pairK2": hcs[c]["pairK2"], "caus2": hcs[c]["caus2"],
        "denbc2": hcs[c]["denbc2"], "pairsel2": hcs[c]["pairsel2"],
        "xn1T": np.asarray(r2[c]["xn1T"]),
    } for c in cores]
    r3 = run_bass_kernel_spmd(nc3, m3, core_ids=cores).results

    nc4 = _get_nc("l4", build_l4)
    m4 = [{
        "W1": I["L1_ffn_W1"], "b1": I["L1_ffn_b1"], "W2": I["L1_ffn_W2"],
        "b2": I["L1_ffn_b2"], "Wr": I["L1_ffn_Wr"], "br": I["L1_ffn_br"],
        "ident": hcs[c]["ident"], "esel": hcs[c]["esel"],
        "xn1T": np.asarray(r2[c]["xn1T"]), "x1": np.asarray(r2[c]["x1"]),
        "s1tm": np.asarray(r3[c]["s1tm"]),
    } for c in cores]
    r4 = run_bass_kernel_spmd(nc4, m4, core_ids=cores).results

    out = np.zeros((B, S, D), dtype=np.float32)
    for c in cores:
        out[:, c * SL:(c + 1) * SL, :] = np.asarray(r4[c]["yout"])
    return out
